# version 27
# baseline (speedup 1.0000x reference)
import numpy as np
import ml_dtypes
import concourse.bass as bass
import concourse.mybir as mybir
from concourse.bass_utils import run_bass_kernel_spmd
from contextlib import ExitStack

F32 = mybir.dt.float32
F16 = mybir.dt.float16
BF16 = mybir.dt.bfloat16
I32 = mybir.dt.int32
AF = mybir.ActivationFunctionType
ALU = mybir.AluOpType

NCORES = 8
B, S, D = 16, 2048, 1024
H, DH, WS = 16, 64, 128
HID = 3072
SEQ_PER_CORE = B // NCORES
TOK = SEQ_PER_CORE * S
TILE = 512
NT = TOK // TILE
TPS = NT // SEQ_PER_CORE
WPT = TILE // WS
DC = D // 128
HC = HID // 128
THETA = 10000.0
ESHIFT = 5.0
EPS = 1e-5


def _host_tables():
    inv_freq = 1.0 / THETA ** (np.arange(0, DH, 2, dtype=np.float64) / DH)
    d_in_head = np.arange(128) % 64
    freq_p = inv_freq[d_in_head % 32]
    pos = np.arange(WS, dtype=np.float64)
    ang = pos[None, :] * freq_p[:, None]
    cos = np.cos(ang)
    sin = np.sin(ang)
    sign = np.where(d_in_head < 32, -1.0, 1.0)[:, None]
    d_in_head2 = d_in_head
    sign = np.where((d_in_head2 % 64) < 32, -1.0, 1.0)[:, None]
    ssin = sin * sign
    cos4 = np.tile(cos, (1, WPT))
    ssin4 = np.tile(ssin, (1, WPT))
    scale = DH ** (-0.5)
    tabs = {
        "cos_q": (cos4 * scale).astype(np.float16),
        "sin_q": (ssin4 * scale).astype(np.float16),
        "cos_k": cos4.astype(np.float16),
        "sin_k": ssin4.astype(np.float16),
    }
    a128 = 128.0 * inv_freq[np.arange(64) % 32]
    c1, s1 = np.cos(a128), np.sin(a128)
    R = np.zeros((64, 64), dtype=np.float64)
    for d in range(64):
        R[d, d] = c1[d]
        if d < 32:
            R[d, d + 32] = -s1[d]
        else:
            R[d, d - 32] = s1[d]
    R2 = np.zeros((128, 128), dtype=np.float64)
    R2[:64, :64] = R
    R2[64:, 64:] = R
    tabs["m128T"] = np.ascontiguousarray(R2.T).astype(np.float16)
    j = np.arange(128)[:, None]
    i = np.arange(128)[None, :]
    half0 = (j >= i).astype(np.float64)
    half1 = (j <= i).astype(np.float64)
    maskA = np.concatenate([half0, half1], axis=1)
    madd = (1.0 - np.tile(maskA, (1, 2))) * (-60000.0)
    madd0 = madd.copy()
    madd0[:, 0:128] = -60000.0
    madd0[:, 256:384] = -60000.0
    tabs["mask_n"] = madd.astype(np.float16)
    tabs["mask_0"] = madd0.astype(np.float16)
    tabs["ident"] = np.eye(128).astype(np.float16)
    tabs["ones_col"] = np.ones((128, 1)).astype(ml_dtypes.bfloat16)
    tabs["ones_col16"] = np.ones((128, 1)).astype(np.float16)
    tabs["ones_row"] = np.ones((1, 128)).astype(ml_dtypes.bfloat16)
    consts = np.zeros((128, 8), dtype=np.float32)
    consts[:, 0] = -ESHIFT
    consts[:, 1] = 1.0 / 127.0
    consts[:, 2] = 1.0 / (127.0 * 127.0)
    consts[:, 3] = 1.0 / float(D)
    consts[:, 4] = EPS
    consts[:, 5] = -0.5
    consts[:, 6] = 1.0
    return tabs, consts


class Prog:
    ENGS = ("pe", "act", "dve", "pool", "sync")

    def __init__(self, nc, ctx):
        self.nc = nc
        self.sems = {}
        self.cnt = {}
        names = ["pe", "act", "dve", "pool",
                 "qx0", "qx1", "qrot", "qrot2", "qwk0", "qwk1", "qwk2", "qwk3",
                 "qwo0", "qwo1", "qwo2", "qwo3",
                 "qwu0", "qwu1", "qwu2", "qwu3", "qwd0", "qwd1", "qwd2", "qwd3", "qtab",
                 "qpl0", "qpl1", "qpl2", "qpl3", "qpl4", "qpl5",
                 "qis0", "qis1", "qis2", "qis3", "qis4", "qis5",
                 "qps0", "qps1", "qps2", "qps3", "qout0", "qout1",
                 "qil0", "qil1", "qil2", "qil3",
                 "qwg0", "qwg1", "qwg2", "qwg3",
                 "qwg4", "qwg5", "qwg6", "qwg7"]
        for name in names:
            self.sems[name] = ctx.enter_context(nc.semaphore("s_" + name))
            self.cnt[name] = 0
        self.ops = {k: [] for k in self.ENGS}

    def op(self, eng, fn, waits=(), inc=None, inc_k=1):
        waits = tuple(w for w in waits if w is not None)
        self.ops[eng].append((fn, waits, inc, inc_k))
        if inc is not None:
            self.cnt[inc] += inc_k
            return (inc, self.cnt[inc])
        return None

    def dma(self, eng, out, in_, waits=(), sem="qtab"):
        return self.op(eng, lambda e, o=out, i=in_: e.dma_start(out=o, in_=i),
                       waits=waits, inc=sem, inc_k=16)

    def emit(self, blk):
        engmap = {"pe": blk.tensor, "act": blk.scalar, "dve": blk.vector,
                  "pool": blk.gpsimd, "sync": blk.sync}
        for name in self.ENGS:
            ops = self.ops[name]
            if not ops:
                continue

            def body(e, ops=ops):
                hw = {}
                for fn, waits, inc, inc_k in ops:
                    need = {}
                    for (sname, val) in waits:
                        if hw.get(sname, 0) < val and need.get(sname, 0) < val:
                            need[sname] = val
                    for sname, val in need.items():
                        e.wait_ge(self.sems[sname], val)
                        hw[sname] = val
                    inst = fn(e)
                    if inc is not None:
                        inst.then_inc(self.sems[inc], inc_k)
            engmap[name](body)
        self.ops = {k: [] for k in self.ENGS}


def build_core_program(phases=4, p1_stage=4, p1_tiles=NT, p1_attn=6):
    nc = bass.Bass("TRN2", target_bir_lowering=False)
    tabs, consts_np = _host_tables()

    x_fm = nc.dram_tensor("x_fm", [SEQ_PER_CORE, D, S], F32, kind="ExternalInput").ap()
    w_qkv = nc.dram_tensor("w_qkv", [D, 3 * D], F32, kind="ExternalInput").ap()
    qk_shuf = nc.dram_tensor("qk_shuf", [16, 128, 1024], F32, kind="ExternalInput").ap()
    wo_shuf = nc.dram_tensor("wo_shuf", [8, 128, 1024], F32, kind="ExternalInput").ap()
    wgT_d = nc.dram_tensor("wgT", [D, HID], I32, kind="ExternalInput").ap()
    wu_shuf = nc.dram_tensor("wu_shuf", [HC, 128, 1024], I32, kind="ExternalInput").ap()
    wdT_d = nc.dram_tensor("wdT", [HID, D], I32, kind="ExternalInput").ap()
    out_fm = nc.dram_tensor("out_fm", [SEQ_PER_CORE, D, S], F32, kind="ExternalOutput").ap()

    wqk_sc = nc.dram_tensor("wqk_sc", [16, 128, 1024], F16, kind="Internal").ap()
    wv_sc = nc.dram_tensor("wv_sc", [8, 128, 1024], F16, kind="Internal").ap()
    wo_sc = nc.dram_tensor("wo_sc", [8, 128, 1024], F16, kind="Internal").ap()
    wg_sc = nc.dram_tensor("wg_sc", [8, 128, HID], BF16, kind="Internal").ap()
    wu_sc = nc.dram_tensor("wu_sc", [HC, 128, 1024], BF16, kind="Internal").ap()
    wd_sc = nc.dram_tensor("wd_sc", [HC, 128, 1024], BF16, kind="Internal").ap()

    cst = {k: nc.inline_tensor(v, "c_" + k).ap() for k, v in tabs.items()}
    cst["consts"] = nc.inline_tensor(consts_np, "c_consts").ap()

    with ExitStack() as top:
        sb = lambda name, shape, dt: top.enter_context(nc.sbuf_tensor(name, shape, dt))
        P = Prog(nc, top)

        res = sb("res", [128, DC * TOK], BF16)
        rows = sb("rows", [1, 2 * TOK], BF16)
        consts = sb("consts", [128, 8], F32)
        onec = sb("onec", [128, 1], BF16)
        onec16 = sb("onec16", [128, 1], F16)
        oner = sb("oner", [1, 128], BF16)

        with ExitStack() as ctx:
            psb = lambda name, shape, dt: ctx.enter_context(nc.sbuf_tensor(name, shape, dt))
            pinf = [psb(f"pinf_{i}", [128, 1024], F32) for i in range(4)]
            pot16 = [psb(f"pot16_{i}", [128, 1024], F16) for i in range(4)]

            P.dma("sync", consts[:], cst["consts"], sem="qtab")
            P.dma("sync", onec[:], cst["ones_col"], sem="qtab")
            P.dma("sync", onec16[:], cst["ones_col16"], sem="qtab")
            P.dma("sync", oner[:], cst["ones_row"], sem="qtab")

            freeL = {i: None for i in range(4)}
            freeS = {i: None for i in range(4)}
            fp16_jobs = []
            for oc in range(16):
                fp16_jobs.append((qk_shuf[oc], wqk_sc[oc]))
            for dc in range(DC):
                fp16_jobs.append((w_qkv[dc * 128:(dc + 1) * 128, 2048:3072], wv_sc[dc]))
            for ds in range(DC):
                fp16_jobs.append((wo_shuf[ds], wo_sc[ds]))
            for j, (srcap, dstap) in enumerate(fp16_jobs):
                bL = j % 4
                eng_c = "act" if j % 2 == 0 else "dve"
                ld = P.dma("sync", pinf[bL][:, 0:1024], srcap, waits=[freeL[bL]],
                           sem="qpl%d" % bL)
                cv = P.op(eng_c, lambda e, bL=bL, eng_c=eng_c: (
                    e.copy if eng_c == "act" else e.tensor_copy)(
                    out=pot16[bL][:], in_=pinf[bL][:]),
                    waits=[ld, freeS[bL]], inc=eng_c)
                freeL[bL] = cv
                freeS[bL] = P.dma("act", dstap, pot16[bL][:], waits=[cv],
                                  sem="qps%d" % bL)
            P.op("sync", lambda e: e.nop(),
                 waits=[("qps0", P.cnt["qps0"]), ("qps1", P.cnt["qps1"]),
                        ("qps2", P.cnt["qps2"]), ("qps3", P.cnt["qps3"]),
                        ("qtab", P.cnt["qtab"])])
            with nc.Block("prep") as blk:
                P.emit(blk)
        nc.all_engine_barrier()
        if phases < 2:
            return nc

        with ExitStack() as ctx:
            psb = lambda name, shape, dt: ctx.enter_context(nc.sbuf_tensor(name, shape, dt))
            pps = lambda name, shape, dt: ctx.enter_context(nc.psum_tensor(name, shape, dt))
            cos_q = psb("cos_q", [128, 512], F16); sin_q = psb("sin_q", [128, 512], F16)
            cos_k = psb("cos_k", [128, 512], F16); sin_k = psb("sin_k", [128, 512], F16)
            m128T = psb("m128T", [128, 128], F16)
            mask_n = psb("mask_n", [128, 512], F16)
            mask_0 = psb("mask_0", [128, 512], F16)
            ident = psb("ident", [128, 128], F16)
            wv_sb = psb("wv_sb", [128, 8 * 1024], F16)
            wqk_buf = psb("wqk_buf", [128, 4 * 1024], F16)
            x32 = [psb(f"x32_{i}", [128, 512], F32) for i in range(2)]
            x16 = [psb(f"x16_{i}", [128, DC * 512], F16) for i in range(2)]
            q_sb = psb("q_sb", [128, DC * 512], F16)
            k_sb = psb("k_sb", [128, DC * 512], F16)
            rotb = [psb(f"rotb{i}", [128, 1024], F16) for i in range(2)]
            q_lo = psb("q_lo", [128, DC * 512], F16)
            q_hi = psb("q_hi", [128, DC * 512], F16)
            k_lo = psb("k_lo", [128, DC * 640], F16)
            vhat = psb("vhat", [128, 5 * 1040], F16)
            pt_sb = [psb(f"pt_sb{i}", [128, 1024], F16) for i in range(2)]
            attn_fm = q_sb
            attn_tm = k_sb
            zsq = q_hi
            rowsc = psb("rowsc", [1, 1024], F32)
            recb = psb("recb", [128, 16], F32)
            pin32 = [psb(f"pin32_{i}", [128, 512], I32) for i in range(4)]
            potbf = [psb(f"potbf_{i}", [128, 512], BF16) for i in range(4)]
            ps_ab = pps("ps_ab", [128, 1024], F32)
            ps_sim = pps("ps_sim", [128, 1024], F32)
            ps_pv = pps("ps_pv", [128, 2048], F32)

            tl = []
            for t_sb, key in ((cos_q, "cos_q"), (sin_q, "sin_q"), (cos_k, "cos_k"),
                              (sin_k, "sin_k"), (m128T, "m128T"), (mask_n, "mask_n"),
                              (mask_0, "mask_0"), (ident, "ident")):
                tl.append(P.dma("sync", t_sb[:], cst[key]))
            for dc in range(DC):
                tl.append(P.dma("sync", wv_sb[:, dc * 1024:(dc + 1) * 1024], wv_sc[dc]))
            tables_tok = tl[-1]
            vz = P.op("pool", lambda e: e.memset(vhat[:], 0.0), inc="pool")
            ones_ready = None
            for sl in range(1, 5):
                ap = vhat.ap()[:, sl * 1040:(sl + 1) * 1040].rearrange(
                    "p (h c) -> p h c", h=16)[:, :, 64:65]
                ones_ready = P.op("pool", lambda e, ap=ap: e.memset(ap, 1.0),
                                  waits=[vz], inc="pool")

            pending_rows = [None]
            int_jobs = []
            for dc in range(DC):
                for p_ in range(6):
                    int_jobs.append((wgT_d[dc * 128:(dc + 1) * 128, p_ * 512:(p_ + 1) * 512],
                                     wg_sc[dc, :, p_ * 512:(p_ + 1) * 512]))
            for hc in range(HC):
                for p_ in range(2):
                    int_jobs.append((wu_shuf[hc, :, p_ * 512:(p_ + 1) * 512],
                                     wu_sc[hc, :, p_ * 512:(p_ + 1) * 512]))
            for hc in range(HC):
                for p_ in range(2):
                    int_jobs.append((wdT_d[hc * 128:(hc + 1) * 128, p_ * 512:(p_ + 1) * 512],
                                     wd_sc[hc, :, p_ * 512:(p_ + 1) * 512]))
            NJOB = len(int_jobs)
            ip = {"ld": 0, "cv": 0, "stq": 0, "xt_allow": 0,
                  "ld_tok": [None] * NJOB, "cv_tok": [None] * NJOB,
                  "st_tok": [None] * NJOB}

            xp = {"t": 1, "ld": 0, "cv": 0, "ld_tok": [None] * DC,
                  "cv_tok": [None] * DC}

            def x_step():
                if xp["t"] >= NT or xp["t"] > ip["xt_allow"]:
                    return
                t_ = xp["t"]
                xb = t_ % 2
                x16t = x16[xb]
                prev_ld, prev_cv = xp["ld"], xp["cv"]
                while xp["cv"] < prev_ld:
                    j = xp["cv"]
                    cv = P.op("act", lambda e, j=j, x16t=x16t: e.copy(
                        out=x16t[:, j * 512:(j + 1) * 512], in_=x32[j % 2][:]),
                        waits=[xp["ld_tok"][j],
                               st["x16_free"][xb] if j == 0 else None], inc="act")
                    st["x32_free"][j % 2] = cv
                    xp["cv_tok"][j] = cv
                    xp["cv"] = j + 1
                n = 0
                while n < 2 and xp["ld"] < DC and xp["ld"] < prev_cv + 2:
                    j = xp["ld"]
                    src = x_fm[t_ // TPS, j * 128:(j + 1) * 128,
                               (t_ % TPS) * TILE:(t_ % TPS) * TILE + TILE]
                    xp["ld_tok"][j] = P.dma("pool", x32[j % 2][:], src,
                                            waits=[st["x32_free"][j % 2]],
                                            sem="qx%d" % (j % 2))
                    xp["ld"] = j + 1
                    n += 1
                if xp["cv"] == DC:
                    st["x_ready"][t_] = xp["cv_tok"][DC - 1]
                    xp["t"] += 1
                    xp["ld"] = 0
                    xp["cv"] = 0

            def bg_prep():
                x_step()
                prev_ld, prev_cv, prev_st = ip["ld"], ip["cv"], ip["stq"]
                while ip["stq"] < prev_cv:
                    j = ip["stq"]
                    ip["st_tok"][j] = P.dma("act", int_jobs[j][1], potbf[j % 4][:],
                                            waits=[ip["cv_tok"][j]], sem="qis%d" % (j % 4))
                    ip["stq"] = j + 1
                while ip["cv"] < min(prev_ld, prev_st + 4):
                    j = ip["cv"]
                    eng_c = "dve" if j % 2 == 0 else "pool"
                    ip["cv_tok"][j] = P.op(eng_c, lambda e, b_=j % 4: e.tensor_copy(
                        out=potbf[b_][:], in_=pin32[b_][:]),
                        waits=[ip["ld_tok"][j],
                               ip["st_tok"][j - 4] if j >= 4 else None], inc=eng_c)
                    ip["cv"] = j + 1
                n_new = 0
                while n_new < 2 and ip["ld"] < min(NJOB, prev_cv + 4):
                    j = ip["ld"]
                    ip["ld_tok"][j] = P.dma("act", pin32[j % 4][:], int_jobs[j][0],
                                            waits=[ip["cv_tok"][j - 4] if j >= 4 else None],
                                            sem="qil%d" % (j % 4))
                    ip["ld"] = j + 1
                    n_new += 1

            st = {
                "x_ready": [None] * NT,
                "x32_free": [None, None],
                "x16_free": [None, None],
                "rope_q_done": None, "tcs_k_prev": None,
                "rot_free": [None, None], "qhi_evac": None,
                "qk_used": None, "ring_copied": None,
                "psA_free": None, "psB_free": None,
                "psS0_free": None, "psS1_free": None, "psPV_free": None,
                "wqk_free": [None] * 4,
                "attn_tm_free": None, "attn_fm_used": None,
                "zsq_free": [None, None], "pt_free": [None, None], "pt_m_used": None,
                "pend_pv": None,
                "recb_free": None,
                "stats_free": None, "rows_used": None,
            }
            zrows_toks = []

            for t in range(min(NT, p1_tiles)):
                seq = t // TPS
                col0 = (t % TPS) * TILE
                first_tile_of_seq = (col0 == 0)
                xb = t % 2
                x16t = x16[xb]

                if t == 0:
                    x_cast_toks = []
                    for dc in range(DC):
                        b = dc % 2
                        src = x_fm[seq, dc * 128:(dc + 1) * 128, col0:col0 + TILE]
                        ld = P.dma("pool", x32[b][:], src, waits=[st["x32_free"][b]],
                                   sem="qx%d" % b)
                        cv = P.op("act", lambda e, b=b, dc=dc, x16t=x16t: e.copy(
                            out=x16t[:, dc * 512:(dc + 1) * 512], in_=x32[b][:]),
                            waits=[ld, st["x16_free"][xb]], inc="act")
                        st["x32_free"][b] = cv
                        x_cast_toks.append(cv)
                    st["x_ready"][0] = x_cast_toks[-1]
                ip["xt_allow"] = t + 1
                x_ready = st["x_ready"][t]
                if pending_rows[0] is not None:
                    pending_rows[0]()
                    pending_rows[0] = None
                bg_prep()

                qk_ev = {}
                slots = [ps_ab[:, 0:512], ps_ab[:, 512:1024],
                         ps_sim[:, 0:512], ps_sim[:, 512:1024]]
                skey = ["psA_free", "psB_free", "psS0_free", "psS1_free"]
                def emit_v(vb, x16t=x16t, x_ready=x_ready):
                    w_ = vb // 2
                    half = vb % 2
                    sl = w_ + 1
                    mmtok = None
                    for dci in range(DC):
                        w = ([x_ready, tables_tok, ones_ready, st["psS1_free"]]
                             if dci == 0 else [])
                        mmtok = P.op("pe", lambda e, w_=w_, half=half, dci=dci,
                                     x16t=x16t: e.matmul(
                            out=ps_sim[:, 512:1024],
                            lhsT=x16t[:, dci * 512 + w_ * 128: dci * 512 + w_ * 128 + 128],
                            rhs=wv_sb[:, dci * 1024 + half * 512: dci * 1024 + (half + 1) * 512],
                            start=(dci == 0), stop=(dci == DC - 1)),
                            waits=w, inc="pe" if dci == DC - 1 else None)
                    hbase = half * 8
                    dst = vhat.ap()[:, sl * 1040 + hbase * 65: sl * 1040 + (hbase + 8) * 65]
                    dst = dst.rearrange("p (h c) -> p h c", h=8)[:, :, 0:64]
                    srcv = ps_sim.ap()[:, 512:1024].rearrange("p (h c) -> p h c", h=8)
                    ev = P.op("act", lambda e, dst=dst, srcv=srcv: e.copy(
                        out=dst, in_=srcv), waits=[mmtok, st["ring_copied"]],
                        inc="act")
                    st["psS1_free"] = ev
                    st["v_ev_last"] = ev

                for idx, oc in enumerate([8, 9, 10, 11, 12, 13, 14, 15,
                                          0, 1, 2, 3, 4, 5, 6, 7]):
                    if idx in (4, 8, 12):
                        bg_prep()
                    b = idx % 4
                    s4 = idx % 3
                    ld = P.dma("sync", wqk_buf[:, b * 1024:(b + 1) * 1024], wqk_sc[oc],
                               waits=[tables_tok, st["wqk_free"][b]], sem="qwk%d" % b)
                    ps = slots[s4]
                    ps_free = st[skey[s4]]
                    mmtok = None
                    for dci in range(DC):
                        w = [ld, x_ready, ps_free] if dci == 0 else []
                        mmtok = P.op("pe", lambda e, b=b, dci=dci, ps=ps, x16t=x16t: e.matmul(
                            out=ps,
                            lhsT=wqk_buf[:, b * 1024 + dci * 128: b * 1024 + (dci + 1) * 128],
                            rhs=x16t[:, dci * 512:(dci + 1) * 512],
                            start=(dci == 0), stop=(dci == DC - 1)),
                            waits=w, inc="pe" if dci == DC - 1 else None)
                    st["wqk_free"][b] = mmtok
                    dest = q_sb if oc < 8 else k_sb
                    ci = oc % 8
                    wv_ = [mmtok]
                    if oc < 8:
                        wv_.append(st["rope_q_done"])
                    else:
                        if ci < 2:
                            wv_.append(st["attn_tm_free"])
                        if st["tcs_k_prev"] is not None:
                            wv_.append(st["tcs_k_prev"][ci // 4])
                    ev = P.op("act", lambda e, dest=dest, ci=ci, ps=ps: e.copy(
                        out=dest[:, ci * 512:(ci + 1) * 512], in_=ps),
                        waits=wv_, inc="act")
                    qk_ev[oc] = ev
                    st[skey[s4]] = ev
                    if idx % 2 == 1:
                        emit_v(idx // 2)

                if p1_stage < 2:
                    continue
                v_ready = st["v_ev_last"]

                x3 = lambda t_: t_.ap().rearrange("p (c n) -> p c n", c=DC)
                bc2 = lambda tab: tab.ap().unsqueeze(1).broadcast_to([128, 2, 512])
                bc4_ = lambda tab: tab.ap().unsqueeze(1).broadcast_to([128, 4, 512])

                def rope_pass2(src_sb, scratch, dst_half, cos_t, sin_t, ev_half,
                               w_scr0, w_dst):
                    tas = []
                    tbs = [None] * 4
                    tcs = []
                    for h in range(2):
                        sl = src_sb.ap()[:, h * 2048:(h + 1) * 2048].rearrange(
                            "p (c n) -> p c n", c=4)
                        scl = scratch.ap()[:, h * 2048:(h + 1) * 2048].rearrange(
                            "p (c n) -> p c n", c=4)
                        w0 = [ev_half[h]] + w_scr0[h]
                        tas.append(P.op("dve", lambda e, sl=sl, scl=scl, cos_t=cos_t:
                                        e.tensor_tensor(out=scl, in0=sl, in1=bc4_(cos_t),
                                                        op=ALU.mult), waits=w0,
                                        inc="dve"))
                    for u in range(4):
                        h = u // 2
                        bq = u % 2
                        c0 = u * 1024
                        rdw = [ev_half[h], st["rot_free"][bq]]
                        toks = []
                        for g in (0, 64):
                            toks.append(P.dma("pool", rotb[bq][g:g + 32, :],
                                              src_sb[g + 32:g + 64, c0:c0 + 1024],
                                              waits=rdw,
                                              sem="qrot" if bq == 0 else "qrot2"))
                            rdw = []
                            toks.append(P.dma("pool", rotb[bq][g + 32:g + 64, :],
                                              src_sb[g:g + 32, c0:c0 + 1024],
                                              sem="qrot" if bq == 0 else "qrot2"))
                        tbs[u] = P.op("dve", lambda e, u=u, bq=bq, sin_t=sin_t:
                                      e.tensor_tensor(
                            out=src_sb.ap()[:, u * 1024:(u + 1) * 1024].rearrange(
                                "p (c n) -> p c n", c=2),
                            in0=rotb[bq].ap().rearrange("p (c n) -> p c n", c=2),
                            in1=bc2(sin_t), op=ALU.mult),
                            waits=[toks[-1], tas[h]], inc="dve")
                        st["rot_free"][bq] = tbs[u]
                    for h in range(2):
                        sl = src_sb.ap()[:, h * 2048:(h + 1) * 2048].rearrange(
                            "p (c n) -> p c n", c=4)
                        scl = scratch.ap()[:, h * 2048:(h + 1) * 2048].rearrange(
                            "p (c n) -> p c n", c=4)
                        tcs.append(P.op("dve", lambda e, sl=sl, scl=scl, h=h,
                                        dst_half=dst_half: e.tensor_tensor(
                            out=dst_half[h], in0=scl, in1=sl, op=ALU.add),
                            waits=[tbs[2 * h + 1]] + w_dst, inc="dve"))
                    return tcs

                k3 = k_lo.ap().rearrange("p (c n) -> p c n", c=DC)
                tcs_k = rope_pass2(k_sb, q_hi,
                                   [k3[:, 0:4, 128:640], k3[:, 4:8, 128:640]],
                                   cos_k, sin_k, [qk_ev[11], qk_ev[15]],
                                   [[st["qk_used"], st["qhi_evac"],
                                     st["zsq_free"][0], st["zsq_free"][1]],
                                    [st["qhi_evac"]]],
                                   [st["qk_used"], st["ring_copied"]])
                st["tcs_k_prev"] = tcs_k
                tcs_q = rope_pass2(q_sb, q_hi,
                                   [x3(q_lo)[:, 0:4, :], x3(q_lo)[:, 4:8, :]],
                                   cos_q, sin_q, [qk_ev[3], qk_ev[7]],
                                   [[tcs_k[0]], [tcs_k[1]]],
                                   [st["qk_used"]])
                qhi_ev_h = [None, None]
                for dci in range(DC):
                    ps = slots[dci % 3]
                    pf = st[skey[dci % 3]]
                    mm = P.op("pe", lambda e, dci=dci, ps=ps: e.matmul(
                        out=ps, lhsT=m128T[:], rhs=q_lo[:, dci * 512:(dci + 1) * 512],
                        start=True, stop=True), waits=[tcs_q[dci // 4], pf], inc="pe")
                    ev = P.op("act", lambda e, dci=dci, ps=ps: e.copy(
                        out=q_hi[:, dci * 512:(dci + 1) * 512], in_=ps),
                        waits=[mm, tcs_q[dci // 4]], inc="act")
                    st[skey[dci % 3]] = ev
                    qhi_ev_h[dci // 4] = ev
                st["qhi_evac"] = qhi_ev_h[1]
                bg_prep()

                if p1_stage < 3:
                    continue
                if first_tile_of_seq:
                    kprev = k_lo.ap().rearrange("p (c n) -> p c n", c=DC)[:, :, 0:128]
                    z1_ = P.op("pool", lambda e, kprev=kprev: e.memset(kprev, 0.0),
                               waits=[st["qk_used"], st["ring_copied"]], inc="pool")
                    seq_zero = P.op("pool", lambda e: e.memset(vhat[:, 0:1040], 0.0),
                                    waits=[z1_, st["pt_m_used"]], inc="pool")
                else:
                    seq_zero = st["ring_copied"]

                sim_last_all = None
                pend_tail = [None]
                for w_ in range(WPT):
                    wseq0 = first_tile_of_seq and w_ == 0
                    mtile = mask_0 if wseq0 else mask_n
                    pv_last = None
                    for g in range(4):
                        bankT = ps_sim if g % 2 == 0 else ps_ab
                        pt = pt_sb[g % 2]
                        if g % 2 == 0:
                            bfree = [st["psS0_free"], st["psS1_free"]]
                        else:
                            bfree = [st["psA_free"], st["psB_free"]]
                        first_w = [tcs_k[g // 2], qhi_ev_h[g // 2], v_ready,
                                   seq_zero, st["attn_tm_free"]] + bfree
                        for pp in range(2):
                            hp = 2 * g + pp
                            for sub in range(2):
                                pb = 64 * sub
                                base_i = hp * 512 + w_ * 128
                                klo_c = hp * 640
                                outc = sub * 512 + pp * 256
                                st_flag = (pp == 0)
                                w0 = first_w if (pp == 0 and sub == 0) else []
                                P.op("pe", lambda e, pb=pb, klo_c=klo_c, w_=w_,
                                     base_i=base_i, outc=outc, st_flag=st_flag,
                                     bankT=bankT: e.matmul(
                                    out=bankT[:, outc:outc + 128],
                                    lhsT=k_lo[pb:pb + 64, klo_c + 128 * w_: klo_c + 128 * w_ + 128],
                                    rhs=q_hi[pb:pb + 64, base_i:base_i + 128],
                                    start=st_flag, stop=False), waits=w0)
                                P.op("pe", lambda e, pb=pb, klo_c=klo_c, w_=w_,
                                     base_i=base_i, outc=outc, bankT=bankT: e.matmul(
                                    out=bankT[:, outc + 128:outc + 256],
                                    lhsT=k_lo[pb:pb + 64, klo_c + 128 + 128 * w_: klo_c + 256 + 128 * w_],
                                    rhs=q_lo[pb:pb + 64, base_i:base_i + 128],
                                    start=False, stop=False))
                        mk0 = P.op("pe", lambda e, bankT=bankT, mtile=mtile: e.matmul(
                            out=bankT[:, 0:512], lhsT=ident[:], rhs=mtile[:],
                            start=False, stop=True), inc="pe")
                        mk1 = P.op("pe", lambda e, bankT=bankT, mtile=mtile: e.matmul(
                            out=bankT[:, 512:1024], lhsT=ident[:], rhs=mtile[:],
                            start=False, stop=True), inc="pe")
                        sim_last_all = mk1
                        ex = P.op("act", lambda e, bankT=bankT, pt=pt: e.activation(
                            out=pt[:], in_=bankT[:], func=AF.Exp,
                            bias=consts[:, 0:1]),
                            waits=[mk0, mk1, st["pt_free"][g % 2]], inc="act")
                        if g % 2 == 0:
                            st["psS0_free"] = ex
                            st["psS1_free"] = ex
                        else:
                            st["psA_free"] = ex
                            st["psB_free"] = ex

                        def plan_pv(g=g, w_=w_, ex=ex, pt=pt):
                            pvl = None
                            for hh in range(4):
                                h = 4 * g + hh
                                sub = hh % 2
                                pp = hh // 2
                                ptc = sub * 512 + pp * 256
                                off = h * 128
                                w0 = [ex, v_ready, st["psPV_free"]] if hh == 0 else []
                                P.op("pe", lambda e, h=h, w_=w_, off=off, ptc=ptc,
                                     pt=pt: e.matmul(
                                    out=ps_pv[:, off:off + 65],
                                    lhsT=pt[:, ptc: ptc + 128],
                                    rhs=vhat[:, w_ * 1040 + h * 65: w_ * 1040 + (h + 1) * 65],
                                    start=True, stop=False), waits=w0)
                                pvl = P.op("pe", lambda e, h=h, w_=w_, off=off, ptc=ptc,
                                           pt=pt: e.matmul(
                                    out=ps_pv[:, off:off + 65],
                                    lhsT=pt[:, ptc + 128: ptc + 256],
                                    rhs=vhat[:, (w_ + 1) * 1040 + h * 65: (w_ + 1) * 1040 + (h + 1) * 65],
                                    start=False, stop=True),
                                    inc="pe" if hh == 3 else None)
                            st["pt_free"][g % 2] = pvl
                            return pvl
                        if g == 1 and pend_tail[0] is not None:
                            pend_tail[0]()
                            pend_tail[0] = None
                        if st["pend_pv"] is not None:
                            pv_last = st["pend_pv"]()
                        st["pend_pv"] = plan_pv
                    pv_last = st["pend_pv"]()
                    st["pend_pv"] = None
                    st["pt_m_used"] = pv_last

                    def plan_tail(w_=w_, pv_last=pv_last):
                        pv3 = ps_pv.ap().rearrange("p (h c) -> p h c", h=16)
                        rc = P.op("dve", lambda e, pv3=pv3: e.reciprocal(
                            out=recb.ap().unsqueeze(2), in_=pv3[:, :, 64:65]),
                            waits=[pv_last, st["recb_free"]], inc="dve")
                        at3 = attn_tm.ap()[:, 0:1024].rearrange("p (h c) -> p h c", h=16)
                        ev = P.op("dve", lambda e, pv3=pv3, at3=at3: e.tensor_tensor(
                            out=at3, in0=pv3[:, :, 0:64],
                            in1=recb.ap().unsqueeze(2).broadcast_to([128, 16, 64]),
                            op=ALU.mult), waits=[rc, st["attn_tm_free"]], inc="dve")
                        st["recb_free"] = ev
                        s0v = ps_pv.ap().bitcast(F16)[:, 0:1024]
                        trs = []
                        for dci in range(DC):
                            trs.append(P.op("pe", lambda e, dci=dci, s0v=s0v: e.transpose(
                                out=s0v[:, dci * 128:(dci + 1) * 128],
                                in_=attn_tm[:, dci * 128:(dci + 1) * 128], identity=ident[:]),
                                waits=[ev] if dci == 0 else [],
                                inc="pe" if dci == DC - 1 else None))
                        cp1 = P.op("dve", lambda e, s0v=s0v, w_=w_: e.tensor_copy(
                            out=attn_fm.ap().rearrange("p (c n) -> p c n", c=DC)[:, :, w_ * 128:(w_ + 1) * 128],
                            in_=s0v.rearrange("p (c n) -> p c n", c=DC)),
                            waits=[trs[-1], st["attn_fm_used"]], inc="dve")
                        st["psPV_free"] = cp1
                        st["attn_tm_free"] = trs[-1]
                        return cp1
                    pend_tail[0] = plan_tail
                    bg_prep()
                cp1 = pend_tail[0]()
                pend_tail[0] = None
                bg_prep()
                st["qk_used"] = sim_last_all

                ksrc = k_lo.ap().rearrange("p (c n) -> p c n", c=DC)[:, :, 512:640]
                kdst = k_lo.ap().rearrange("p (c n) -> p c n", c=DC)[:, :, 0:128]
                rc1 = P.op("pool", lambda e, ksrc=ksrc, kdst=kdst: e.tensor_copy(
                    out=kdst, in_=ksrc), waits=[st["qk_used"]], inc="pool")
                rc2 = P.op("pool", lambda e: e.tensor_copy(
                    out=vhat[:, 0:1040], in_=vhat[:, 4 * 1040:5 * 1040]),
                    waits=[rc1, st["pt_m_used"]], inc="pool")
                st["ring_copied"] = rc2

                if p1_stage < 4:
                    continue
                z1_toks = []
                last_opj_mm = [None]
                for ds in range(DC):
                    ps = ps_ab[:, 0:512] if ds % 2 == 0 else ps_ab[:, 512:1024]
                    pf = st["psA_free"] if ds % 2 == 0 else st["psB_free"]
                    wb = ds % 4
                    ldo = P.dma("sync", wqk_buf[:, wb * 1024:(wb + 1) * 1024], wo_sc[ds],
                                waits=[st["wqk_free"][wb]], sem="qwo%d" % wb)
                    mmtok = None
                    for hoc in range(DC):
                        w0 = [cp1, pf, ldo] if hoc == 0 else []
                        mmtok = P.op("pe", lambda e, hoc=hoc, ps=ps, wb=wb: e.matmul(
                            out=ps,
                            lhsT=wqk_buf[:, wb * 1024 + hoc * 128: wb * 1024 + (hoc + 1) * 128],
                            rhs=attn_fm[:, hoc * 512:(hoc + 1) * 512],
                            start=(hoc == 0), stop=(hoc == DC - 1)),
                            waits=w0, inc="pe" if hoc == DC - 1 else None)
                    st["wqk_free"][wb] = mmtok
                    last_opj_mm[0] = mmtok
                    rcol = ds * TOK + t * 512
                    zt = P.op("dve", lambda e, ds=ds, ps=ps, rcol=rcol, x16t=x16t: e.tensor_tensor(
                        out=res[:, rcol:rcol + 512], in0=ps,
                        in1=x16t[:, ds * 512:(ds + 1) * 512], op=ALU.add),
                        waits=[mmtok], inc="dve")
                    z1_toks.append(zt)
                    if ds % 2 == 0:
                        st["psA_free"] = zt
                    else:
                        st["psB_free"] = zt
                st["x16_free"][xb] = z1_toks[-1]
                st["attn_fm_used"] = last_opj_mm[0]
                st["rope_q_done"] = last_opj_mm[0]

                sum_last = None
                sq_last = None
                for ds in range(DC):
                    rcol = ds * TOK + t * 512
                    zb = ds % 2
                    sq = P.op("dve", lambda e, rcol=rcol, zb=zb: e.tensor_tensor(
                        out=zsq[:, zb * 512:(zb + 1) * 512], in0=res[:, rcol:rcol + 512],
                        in1=res[:, rcol:rcol + 512], op=ALU.mult),
                        waits=[z1_toks[ds], st["zsq_free"][zb], st["qk_used"],
                               st["rows_used"] if ds == 0 else None], inc="dve")
                    w0 = [z1_toks[ds], st["psA_free"] if ds == 0 else None]
                    sum_last = P.op("pe", lambda e, rcol=rcol, ds=ds: e.matmul(
                        out=ps_ab[0:1, 0:512], lhsT=onec[:], rhs=res[:, rcol:rcol + 512],
                        start=(ds == 0), stop=(ds == DC - 1)),
                        waits=w0, inc="pe")
                    w1 = [sq, st["psB_free"] if ds == 0 else None]
                    sq_last = P.op("pe", lambda e, zb=zb, ds=ds: e.matmul(
                        out=ps_ab[0:1, 512:1024], lhsT=onec16[:],
                        rhs=zsq[:, zb * 512:(zb + 1) * 512],
                        start=(ds == 0), stop=(ds == DC - 1)),
                        waits=w1, inc="pe")
                    st["zsq_free"][zb] = sq_last
                def mk_rows(t=t, sum_last=sum_last, sq_last=sq_last):
                    d1 = P.op("dve", lambda e, t=t: e.tensor_scalar(
                              out=rows[0:1, t * 512:(t + 1) * 512], in0=ps_ab[0:1, 0:512],
                              scalar1=consts[0:1, 3:4], scalar2=None, op0=ALU.mult),
                              waits=[sum_last, st["rows_used"]], inc="dve")
                    d2 = P.op("dve", lambda e, t=t: e.tensor_tensor(
                              out=rowsc[0:1, 0:512],
                              in0=rows[0:1, t * 512:(t + 1) * 512],
                              in1=rows[0:1, t * 512:(t + 1) * 512], op=ALU.mult),
                              waits=[d1], inc="dve")
                    d3 = P.op("dve", lambda e: e.scalar_tensor_tensor(
                              out=rowsc[0:1, 512:1024], in0=ps_ab[0:1, 512:1024],
                              scalar=consts[0:1, 3:4], in1=rowsc[0:1, 0:512],
                              op0=ALU.mult, op1=ALU.subtract),
                              waits=[sq_last, d2], inc="dve")
                    a4 = P.op("act", lambda e: e.activation(out=rowsc[0:1, 0:512],
                              in_=rowsc[0:1, 512:1024], func=AF.Ln, bias=consts[0:1, 4:5]),
                              waits=[d3, st["rows_used"]], inc="act")
                    a5 = P.op("act", lambda e, t=t: e.activation(
                              out=rows[0:1, TOK + t * 512: TOK + (t + 1) * 512],
                              in_=rowsc[0:1, 0:512], func=AF.Exp,
                              scale=consts[0:1, 5:6]), waits=[a4], inc="act")
                    zrows_toks.append(a5)
                    st["psA_free"] = d1
                    st["psB_free"] = d3
                    st["rows_used"] = a5
                st["psA_free"] = sum_last
                st["psB_free"] = sq_last
                bg_prep()
                if t == NT - 1:
                    mk_rows()
                else:
                    pending_rows[0] = mk_rows

            while ip["stq"] < NJOB:
                bg_prep()
            P.op("sync", lambda e: e.nop(),
                 waits=[("qis0", P.cnt["qis0"]), ("qis1", P.cnt["qis1"]),
                        ("qis2", P.cnt["qis2"]), ("qis3", P.cnt["qis3"])])
            p1_z1 = list(zrows_toks)
            with nc.Block("p1") as blk:
                P.emit(blk)
        nc.all_engine_barrier()
        if phases < 3:
            return nc

        with ExitStack() as ctx:
            psb = lambda name, shape, dt: ctx.enter_context(nc.sbuf_tensor(name, shape, dt))
            pps = lambda name, shape, dt: ctx.enter_context(nc.psum_tensor(name, shape, dt))
            wg_sb = psb("wg_sb", [128, DC * HID], BF16)
            wu_buf = psb("wu_buf", [128, 4 * 1024], BF16)
            wd_buf = psb("wd_buf", [128, 4 * 1024], BF16)
            y_sb = [psb(f"y_sb{i}", [128, DC * 512], BF16) for i in range(2)]
            y2 = psb("y2", [128, 512], BF16)
            a_sb = psb("a_sb", [128, 2 * 512], BF16)
            h_sb = psb("h_sb", [128, HC * 512], BF16)
            mbc = psb("mbc", [128, 512], BF16)
            rbc = psb("rbc", [128, 512], BF16)
            ssq = psb("ssq", [128, 4096], BF16)
            rowsc2 = psb("rowsc2", [1, 1024], F32)
            mbc2 = psb("mbc2", [128, 512], BF16)
            rbc2 = psb("rbc2", [128, 512], BF16)
            o32 = [psb(f"o32_{i}", [128, 512], F32) for i in range(2)]
            pg = [pps("pg0", [128, 512], F32), pps("pg1", [128, 512], F32)]
            pu = [pps("pu0", [128, 512], F32), pps("pu1", [128, 512], F32)]
            pd = [pps(f"pd{i}", [128, 512], F32) for i in range(4)]

            ldw = []
            for dc in range(DC):
                ldw.append(P.dma("sync", wg_sb[:, dc * HID:(dc + 1) * HID], wg_sc[dc],
                                 sem="qwg%d" % dc))

            st2 = {
                "pg_free": [None, None], "pu_free": [None, None],
                "pd_free": [None] * 4, "a_free": [None, None],
                "h_free": None, "y_free": [None, None], "bc_free": None,
                "y_ready": [None] * (NT + 1),
                "wd_free": [None] * 4, "wu_free": [None] * 4, "spre": [None] * NT,
                "yrows_done": [None] * (NT + 1),
            }
            st3 = {"ssq_free": None, "rows2_used": [None, None],
                   "bc_free": None, "rowsc_free": None, "stage": [None] * NT,
                   "sq": [None] * NT, "o32_free": [None, None]}

            def plan_y(t):
                yb = t % 2
                bm = P.op("pe", lambda e, t=t: e.matmul(
                    out=pg[0][:], lhsT=oner[:], rhs=rows[0:1, t * 512:(t + 1) * 512],
                    start=True, stop=True),
                    waits=[st2["pg_free"][0], st2["bc_free"]], inc="pe")
                em = P.op("act", lambda e: e.copy(out=mbc[:], in_=pd[0][:] if False else pg[0][:]),
                          waits=[bm], inc="act")
                br = P.op("pe", lambda e, t=t: e.matmul(
                    out=pg[1][:], lhsT=oner[:], rhs=rows[0:1, TOK + t * 512: TOK + (t + 1) * 512],
                    start=True, stop=True), waits=[st2["pg_free"][1], em], inc="pe")
                er = P.op("act", lambda e: e.copy(out=rbc[:], in_=pg[1][:]),
                          waits=[br], inc="act")
                st2["pg_free"][0] = em
                st2["pg_free"][1] = er
                st2["yrows_done"][t] = er
                prev_ty2 = None
                for dc in range(DC):
                    rcol = dc * TOK + t * 512
                    w0 = [em, st2["y_free"][yb], prev_ty2]
                    ty1 = P.op("dve", lambda e, rcol=rcol: e.tensor_tensor(
                        out=y2[:], in0=res[:, rcol:rcol + 512], in1=mbc[:], op=ALU.subtract),
                        waits=w0, inc="dve")
                    ty2 = P.op("dve", lambda e, dc=dc, yb=yb: e.tensor_tensor(
                        out=y_sb[yb][:, dc * 512:(dc + 1) * 512], in0=y2[:], in1=rbc[:],
                        op=ALU.mult), waits=[ty1, er], inc="dve")
                    prev_ty2 = ty2
                st2["bc_free"] = prev_ty2
                st2["y_ready"][t] = prev_ty2

            def p2b_sq(t, spre_toks):
                sq_tts = []
                for ds in range(DC):
                    rcol = ds * TOK + t * 512
                    sq = P.op("dve", lambda e, rcol=rcol, ds=ds: e.tensor_tensor(
                        out=ssq[:, ds * 512:(ds + 1) * 512],
                        in0=res[:, rcol:rcol + 512],
                        in1=res[:, rcol:rcol + 512], op=ALU.mult),
                        waits=[spre_toks[ds], st3["ssq_free"] if ds == 0 else None],
                        inc="dve")
                    sq_tts.append(sq)
                st3["sq"][t] = (sq_tts, spre_toks)

            def p2b_stage(t):
                tp_ = t % 2
                sq_tts, spre_toks = st3["sq"][t]
                sum_last = None
                sq_last = None
                for ds in range(DC):
                    rcol = ds * TOK + t * 512
                    sum_last = P.op("pe", lambda e, rcol=rcol, ds=ds: e.matmul(
                        out=pd[0][0:1, :], lhsT=onec[:], rhs=res[:, rcol:rcol + 512],
                        start=(ds == 0), stop=(ds == DC - 1)),
                        waits=[spre_toks[ds]], inc="pe" if ds == DC - 1 else None)
                    sq_last = P.op("pe", lambda e, ds=ds: e.matmul(
                        out=pd[1][0:1, :], lhsT=onec[:],
                        rhs=ssq[:, ds * 512:(ds + 1) * 512],
                        start=(ds == 0), stop=(ds == DC - 1)),
                        waits=[sq_tts[ds]], inc="pe")
                st3["ssq_free"] = sq_last
                d1 = P.op("dve", lambda e, t=t: e.tensor_scalar(
                          out=rows[0:1, t * 512:(t + 1) * 512], in0=pd[0][0:1, :],
                          scalar1=consts[0:1, 3:4], scalar2=None, op0=ALU.mult),
                          waits=[sum_last, st3["rows2_used"][tp_],
                                 st2["yrows_done"][t]], inc="dve")
                d2 = P.op("dve", lambda e, t=t: e.tensor_tensor(
                          out=rowsc2[0:1, 0:512],
                          in0=rows[0:1, t * 512:(t + 1) * 512],
                          in1=rows[0:1, t * 512:(t + 1) * 512], op=ALU.mult),
                          waits=[d1, st3["rowsc_free"]], inc="dve")
                d3 = P.op("dve", lambda e: e.scalar_tensor_tensor(
                          out=rowsc2[0:1, 512:1024], in0=pd[1][0:1, :],
                          scalar=consts[0:1, 3:4], in1=rowsc2[0:1, 0:512],
                          op0=ALU.mult, op1=ALU.subtract),
                          waits=[sq_last, d2], inc="dve")
                a4 = P.op("act", lambda e: e.activation(
                          out=rowsc2[0:1, 0:512],
                          in_=rowsc2[0:1, 512:1024], func=AF.Ln,
                          bias=consts[0:1, 4:5]),
                          waits=[d3, st3["rows2_used"][tp_]], inc="act")
                a5 = P.op("act", lambda e, t=t: e.activation(
                          out=rows[0:1, TOK + t * 512: TOK + (t + 1) * 512],
                          in_=rowsc2[0:1, 0:512], func=AF.Exp,
                          scale=consts[0:1, 5:6]),
                          waits=[a4, st2["yrows_done"][t]], inc="act")
                st3["rowsc_free"] = a4
                st3["rows2_used"][tp_] = a5
                st2["pd_free"][0] = d1
                st2["pd_free"][1] = d3
                st3["stage"][t] = (d1, a5)

            def p2b_fin(t):
                d1, a5 = st3["stage"][t]
                pb_ = t % 2
                seq = t // TPS
                col0 = (t % TPS) * TILE
                bm = P.op("pe", lambda e, t=t: e.matmul(
                    out=pd[2][:], lhsT=oner[:],
                    rhs=rows[0:1, t * 512:(t + 1) * 512],
                    start=True, stop=True),
                    waits=[d1, st2["pd_free"][2]], inc="pe")
                em = P.op("act", lambda e: e.copy(
                    out=mbc2[:], in_=pd[2][:]),
                    waits=[bm, st3["bc_free"]], inc="act")
                br = P.op("pe", lambda e, t=t: e.matmul(
                    out=pd[3][:], lhsT=oner[:],
                    rhs=rows[0:1, TOK + t * 512: TOK + (t + 1) * 512],
                    start=True, stop=True),
                    waits=[a5, st2["pd_free"][3], em], inc="pe")
                er = P.op("act", lambda e: e.copy(
                    out=rbc2[:], in_=pd[3][:]),
                    waits=[br, st3["bc_free"]], inc="act")
                od_toks = [None] * DC
                tt2 = None
                for ds in range(DC):
                    rcol = ds * TOK + t * 512
                    sl = ds % 2
                    w1 = [em]
                    if ds >= 2:
                        w1.append(od_toks[ds - 2])
                    else:
                        w1.append(st3["o32_free"][sl])
                    tt1 = P.op("pool", lambda e, rcol=rcol, sl=sl: e.tensor_tensor(
                        out=o32[sl][:], in0=res[:, rcol:rcol + 512],
                        in1=mbc2[:], op=ALU.subtract),
                        waits=w1, inc="pool")
                    tt2 = P.op("dve", lambda e, sl=sl: e.tensor_tensor(
                        out=o32[sl][:], in0=o32[sl][:],
                        in1=rbc2[:], op=ALU.mult),
                        waits=[tt1, er], inc="dve")
                    od_toks[ds] = P.dma("pool",
                                        out_fm[seq, ds * 128:(ds + 1) * 128,
                                               col0:col0 + TILE],
                                        o32[sl][:], waits=[tt2],
                                        sem="qout%d" % (ds % 2))
                st2["pd_free"][2] = em
                st2["pd_free"][3] = er
                st3["bc_free"] = tt2
                st3["o32_free"][0] = od_toks[6]
                st3["o32_free"][1] = od_toks[7]

            plan_y(0)
            for t in range(NT):
                yb = t % 2
                yt = y_sb[yb]
                y_ready = st2["y_ready"][t]

                h_toks = []
                for hc in range(HC):
                    if hc == 3 and t >= 1:
                        p2b_stage(t - 1)
                    gb = hc % 2
                    ub = hc % 4
                    ldu = P.dma("sync", wu_buf[:, ub * 1024:(ub + 1) * 1024], wu_sc[hc],
                                waits=[st2["wu_free"][ub]], sem="qwu%d" % ub)
                    g_last = None
                    u_last = None
                    for dci in range(DC):
                        w0 = [y_ready, st2["pg_free"][gb]] if dci == 0 else []
                        if t == 0 and hc == 0:
                            w0 = w0 + [ldw[dci]]
                        g_last = P.op("pe", lambda e, hc=hc, dci=dci, gb=gb, yt=yt: e.matmul(
                            out=pg[gb][:],
                            lhsT=wg_sb[:, dci * HID + hc * 128: dci * HID + (hc + 1) * 128],
                            rhs=yt[:, dci * 512:(dci + 1) * 512],
                            start=(dci == 0), stop=(dci == DC - 1)),
                            waits=w0, inc="pe" if dci == DC - 1 else None)
                    for dci in range(DC):
                        w0 = [ldu, y_ready, st2["pu_free"][gb]] if dci == 0 else []
                        u_last = P.op("pe", lambda e, dci=dci, gb=gb, ub=ub, yt=yt: e.matmul(
                            out=pu[gb][:],
                            lhsT=wu_buf[:, ub * 1024 + dci * 128: ub * 1024 + (dci + 1) * 128],
                            rhs=yt[:, dci * 512:(dci + 1) * 512],
                            start=(dci == 0), stop=(dci == DC - 1)),
                            waits=w0, inc="pe" if dci == DC - 1 else None)
                    st2["wu_free"][ub] = u_last
                    sl = P.op("act", lambda e, gb=gb: e.activation(
                        out=a_sb[:, gb * 512:(gb + 1) * 512], in_=pg[gb][:], func=AF.Silu,
                        scale=consts[:, 1:2]),
                        waits=[g_last, st2["a_free"][gb]], inc="act")
                    st2["pg_free"][gb] = sl
                    ht = P.op("dve", lambda e, hc=hc, gb=gb: e.tensor_tensor(
                        out=h_sb[:, hc * 512:(hc + 1) * 512],
                        in0=a_sb[:, gb * 512:(gb + 1) * 512], in1=pu[gb][:], op=ALU.mult),
                        waits=[sl, u_last, st2["h_free"]], inc="dve")
                    st2["pu_free"][gb] = ht
                    st2["a_free"][gb] = ht
                    h_toks.append(ht)

                if t + 1 < NT:
                    plan_y(t + 1)
                if t >= 1:
                    p2b_fin(t - 1)

                dbank = [pd[0], pd[1], pd[2], pd[3], pg[0], pg[1], pu[0], pu[1]]
                dfree = (st2["pd_free"]
                         + [st2["pg_free"][0], st2["pg_free"][1],
                            st2["pu_free"][0], st2["pu_free"][1]])
                spre_toks = []
                acc = [None] * 8
                for hc in range(HC):
                    wb2 = hc % 4
                    ld = P.dma("sync", wd_buf[:, wb2 * 1024:(wb2 + 1) * 1024], wd_sc[hc],
                               waits=[st2["wd_free"][wb2]], sem="qwd%d" % wb2)
                    mmt = None
                    for dsi in range(8):
                        w0 = [ld, h_toks[hc]]
                        if hc == 0:
                            w0.append(dfree[dsi])
                        need_inc = (dsi == 7) or (hc == HC - 1)
                        mmt = P.op("pe", lambda e, wb2=wb2, dsi=dsi, hc=hc: e.matmul(
                            out=dbank[dsi][:],
                            lhsT=wd_buf[:, wb2 * 1024 + dsi * 128: wb2 * 1024 + (dsi + 1) * 128],
                            rhs=h_sb[:, hc * 512:(hc + 1) * 512],
                            start=(hc == 0), stop=(hc == HC - 1)),
                            waits=w0 if (dsi == 0 or hc == 0) else [],
                            inc="pe" if need_inc else None)
                        acc[dsi] = mmt
                    st2["wd_free"][wb2] = mmt
                for dsi in range(8):
                    rcol = dsi * TOK + t * 512
                    sp = P.op("dve", lambda e, dsi=dsi, rcol=rcol, yt=yt: e.scalar_tensor_tensor(
                        out=res[:, rcol:rcol + 512], in0=dbank[dsi][:], scalar=consts[:, 2:3],
                        in1=yt[:, dsi * 512:(dsi + 1) * 512], op0=ALU.mult, op1=ALU.add),
                        waits=[acc[dsi]], inc="dve")
                    if dsi < 4:
                        st2["pd_free"][dsi] = sp
                    elif dsi < 6:
                        st2["pg_free"][dsi - 4] = sp
                    else:
                        st2["pu_free"][dsi - 6] = sp
                    spre_toks.append(sp)
                st2["y_free"][yb] = spre_toks[-1]
                st2["h_free"] = spre_toks[-1]
                st2["spre"][t] = spre_toks[-1]
                p2b_sq(t, spre_toks)
            p2b_stage(NT - 1)
            p2b_fin(NT - 1)
            P.op("pool", lambda e: e.nop(),
                 waits=[("qout0", P.cnt["qout0"]), ("qout1", P.cnt["qout1"])],
                 inc=None)
            with nc.Block("p2a") as blk:
                P.emit(blk)
        nc.all_engine_barrier()
        if phases < 4:
            return nc

    return nc



_NC_CACHE = {}

def _get_program():
    if "nc" not in _NC_CACHE:
        _NC_CACHE["nc"] = build_core_program()
    return _NC_CACHE["nc"]


def stage_weights(L0_attn_W_qkv, L0_attn_W_out, L0_ffn_w_gate, L0_ffn_w_up,
                  L0_ffn_w_down):
    w_qkv = np.ascontiguousarray(np.asarray(L0_attn_W_qkv, dtype=np.float32))
    w_out = np.asarray(L0_attn_W_out, dtype=np.float32)
    wg = np.asarray(L0_ffn_w_gate, dtype=np.int32)
    wu = np.asarray(L0_ffn_w_up, dtype=np.int32)
    wd = np.asarray(L0_ffn_w_down, dtype=np.int32)
    qk_shuf = np.ascontiguousarray(
        w_qkv[:, 0:2048].reshape(8, 128, 16, 128).transpose(2, 1, 0, 3).reshape(16, 128, 1024))
    wo_shuf = np.ascontiguousarray(
        w_out.reshape(8, 128, 8, 128).transpose(2, 1, 0, 3).reshape(8, 128, 1024))
    wgT = np.ascontiguousarray(wg.T)
    wuT = np.ascontiguousarray(wu.T)
    wu_shuf = np.ascontiguousarray(
        wuT.reshape(8, 128, 24, 128).transpose(2, 1, 0, 3).reshape(24, 128, 1024))
    wdT = np.ascontiguousarray(wd.T)
    return {"w_qkv": w_qkv, "qk_shuf": qk_shuf, "wo_shuf": wo_shuf,
            "wgT": wgT, "wu_shuf": wu_shuf, "wdT": wdT}


def kernel(L0_attn_x, L0_attn_W_qkv, L0_attn_W_out, L0_ffn_w_gate, L0_ffn_w_up,
           L0_ffn_w_down):
    x = np.ascontiguousarray(np.asarray(L0_attn_x, dtype=np.float32))
    wmap = stage_weights(L0_attn_W_qkv, L0_attn_W_out, L0_ffn_w_gate,
                         L0_ffn_w_up, L0_ffn_w_down)

    nc = _get_program()
    in_maps = []
    for c in range(NCORES):
        xc = x[c * SEQ_PER_CORE:(c + 1) * SEQ_PER_CORE]
        x_fm = np.ascontiguousarray(xc.transpose(0, 2, 1))
        m = {"x_fm": x_fm}
        m.update(wmap)
        in_maps.append(m)
    res = run_bass_kernel_spmd(nc, in_maps, core_ids=list(range(NCORES)))
    out = np.empty((B, S, D), dtype=np.float32)
    for c in range(NCORES):
        o = res.results[c]["out_fm"]
        out[c * SEQ_PER_CORE:(c + 1) * SEQ_PER_CORE] = o.transpose(0, 2, 1)
    return out


if __name__ == "__main__":
    nc = build_core_program()
    print("built ok, n_inst =", nc.n_instructions())


# version 28
# speedup vs baseline: 1.0352x; 1.0352x over previous
import numpy as np
import ml_dtypes
import concourse.bass as bass
import concourse.mybir as mybir
from concourse.bass_utils import run_bass_kernel_spmd
from contextlib import ExitStack

F32 = mybir.dt.float32
F16 = mybir.dt.float16
BF16 = mybir.dt.bfloat16
I32 = mybir.dt.int32
AF = mybir.ActivationFunctionType
ALU = mybir.AluOpType

NCORES = 8
B, S, D = 16, 2048, 1024
H, DH, WS = 16, 64, 128
HID = 3072
SEQ_PER_CORE = B // NCORES
TOK = SEQ_PER_CORE * S
TILE = 512
NT = TOK // TILE
TPS = NT // SEQ_PER_CORE
WPT = TILE // WS
DC = D // 128
HC = HID // 128
THETA = 10000.0
ESHIFT = 5.0
EPS = 1e-5


def _host_tables():
    inv_freq = 1.0 / THETA ** (np.arange(0, DH, 2, dtype=np.float64) / DH)
    d_in_head = np.arange(128) % 64
    freq_p = inv_freq[d_in_head % 32]
    pos = np.arange(WS, dtype=np.float64)
    ang = pos[None, :] * freq_p[:, None]
    cos = np.cos(ang)
    sin = np.sin(ang)
    sign = np.where(d_in_head < 32, -1.0, 1.0)[:, None]
    d_in_head2 = d_in_head
    sign = np.where((d_in_head2 % 64) < 32, -1.0, 1.0)[:, None]
    ssin = sin * sign
    cos4 = np.tile(cos, (1, WPT))
    ssin4 = np.tile(ssin, (1, WPT))
    scale = DH ** (-0.5)
    tabs = {
        "cos_q": (cos4 * scale).astype(np.float16),
        "sin_q": (ssin4 * scale).astype(np.float16),
        "cos_k": cos4.astype(np.float16),
        "sin_k": ssin4.astype(np.float16),
    }
    a128 = 128.0 * inv_freq[np.arange(64) % 32]
    c1, s1 = np.cos(a128), np.sin(a128)
    R = np.zeros((64, 64), dtype=np.float64)
    for d in range(64):
        R[d, d] = c1[d]
        if d < 32:
            R[d, d + 32] = -s1[d]
        else:
            R[d, d - 32] = s1[d]
    R2 = np.zeros((128, 128), dtype=np.float64)
    R2[:64, :64] = R
    R2[64:, 64:] = R
    tabs["m128T"] = np.ascontiguousarray(R2.T).astype(np.float16)
    j = np.arange(128)[:, None]
    i = np.arange(128)[None, :]
    half0 = (j >= i).astype(np.float64)
    half1 = (j <= i).astype(np.float64)
    maskA = np.concatenate([half0, half1], axis=1)
    madd = (1.0 - np.tile(maskA, (1, 2))) * (-60000.0)
    madd0 = madd.copy()
    madd0[:, 0:128] = -60000.0
    madd0[:, 256:384] = -60000.0
    tabs["mask_n"] = madd.astype(np.float16)
    tabs["mask_0"] = madd0.astype(np.float16)
    tabs["ident"] = np.eye(128).astype(np.float16)
    tabs["ones_col"] = np.ones((128, 1)).astype(ml_dtypes.bfloat16)
    tabs["ones_col16"] = np.ones((128, 1)).astype(np.float16)
    tabs["ones_row"] = np.ones((1, 128)).astype(ml_dtypes.bfloat16)
    consts = np.zeros((128, 8), dtype=np.float32)
    consts[:, 0] = -ESHIFT
    consts[:, 1] = 1.0 / 127.0
    consts[:, 2] = 1.0 / (127.0 * 127.0)
    consts[:, 3] = 1.0 / float(D)
    consts[:, 4] = EPS
    consts[:, 5] = -0.5
    consts[:, 6] = 1.0
    return tabs, consts


class Prog:
    ENGS = ("pe", "act", "dve", "pool", "sync")

    def __init__(self, nc, ctx):
        self.nc = nc
        self.sems = {}
        self.cnt = {}
        names = ["pe", "act", "dve", "pool",
                 "qx0", "qx1", "qrot", "qrot2", "qwk0", "qwk1", "qwk2", "qwk3",
                 "qwo0", "qwo1", "qwo2", "qwo3",
                 "qwu0", "qwu1", "qwu2", "qwu3", "qwd0", "qwd1", "qwd2", "qwd3", "qtab",
                 "qpl0", "qpl1", "qpl2", "qpl3", "qpl4", "qpl5",
                 "qis0", "qis1", "qis2", "qis3", "qis4", "qis5",
                 "qps0", "qps1", "qps2", "qps3", "qout0", "qout1",
                 "qil0", "qil1", "qil2", "qil3",
                 "qwg0", "qwg1", "qwg2", "qwg3",
                 "qwg4", "qwg5", "qwg6", "qwg7"]
        for name in names:
            self.sems[name] = ctx.enter_context(nc.semaphore("s_" + name))
            self.cnt[name] = 0
        self.ops = {k: [] for k in self.ENGS}

    def op(self, eng, fn, waits=(), inc=None, inc_k=1):
        waits = tuple(w for w in waits if w is not None)
        self.ops[eng].append((fn, waits, inc, inc_k))
        if inc is not None:
            self.cnt[inc] += inc_k
            return (inc, self.cnt[inc])
        return None

    def dma(self, eng, out, in_, waits=(), sem="qtab"):
        return self.op(eng, lambda e, o=out, i=in_: e.dma_start(out=o, in_=i),
                       waits=waits, inc=sem, inc_k=16)

    def emit(self, blk):
        engmap = {"pe": blk.tensor, "act": blk.scalar, "dve": blk.vector,
                  "pool": blk.gpsimd, "sync": blk.sync}
        for name in self.ENGS:
            ops = self.ops[name]
            if not ops:
                continue

            def body(e, ops=ops):
                hw = {}
                for fn, waits, inc, inc_k in ops:
                    need = {}
                    for (sname, val) in waits:
                        if hw.get(sname, 0) < val and need.get(sname, 0) < val:
                            need[sname] = val
                    for sname, val in need.items():
                        e.wait_ge(self.sems[sname], val)
                        hw[sname] = val
                    inst = fn(e)
                    if inc is not None:
                        inst.then_inc(self.sems[inc], inc_k)
            engmap[name](body)
        self.ops = {k: [] for k in self.ENGS}


def build_core_program(phases=4, p1_stage=4, p1_tiles=NT, p1_attn=6):
    nc = bass.Bass("TRN2", target_bir_lowering=False)
    tabs, consts_np = _host_tables()

    x_fm = nc.dram_tensor("x_fm", [SEQ_PER_CORE, D, S], F32, kind="ExternalInput").ap()
    w_qkv = nc.dram_tensor("w_qkv", [D, 3 * D], F32, kind="ExternalInput").ap()
    qk_shuf = nc.dram_tensor("qk_shuf", [16, 128, 1024], F32, kind="ExternalInput").ap()
    wo_shuf = nc.dram_tensor("wo_shuf", [8, 128, 1024], F32, kind="ExternalInput").ap()
    wgT_d = nc.dram_tensor("wgT", [D, HID], I32, kind="ExternalInput").ap()
    wu_shuf = nc.dram_tensor("wu_shuf", [HC, 128, 1024], I32, kind="ExternalInput").ap()
    wdT_d = nc.dram_tensor("wdT", [HID, D], I32, kind="ExternalInput").ap()
    out_fm = nc.dram_tensor("out_fm", [SEQ_PER_CORE, D, S], F32, kind="ExternalOutput").ap()

    wqk_sc = nc.dram_tensor("wqk_sc", [16, 128, 1024], F16, kind="Internal").ap()
    wv_sc = nc.dram_tensor("wv_sc", [8, 128, 1024], F16, kind="Internal").ap()
    wo_sc = nc.dram_tensor("wo_sc", [8, 128, 1024], F16, kind="Internal").ap()
    wg_sc = nc.dram_tensor("wg_sc", [8, 128, HID], BF16, kind="Internal").ap()
    wu_sc = nc.dram_tensor("wu_sc", [HC, 128, 1024], BF16, kind="Internal").ap()
    wd_sc = nc.dram_tensor("wd_sc", [HC, 128, 1024], BF16, kind="Internal").ap()

    cst = {k: nc.inline_tensor(v, "c_" + k).ap() for k, v in tabs.items()}
    cst["consts"] = nc.inline_tensor(consts_np, "c_consts").ap()

    with ExitStack() as top:
        sb = lambda name, shape, dt: top.enter_context(nc.sbuf_tensor(name, shape, dt))
        P = Prog(nc, top)

        res = sb("res", [128, DC * TOK], BF16)
        rows = sb("rows", [1, 2 * TOK], BF16)
        consts = sb("consts", [128, 8], F32)
        onec = sb("onec", [128, 1], BF16)
        onec16 = sb("onec16", [128, 1], F16)
        oner = sb("oner", [1, 128], BF16)

        with ExitStack() as ctx:
            psb = lambda name, shape, dt: ctx.enter_context(nc.sbuf_tensor(name, shape, dt))
            pinf = [psb(f"pinf_{i}", [128, 1024], F32) for i in range(4)]
            pot16 = [psb(f"pot16_{i}", [128, 1024], F16) for i in range(4)]

            P.dma("sync", consts[:], cst["consts"], sem="qtab")
            P.dma("sync", onec[:], cst["ones_col"], sem="qtab")
            P.dma("sync", onec16[:], cst["ones_col16"], sem="qtab")
            P.dma("sync", oner[:], cst["ones_row"], sem="qtab")

            freeL = {i: None for i in range(4)}
            freeS = {i: None for i in range(4)}
            fp16_jobs = []
            for oc in range(16):
                fp16_jobs.append((qk_shuf[oc], wqk_sc[oc]))
            for dc in range(DC):
                fp16_jobs.append((w_qkv[dc * 128:(dc + 1) * 128, 2048:3072], wv_sc[dc]))
            for ds in range(DC):
                fp16_jobs.append((wo_shuf[ds], wo_sc[ds]))
            for j, (srcap, dstap) in enumerate(fp16_jobs):
                bL = j % 4
                eng_c = "act" if j % 2 == 0 else "dve"
                ld = P.dma("sync", pinf[bL][:, 0:1024], srcap, waits=[freeL[bL]],
                           sem="qpl%d" % bL)
                cv = P.op(eng_c, lambda e, bL=bL, eng_c=eng_c: (
                    e.copy if eng_c == "act" else e.tensor_copy)(
                    out=pot16[bL][:], in_=pinf[bL][:]),
                    waits=[ld, freeS[bL]], inc=eng_c)
                freeL[bL] = cv
                freeS[bL] = P.dma("act", dstap, pot16[bL][:], waits=[cv],
                                  sem="qps%d" % bL)
            P.op("sync", lambda e: e.nop(),
                 waits=[("qps0", P.cnt["qps0"]), ("qps1", P.cnt["qps1"]),
                        ("qps2", P.cnt["qps2"]), ("qps3", P.cnt["qps3"]),
                        ("qtab", P.cnt["qtab"])])
            with nc.Block("prep") as blk:
                P.emit(blk)
        nc.all_engine_barrier()
        if phases < 2:
            return nc

        with ExitStack() as ctx:
            psb = lambda name, shape, dt: ctx.enter_context(nc.sbuf_tensor(name, shape, dt))
            pps = lambda name, shape, dt: ctx.enter_context(nc.psum_tensor(name, shape, dt))
            cos_q = psb("cos_q", [128, 512], F16); sin_q = psb("sin_q", [128, 512], F16)
            cos_k = psb("cos_k", [128, 512], F16); sin_k = psb("sin_k", [128, 512], F16)
            m128T = psb("m128T", [128, 128], F16)
            mask_n = psb("mask_n", [128, 512], F16)
            mask_0 = psb("mask_0", [128, 512], F16)
            ident = psb("ident", [128, 128], F16)
            wv_sb = psb("wv_sb", [128, 8 * 1024], F16)
            wqk_buf = psb("wqk_buf", [128, 4 * 1024], F16)
            x32 = [psb(f"x32_{i}", [128, 512], F32) for i in range(2)]
            x16 = [psb(f"x16_{i}", [128, DC * 512], F16) for i in range(2)]
            q_sb = psb("q_sb", [128, DC * 512], F16)
            k_sb = psb("k_sb", [128, DC * 512], F16)
            rotb = [psb(f"rotb{i}", [128, 1024], F16) for i in range(2)]
            q_lo = psb("q_lo", [128, DC * 512], F16)
            q_hi = psb("q_hi", [128, DC * 512], F16)
            k_lo = psb("k_lo", [128, DC * 640], F16)
            vhat = psb("vhat", [128, 5 * 1040], F16)
            pt_sb = [psb(f"pt_sb{i}", [128, 1024], F16) for i in range(2)]
            attn_fm = q_sb
            attn_tm = k_sb
            zsq = q_hi
            rowsc = psb("rowsc", [1, 1024], F32)
            recb = psb("recb", [128, 16], F32)
            pin32 = [psb(f"pin32_{i}", [128, 512], I32) for i in range(4)]
            potbf = [psb(f"potbf_{i}", [128, 512], BF16) for i in range(4)]
            ps_ab = pps("ps_ab", [128, 1024], F32)
            ps_sim = pps("ps_sim", [128, 1024], F32)
            ps_pv = pps("ps_pv", [128, 2048], F32)

            tl = []
            for t_sb, key in ((cos_q, "cos_q"), (sin_q, "sin_q"), (cos_k, "cos_k"),
                              (sin_k, "sin_k"), (m128T, "m128T"), (mask_n, "mask_n"),
                              (mask_0, "mask_0"), (ident, "ident")):
                tl.append(P.dma("sync", t_sb[:], cst[key]))
            for dc in range(DC):
                tl.append(P.dma("sync", wv_sb[:, dc * 1024:(dc + 1) * 1024], wv_sc[dc]))
            tables_tok = tl[-1]
            vz = P.op("pool", lambda e: e.memset(vhat[:], 0.0), inc="pool")
            ones_ready = None
            for sl in range(1, 5):
                ap = vhat.ap()[:, sl * 1040:(sl + 1) * 1040].rearrange(
                    "p (h c) -> p h c", h=16)[:, :, 64:65]
                ones_ready = P.op("pool", lambda e, ap=ap: e.memset(ap, 1.0),
                                  waits=[vz], inc="pool")

            pending_rows = [None]
            int_jobs = []
            for dc in range(DC):
                for p_ in range(6):
                    int_jobs.append((wgT_d[dc * 128:(dc + 1) * 128, p_ * 512:(p_ + 1) * 512],
                                     wg_sc[dc, :, p_ * 512:(p_ + 1) * 512]))
            for hc in range(HC):
                for p_ in range(2):
                    int_jobs.append((wu_shuf[hc, :, p_ * 512:(p_ + 1) * 512],
                                     wu_sc[hc, :, p_ * 512:(p_ + 1) * 512]))
            for hc in range(HC):
                for p_ in range(2):
                    int_jobs.append((wdT_d[hc * 128:(hc + 1) * 128, p_ * 512:(p_ + 1) * 512],
                                     wd_sc[hc, :, p_ * 512:(p_ + 1) * 512]))
            NJOB = len(int_jobs)
            ip = {"ld": 0, "cv": 0, "stq": 0, "xt_allow": 0,
                  "ld_tok": [None] * NJOB, "cv_tok": [None] * NJOB,
                  "st_tok": [None] * NJOB}

            xp = {"t": 1, "ld": 0, "cv": 0, "ld_tok": [None] * DC,
                  "cv_tok": [None] * DC}

            def x_step():
                if xp["t"] >= NT or xp["t"] > ip["xt_allow"]:
                    return
                t_ = xp["t"]
                xb = t_ % 2
                x16t = x16[xb]
                prev_ld, prev_cv = xp["ld"], xp["cv"]
                while xp["cv"] < prev_ld:
                    j = xp["cv"]
                    cv = P.op("act", lambda e, j=j, x16t=x16t: e.copy(
                        out=x16t[:, j * 512:(j + 1) * 512], in_=x32[j % 2][:]),
                        waits=[xp["ld_tok"][j],
                               st["x16_free"][xb] if j == 0 else None], inc="act")
                    st["x32_free"][j % 2] = cv
                    xp["cv_tok"][j] = cv
                    xp["cv"] = j + 1
                n = 0
                while n < 2 and xp["ld"] < DC and xp["ld"] < prev_cv + 2:
                    j = xp["ld"]
                    src = x_fm[t_ // TPS, j * 128:(j + 1) * 128,
                               (t_ % TPS) * TILE:(t_ % TPS) * TILE + TILE]
                    xp["ld_tok"][j] = P.dma("pool", x32[j % 2][:], src,
                                            waits=[st["x32_free"][j % 2]],
                                            sem="qx%d" % (j % 2))
                    xp["ld"] = j + 1
                    n += 1
                if xp["cv"] == DC:
                    st["x_ready"][t_] = xp["cv_tok"][DC - 1]
                    xp["t"] += 1
                    xp["ld"] = 0
                    xp["cv"] = 0

            def bg_prep():
                x_step()
                prev_ld, prev_cv, prev_st = ip["ld"], ip["cv"], ip["stq"]
                while ip["stq"] < prev_cv:
                    j = ip["stq"]
                    ip["st_tok"][j] = P.dma("act", int_jobs[j][1], potbf[j % 4][:],
                                            waits=[ip["cv_tok"][j]], sem="qis%d" % (j % 4))
                    ip["stq"] = j + 1
                while ip["cv"] < min(prev_ld, prev_st + 4):
                    j = ip["cv"]
                    ip["cv_tok"][j] = P.op("act", lambda e, b_=j % 4: e.copy(
                        out=potbf[b_][:], in_=pin32[b_][:]),
                        waits=[ip["ld_tok"][j],
                               ip["st_tok"][j - 4] if j >= 4 else None], inc="act")
                    ip["cv"] = j + 1
                n_new = 0
                while n_new < 2 and ip["ld"] < min(NJOB, prev_cv + 4):
                    j = ip["ld"]
                    ip["ld_tok"][j] = P.dma("act", pin32[j % 4][:], int_jobs[j][0],
                                            waits=[ip["cv_tok"][j - 4] if j >= 4 else None],
                                            sem="qil%d" % (j % 4))
                    ip["ld"] = j + 1
                    n_new += 1

            st = {
                "x_ready": [None] * NT,
                "x32_free": [None, None],
                "x16_free": [None, None],
                "rope_q_done": None, "tcs_k_prev": None,
                "rot_free": [None, None], "qhi_evac": None,
                "qk_used": None, "ring_copied": None,
                "psA_free": None, "psB_free": None,
                "psS0_free": None, "psS1_free": None, "psPV_free": None,
                "wqk_free": [None] * 4,
                "attn_tm_free": None, "attn_fm_used": None,
                "zsq_free": [None, None], "pt_free": [None, None], "pt_m_used": None,
                "pend_pv": None,
                "recb_free": None,
                "stats_free": None, "rows_used": None,
            }
            zrows_toks = []

            for t in range(min(NT, p1_tiles)):
                seq = t // TPS
                col0 = (t % TPS) * TILE
                first_tile_of_seq = (col0 == 0)
                xb = t % 2
                x16t = x16[xb]

                if t == 0:
                    x_cast_toks = []
                    for dc in range(DC):
                        b = dc % 2
                        src = x_fm[seq, dc * 128:(dc + 1) * 128, col0:col0 + TILE]
                        ld = P.dma("pool", x32[b][:], src, waits=[st["x32_free"][b]],
                                   sem="qx%d" % b)
                        cv = P.op("act", lambda e, b=b, dc=dc, x16t=x16t: e.copy(
                            out=x16t[:, dc * 512:(dc + 1) * 512], in_=x32[b][:]),
                            waits=[ld, st["x16_free"][xb]], inc="act")
                        st["x32_free"][b] = cv
                        x_cast_toks.append(cv)
                    st["x_ready"][0] = x_cast_toks[-1]
                ip["xt_allow"] = t + 1
                x_ready = st["x_ready"][t]
                if pending_rows[0] is not None:
                    pending_rows[0]()
                    pending_rows[0] = None
                bg_prep()

                qk_ev = {}
                slots = [ps_ab[:, 0:512], ps_ab[:, 512:1024],
                         ps_sim[:, 0:512], ps_sim[:, 512:1024]]
                skey = ["psA_free", "psB_free", "psS0_free", "psS1_free"]
                def emit_v(vb, x16t=x16t, x_ready=x_ready):
                    w_ = vb // 2
                    half = vb % 2
                    sl = w_ + 1
                    mmtok = None
                    for dci in range(DC):
                        w = ([x_ready, tables_tok, ones_ready, st["psS1_free"]]
                             if dci == 0 else [])
                        mmtok = P.op("pe", lambda e, w_=w_, half=half, dci=dci,
                                     x16t=x16t: e.matmul(
                            out=ps_sim[:, 512:1024],
                            lhsT=x16t[:, dci * 512 + w_ * 128: dci * 512 + w_ * 128 + 128],
                            rhs=wv_sb[:, dci * 1024 + half * 512: dci * 1024 + (half + 1) * 512],
                            start=(dci == 0), stop=(dci == DC - 1)),
                            waits=w, inc="pe" if dci == DC - 1 else None)
                    hbase = half * 8
                    dst = vhat.ap()[:, sl * 1040 + hbase * 65: sl * 1040 + (hbase + 8) * 65]
                    dst = dst.rearrange("p (h c) -> p h c", h=8)[:, :, 0:64]
                    srcv = ps_sim.ap()[:, 512:1024].rearrange("p (h c) -> p h c", h=8)
                    ev = P.op("act", lambda e, dst=dst, srcv=srcv: e.copy(
                        out=dst, in_=srcv), waits=[mmtok, st["ring_copied"]],
                        inc="act")
                    st["psS1_free"] = ev
                    st["v_ev_last"] = ev

                for idx, oc in enumerate([8, 9, 10, 11, 12, 13, 14, 15,
                                          0, 1, 2, 3, 4, 5, 6, 7]):
                    if idx in (4, 8, 12):
                        bg_prep()
                    b = idx % 4
                    s4 = idx % 3
                    ld = P.dma("sync", wqk_buf[:, b * 1024:(b + 1) * 1024], wqk_sc[oc],
                               waits=[tables_tok, st["wqk_free"][b]], sem="qwk%d" % b)
                    ps = slots[s4]
                    ps_free = st[skey[s4]]
                    mmtok = None
                    for dci in range(DC):
                        w = [ld, x_ready, ps_free] if dci == 0 else []
                        mmtok = P.op("pe", lambda e, b=b, dci=dci, ps=ps, x16t=x16t: e.matmul(
                            out=ps,
                            lhsT=wqk_buf[:, b * 1024 + dci * 128: b * 1024 + (dci + 1) * 128],
                            rhs=x16t[:, dci * 512:(dci + 1) * 512],
                            start=(dci == 0), stop=(dci == DC - 1)),
                            waits=w, inc="pe" if dci == DC - 1 else None)
                    st["wqk_free"][b] = mmtok
                    dest = q_sb if oc < 8 else k_sb
                    ci = oc % 8
                    wv_ = [mmtok]
                    if oc < 8:
                        wv_.append(st["rope_q_done"])
                    else:
                        if ci < 2:
                            wv_.append(st["attn_tm_free"])
                        if st["tcs_k_prev"] is not None:
                            wv_.append(st["tcs_k_prev"][ci // 4])
                    ev = P.op("act", lambda e, dest=dest, ci=ci, ps=ps: e.copy(
                        out=dest[:, ci * 512:(ci + 1) * 512], in_=ps),
                        waits=wv_, inc="act")
                    qk_ev[oc] = ev
                    st[skey[s4]] = ev
                    if idx % 2 == 1:
                        emit_v(idx // 2)

                if p1_stage < 2:
                    continue
                v_ready = st["v_ev_last"]

                x3 = lambda t_: t_.ap().rearrange("p (c n) -> p c n", c=DC)
                bc2 = lambda tab: tab.ap().unsqueeze(1).broadcast_to([128, 2, 512])
                bc4_ = lambda tab: tab.ap().unsqueeze(1).broadcast_to([128, 4, 512])

                def rope_pass2(src_sb, scratch, dst_qtr, cos_t, sin_t, ev_q,
                               w_scr0, w_dst):
                    tbs = [None] * 4
                    tcs = [None, None]
                    for u in range(4):
                        h = u // 2
                        bq = u % 2
                        c0 = u * 1024
                        sl = src_sb.ap()[:, c0:c0 + 1024].rearrange(
                            "p (c n) -> p c n", c=2)
                        scl = scratch.ap()[:, c0:c0 + 1024].rearrange(
                            "p (c n) -> p c n", c=2)
                        wsc = list(w_scr0[0]) if u == 0 else (
                            list(w_scr0[1]) if u == 2 else [])
                        ta = P.op("dve", lambda e, sl=sl, scl=scl, cos_t=cos_t:
                                  e.tensor_tensor(out=scl, in0=sl, in1=bc2(cos_t),
                                                  op=ALU.mult),
                                  waits=[ev_q[u]] + wsc, inc="dve")
                        rdw = [ev_q[u], st["rot_free"][bq]]
                        toks = []
                        for g in (0, 64):
                            toks.append(P.dma("pool", rotb[bq][g:g + 32, :],
                                              src_sb[g + 32:g + 64, c0:c0 + 1024],
                                              waits=rdw,
                                              sem="qrot" if bq == 0 else "qrot2"))
                            rdw = []
                            toks.append(P.dma("pool", rotb[bq][g + 32:g + 64, :],
                                              src_sb[g:g + 32, c0:c0 + 1024],
                                              sem="qrot" if bq == 0 else "qrot2"))
                        tbs[u] = P.op("dve", lambda e, u=u, bq=bq, c0=c0,
                                      sin_t=sin_t: e.tensor_tensor(
                            out=src_sb.ap()[:, c0:c0 + 1024].rearrange(
                                "p (c n) -> p c n", c=2),
                            in0=rotb[bq].ap().rearrange("p (c n) -> p c n", c=2),
                            in1=bc2(sin_t), op=ALU.mult),
                            waits=[toks[-1], ta], inc="dve")
                        st["rot_free"][bq] = tbs[u]
                        tc = P.op("dve", lambda e, sl=sl, scl=scl, u=u,
                                  dst_qtr=dst_qtr: e.tensor_tensor(
                            out=dst_qtr[u], in0=scl, in1=sl, op=ALU.add),
                            waits=[tbs[u]] + (w_dst if u == 0 else []), inc="dve")
                        tcs[h] = tc
                    return tcs

                k3 = k_lo.ap().rearrange("p (c n) -> p c n", c=DC)
                tcs_k = rope_pass2(k_sb, q_hi,
                                   [k3[:, 0:2, 128:640], k3[:, 2:4, 128:640],
                                    k3[:, 4:6, 128:640], k3[:, 6:8, 128:640]],
                                   cos_k, sin_k,
                                   [qk_ev[9], qk_ev[11], qk_ev[13], qk_ev[15]],
                                   [[st["qk_used"], st["qhi_evac"],
                                     st["zsq_free"][0], st["zsq_free"][1]],
                                    [st["qhi_evac"]]],
                                   [st["qk_used"], st["ring_copied"]])
                st["tcs_k_prev"] = tcs_k
                tcs_q = rope_pass2(q_sb, q_hi,
                                   [x3(q_lo)[:, 0:2, :], x3(q_lo)[:, 2:4, :],
                                    x3(q_lo)[:, 4:6, :], x3(q_lo)[:, 6:8, :]],
                                   cos_q, sin_q,
                                   [qk_ev[1], qk_ev[3], qk_ev[5], qk_ev[7]],
                                   [[tcs_k[0]], [tcs_k[1]]],
                                   [st["qk_used"]])
                qhi_ev_h = [None, None]
                for dci in range(DC):
                    ps = slots[dci % 3]
                    pf = st[skey[dci % 3]]
                    mm = P.op("pe", lambda e, dci=dci, ps=ps: e.matmul(
                        out=ps, lhsT=m128T[:], rhs=q_lo[:, dci * 512:(dci + 1) * 512],
                        start=True, stop=True), waits=[tcs_q[dci // 4], pf], inc="pe")
                    ev = P.op("act", lambda e, dci=dci, ps=ps: e.copy(
                        out=q_hi[:, dci * 512:(dci + 1) * 512], in_=ps),
                        waits=[mm, tcs_q[dci // 4]], inc="act")
                    st[skey[dci % 3]] = ev
                    qhi_ev_h[dci // 4] = ev
                st["qhi_evac"] = qhi_ev_h[1]
                bg_prep()

                if p1_stage < 3:
                    continue
                if first_tile_of_seq:
                    kprev = k_lo.ap().rearrange("p (c n) -> p c n", c=DC)[:, :, 0:128]
                    z1_ = P.op("pool", lambda e, kprev=kprev: e.memset(kprev, 0.0),
                               waits=[st["qk_used"], st["ring_copied"]], inc="pool")
                    seq_zero = P.op("pool", lambda e: e.memset(vhat[:, 0:1040], 0.0),
                                    waits=[z1_, st["pt_m_used"]], inc="pool")
                else:
                    seq_zero = st["ring_copied"]

                sim_last_all = None
                pend_tail = [None]
                for w_ in range(WPT):
                    wseq0 = first_tile_of_seq and w_ == 0
                    mtile = mask_0 if wseq0 else mask_n
                    pv_last = None
                    for g in range(4):
                        bankT = ps_sim if g % 2 == 0 else ps_ab
                        pt = pt_sb[g % 2]
                        if g % 2 == 0:
                            bfree = [st["psS0_free"], st["psS1_free"]]
                        else:
                            bfree = [st["psA_free"], st["psB_free"]]
                        first_w = [tcs_k[g // 2], qhi_ev_h[g // 2], v_ready,
                                   seq_zero, st["attn_tm_free"]] + bfree
                        for pp in range(2):
                            hp = 2 * g + pp
                            for sub in range(2):
                                pb = 64 * sub
                                base_i = hp * 512 + w_ * 128
                                klo_c = hp * 640
                                outc = sub * 512 + pp * 256
                                st_flag = (pp == 0)
                                w0 = first_w if (pp == 0 and sub == 0) else []
                                P.op("pe", lambda e, pb=pb, klo_c=klo_c, w_=w_,
                                     base_i=base_i, outc=outc, st_flag=st_flag,
                                     bankT=bankT: e.matmul(
                                    out=bankT[:, outc:outc + 128],
                                    lhsT=k_lo[pb:pb + 64, klo_c + 128 * w_: klo_c + 128 * w_ + 128],
                                    rhs=q_hi[pb:pb + 64, base_i:base_i + 128],
                                    start=st_flag, stop=False), waits=w0)
                                P.op("pe", lambda e, pb=pb, klo_c=klo_c, w_=w_,
                                     base_i=base_i, outc=outc, bankT=bankT: e.matmul(
                                    out=bankT[:, outc + 128:outc + 256],
                                    lhsT=k_lo[pb:pb + 64, klo_c + 128 + 128 * w_: klo_c + 256 + 128 * w_],
                                    rhs=q_lo[pb:pb + 64, base_i:base_i + 128],
                                    start=False, stop=False))
                        mk0 = P.op("pe", lambda e, bankT=bankT, mtile=mtile: e.matmul(
                            out=bankT[:, 0:512], lhsT=ident[:], rhs=mtile[:],
                            start=False, stop=True), inc="pe")
                        mk1 = P.op("pe", lambda e, bankT=bankT, mtile=mtile: e.matmul(
                            out=bankT[:, 512:1024], lhsT=ident[:], rhs=mtile[:],
                            start=False, stop=True), inc="pe")
                        sim_last_all = mk1
                        ex = P.op("act", lambda e, bankT=bankT, pt=pt: e.activation(
                            out=pt[:], in_=bankT[:], func=AF.Exp,
                            bias=consts[:, 0:1]),
                            waits=[mk0, mk1, st["pt_free"][g % 2]], inc="act")
                        if g % 2 == 0:
                            st["psS0_free"] = ex
                            st["psS1_free"] = ex
                        else:
                            st["psA_free"] = ex
                            st["psB_free"] = ex

                        def plan_pv(g=g, w_=w_, ex=ex, pt=pt):
                            pvl = None
                            for hh in range(4):
                                h = 4 * g + hh
                                sub = hh % 2
                                pp = hh // 2
                                ptc = sub * 512 + pp * 256
                                off = h * 128
                                w0 = [ex, v_ready, st["psPV_free"]] if hh == 0 else []
                                P.op("pe", lambda e, h=h, w_=w_, off=off, ptc=ptc,
                                     pt=pt: e.matmul(
                                    out=ps_pv[:, off:off + 65],
                                    lhsT=pt[:, ptc: ptc + 128],
                                    rhs=vhat[:, w_ * 1040 + h * 65: w_ * 1040 + (h + 1) * 65],
                                    start=True, stop=False), waits=w0)
                                pvl = P.op("pe", lambda e, h=h, w_=w_, off=off, ptc=ptc,
                                           pt=pt: e.matmul(
                                    out=ps_pv[:, off:off + 65],
                                    lhsT=pt[:, ptc + 128: ptc + 256],
                                    rhs=vhat[:, (w_ + 1) * 1040 + h * 65: (w_ + 1) * 1040 + (h + 1) * 65],
                                    start=False, stop=True),
                                    inc="pe" if hh == 3 else None)
                            st["pt_free"][g % 2] = pvl
                            return pvl
                        if g == 1 and pend_tail[0] is not None:
                            pend_tail[0]()
                            pend_tail[0] = None
                        if st["pend_pv"] is not None:
                            pv_last = st["pend_pv"]()
                        st["pend_pv"] = plan_pv
                    pv_last = st["pend_pv"]()
                    st["pend_pv"] = None
                    st["pt_m_used"] = pv_last

                    def plan_tail(w_=w_, pv_last=pv_last):
                        pv3 = ps_pv.ap().rearrange("p (h c) -> p h c", h=16)
                        rc = P.op("dve", lambda e, pv3=pv3: e.reciprocal(
                            out=recb.ap().unsqueeze(2), in_=pv3[:, :, 64:65]),
                            waits=[pv_last, st["recb_free"]], inc="dve")
                        at3 = attn_tm.ap()[:, 0:1024].rearrange("p (h c) -> p h c", h=16)
                        ev = P.op("dve", lambda e, pv3=pv3, at3=at3: e.tensor_tensor(
                            out=at3, in0=pv3[:, :, 0:64],
                            in1=recb.ap().unsqueeze(2).broadcast_to([128, 16, 64]),
                            op=ALU.mult), waits=[rc, st["attn_tm_free"]], inc="dve")
                        st["recb_free"] = ev
                        s0v = ps_pv.ap().bitcast(F16)[:, 0:1024]
                        trs = []
                        for dci in range(DC):
                            trs.append(P.op("pe", lambda e, dci=dci, s0v=s0v: e.transpose(
                                out=s0v[:, dci * 128:(dci + 1) * 128],
                                in_=attn_tm[:, dci * 128:(dci + 1) * 128], identity=ident[:]),
                                waits=[ev] if dci == 0 else [],
                                inc="pe" if dci == DC - 1 else None))
                        cp1 = P.op("dve", lambda e, s0v=s0v, w_=w_: e.tensor_copy(
                            out=attn_fm.ap().rearrange("p (c n) -> p c n", c=DC)[:, :, w_ * 128:(w_ + 1) * 128],
                            in_=s0v.rearrange("p (c n) -> p c n", c=DC)),
                            waits=[trs[-1], st["attn_fm_used"]], inc="dve")
                        st["psPV_free"] = cp1
                        st["attn_tm_free"] = trs[-1]
                        return cp1
                    pend_tail[0] = plan_tail
                    bg_prep()
                cp1 = pend_tail[0]()
                pend_tail[0] = None
                bg_prep()
                st["qk_used"] = sim_last_all

                ksrc = k_lo.ap().rearrange("p (c n) -> p c n", c=DC)[:, :, 512:640]
                kdst = k_lo.ap().rearrange("p (c n) -> p c n", c=DC)[:, :, 0:128]
                rc1 = P.op("pool", lambda e, ksrc=ksrc, kdst=kdst: e.tensor_copy(
                    out=kdst, in_=ksrc), waits=[st["qk_used"]], inc="pool")
                rc2 = P.op("pool", lambda e: e.tensor_copy(
                    out=vhat[:, 0:1040], in_=vhat[:, 4 * 1040:5 * 1040]),
                    waits=[rc1, st["pt_m_used"]], inc="pool")
                st["ring_copied"] = rc2

                if p1_stage < 4:
                    continue
                z1_toks = []
                last_opj_mm = [None]
                for ds in range(DC):
                    ps = ps_ab[:, 0:512] if ds % 2 == 0 else ps_ab[:, 512:1024]
                    pf = st["psA_free"] if ds % 2 == 0 else st["psB_free"]
                    wb = ds % 4
                    ldo = P.dma("sync", wqk_buf[:, wb * 1024:(wb + 1) * 1024], wo_sc[ds],
                                waits=[st["wqk_free"][wb]], sem="qwo%d" % wb)
                    mmtok = None
                    for hoc in range(DC):
                        w0 = [cp1, pf, ldo] if hoc == 0 else []
                        mmtok = P.op("pe", lambda e, hoc=hoc, ps=ps, wb=wb: e.matmul(
                            out=ps,
                            lhsT=wqk_buf[:, wb * 1024 + hoc * 128: wb * 1024 + (hoc + 1) * 128],
                            rhs=attn_fm[:, hoc * 512:(hoc + 1) * 512],
                            start=(hoc == 0), stop=(hoc == DC - 1)),
                            waits=w0, inc="pe" if hoc == DC - 1 else None)
                    st["wqk_free"][wb] = mmtok
                    last_opj_mm[0] = mmtok
                    rcol = ds * TOK + t * 512
                    zt = P.op("dve", lambda e, ds=ds, ps=ps, rcol=rcol, x16t=x16t: e.tensor_tensor(
                        out=res[:, rcol:rcol + 512], in0=ps,
                        in1=x16t[:, ds * 512:(ds + 1) * 512], op=ALU.add),
                        waits=[mmtok], inc="dve")
                    z1_toks.append(zt)
                    if ds % 2 == 0:
                        st["psA_free"] = zt
                    else:
                        st["psB_free"] = zt
                st["x16_free"][xb] = z1_toks[-1]
                st["attn_fm_used"] = last_opj_mm[0]
                st["rope_q_done"] = last_opj_mm[0]

                sum_last = None
                sq_last = None
                for ds in range(DC):
                    rcol = ds * TOK + t * 512
                    zb = ds % 2
                    sq = P.op("dve", lambda e, rcol=rcol, zb=zb: e.tensor_tensor(
                        out=zsq[:, zb * 512:(zb + 1) * 512], in0=res[:, rcol:rcol + 512],
                        in1=res[:, rcol:rcol + 512], op=ALU.mult),
                        waits=[z1_toks[ds], st["zsq_free"][zb], st["qk_used"],
                               st["rows_used"] if ds == 0 else None], inc="dve")
                    w0 = [z1_toks[ds], st["psA_free"] if ds == 0 else None]
                    sum_last = P.op("pe", lambda e, rcol=rcol, ds=ds: e.matmul(
                        out=ps_ab[0:1, 0:512], lhsT=onec[:], rhs=res[:, rcol:rcol + 512],
                        start=(ds == 0), stop=(ds == DC - 1)),
                        waits=w0, inc="pe")
                    w1 = [sq, st["psB_free"] if ds == 0 else None]
                    sq_last = P.op("pe", lambda e, zb=zb, ds=ds: e.matmul(
                        out=ps_ab[0:1, 512:1024], lhsT=onec16[:],
                        rhs=zsq[:, zb * 512:(zb + 1) * 512],
                        start=(ds == 0), stop=(ds == DC - 1)),
                        waits=w1, inc="pe")
                    st["zsq_free"][zb] = sq_last
                def mk_rows(t=t, sum_last=sum_last, sq_last=sq_last):
                    d1 = P.op("dve", lambda e, t=t: e.tensor_scalar(
                              out=rows[0:1, t * 512:(t + 1) * 512], in0=ps_ab[0:1, 0:512],
                              scalar1=consts[0:1, 3:4], scalar2=None, op0=ALU.mult),
                              waits=[sum_last, st["rows_used"]], inc="dve")
                    d2 = P.op("dve", lambda e, t=t: e.tensor_tensor(
                              out=rowsc[0:1, 0:512],
                              in0=rows[0:1, t * 512:(t + 1) * 512],
                              in1=rows[0:1, t * 512:(t + 1) * 512], op=ALU.mult),
                              waits=[d1], inc="dve")
                    d3 = P.op("dve", lambda e: e.scalar_tensor_tensor(
                              out=rowsc[0:1, 512:1024], in0=ps_ab[0:1, 512:1024],
                              scalar=consts[0:1, 3:4], in1=rowsc[0:1, 0:512],
                              op0=ALU.mult, op1=ALU.subtract),
                              waits=[sq_last, d2], inc="dve")
                    a4 = P.op("act", lambda e: e.activation(out=rowsc[0:1, 0:512],
                              in_=rowsc[0:1, 512:1024], func=AF.Ln, bias=consts[0:1, 4:5]),
                              waits=[d3, st["rows_used"]], inc="act")
                    a5 = P.op("act", lambda e, t=t: e.activation(
                              out=rows[0:1, TOK + t * 512: TOK + (t + 1) * 512],
                              in_=rowsc[0:1, 0:512], func=AF.Exp,
                              scale=consts[0:1, 5:6]), waits=[a4], inc="act")
                    zrows_toks.append(a5)
                    st["psA_free"] = d1
                    st["psB_free"] = d3
                    st["rows_used"] = a5
                st["psA_free"] = sum_last
                st["psB_free"] = sq_last
                bg_prep()
                if t == NT - 1:
                    mk_rows()
                else:
                    pending_rows[0] = mk_rows

            while ip["stq"] < NJOB:
                bg_prep()
            P.op("sync", lambda e: e.nop(),
                 waits=[("qis0", P.cnt["qis0"]), ("qis1", P.cnt["qis1"]),
                        ("qis2", P.cnt["qis2"]), ("qis3", P.cnt["qis3"])])
            p1_z1 = list(zrows_toks)
            with nc.Block("p1") as blk:
                P.emit(blk)
        nc.all_engine_barrier()
        if phases < 3:
            return nc

        with ExitStack() as ctx:
            psb = lambda name, shape, dt: ctx.enter_context(nc.sbuf_tensor(name, shape, dt))
            pps = lambda name, shape, dt: ctx.enter_context(nc.psum_tensor(name, shape, dt))
            wg_sb = psb("wg_sb", [128, DC * HID], BF16)
            wu_buf = psb("wu_buf", [128, 4 * 1024], BF16)
            wd_buf = psb("wd_buf", [128, 4 * 1024], BF16)
            y_sb = [psb(f"y_sb{i}", [128, DC * 512], BF16) for i in range(2)]
            y2 = psb("y2", [128, 512], BF16)
            a_sb = psb("a_sb", [128, 2 * 512], BF16)
            h_sb = psb("h_sb", [128, HC * 512], BF16)
            mbc = psb("mbc", [128, 512], BF16)
            rbc = psb("rbc", [128, 512], BF16)
            ssq = psb("ssq", [128, 4096], BF16)
            rowsc2 = psb("rowsc2", [1, 1024], F32)
            mbc2 = psb("mbc2", [128, 512], BF16)
            rbc2 = psb("rbc2", [128, 512], BF16)
            o32 = [psb(f"o32_{i}", [128, 512], F32) for i in range(2)]
            pg = [pps("pg0", [128, 512], F32), pps("pg1", [128, 512], F32)]
            pu = [pps("pu0", [128, 512], F32), pps("pu1", [128, 512], F32)]
            pd = [pps(f"pd{i}", [128, 512], F32) for i in range(4)]

            ldw = []
            for dc in range(DC):
                ldw.append(P.dma("sync", wg_sb[:, dc * HID:(dc + 1) * HID], wg_sc[dc],
                                 sem="qwg%d" % dc))

            st2 = {
                "pg_free": [None, None], "pu_free": [None, None],
                "pd_free": [None] * 4, "a_free": [None, None],
                "h_free": None, "y_free": [None, None], "bc_free": None,
                "y_ready": [None] * (NT + 1),
                "wd_free": [None] * 4, "wu_free": [None] * 4, "spre": [None] * NT,
                "yrows_done": [None] * (NT + 1),
            }
            st3 = {"ssq_free": None, "rows2_used": [None, None],
                   "bc_free": None, "rowsc_free": None, "stage": [None] * NT,
                   "sq": [None] * NT, "o32_free": [None, None]}

            def plan_y(t):
                yb = t % 2
                bm = P.op("pe", lambda e, t=t: e.matmul(
                    out=pg[0][:], lhsT=oner[:], rhs=rows[0:1, t * 512:(t + 1) * 512],
                    start=True, stop=True),
                    waits=[st2["pg_free"][0], st2["bc_free"]], inc="pe")
                em = P.op("act", lambda e: e.copy(out=mbc[:], in_=pd[0][:] if False else pg[0][:]),
                          waits=[bm], inc="act")
                br = P.op("pe", lambda e, t=t: e.matmul(
                    out=pg[1][:], lhsT=oner[:], rhs=rows[0:1, TOK + t * 512: TOK + (t + 1) * 512],
                    start=True, stop=True), waits=[st2["pg_free"][1], em], inc="pe")
                er = P.op("act", lambda e: e.copy(out=rbc[:], in_=pg[1][:]),
                          waits=[br], inc="act")
                st2["pg_free"][0] = em
                st2["pg_free"][1] = er
                st2["yrows_done"][t] = er
                prev_ty2 = None
                for dc in range(DC):
                    rcol = dc * TOK + t * 512
                    w0 = [em, st2["y_free"][yb], prev_ty2]
                    ty1 = P.op("dve", lambda e, rcol=rcol: e.tensor_tensor(
                        out=y2[:], in0=res[:, rcol:rcol + 512], in1=mbc[:], op=ALU.subtract),
                        waits=w0, inc="dve")
                    ty2 = P.op("dve", lambda e, dc=dc, yb=yb: e.tensor_tensor(
                        out=y_sb[yb][:, dc * 512:(dc + 1) * 512], in0=y2[:], in1=rbc[:],
                        op=ALU.mult), waits=[ty1, er], inc="dve")
                    prev_ty2 = ty2
                st2["bc_free"] = prev_ty2
                st2["y_ready"][t] = prev_ty2

            def p2b_sq(t, spre_toks):
                sq_tts = []
                for ds in range(DC):
                    rcol = ds * TOK + t * 512
                    sq = P.op("dve", lambda e, rcol=rcol, ds=ds: e.tensor_tensor(
                        out=ssq[:, ds * 512:(ds + 1) * 512],
                        in0=res[:, rcol:rcol + 512],
                        in1=res[:, rcol:rcol + 512], op=ALU.mult),
                        waits=[spre_toks[ds], st3["ssq_free"] if ds == 0 else None],
                        inc="dve")
                    sq_tts.append(sq)
                st3["sq"][t] = (sq_tts, spre_toks)

            def p2b_stage(t):
                tp_ = t % 2
                sq_tts, spre_toks = st3["sq"][t]
                sum_last = None
                sq_last = None
                for ds in range(DC):
                    rcol = ds * TOK + t * 512
                    sum_last = P.op("pe", lambda e, rcol=rcol, ds=ds: e.matmul(
                        out=pd[0][0:1, :], lhsT=onec[:], rhs=res[:, rcol:rcol + 512],
                        start=(ds == 0), stop=(ds == DC - 1)),
                        waits=[spre_toks[ds]], inc="pe" if ds == DC - 1 else None)
                    sq_last = P.op("pe", lambda e, ds=ds: e.matmul(
                        out=pd[1][0:1, :], lhsT=onec[:],
                        rhs=ssq[:, ds * 512:(ds + 1) * 512],
                        start=(ds == 0), stop=(ds == DC - 1)),
                        waits=[sq_tts[ds]], inc="pe")
                st3["ssq_free"] = sq_last
                d1 = P.op("dve", lambda e, t=t: e.tensor_scalar(
                          out=rows[0:1, t * 512:(t + 1) * 512], in0=pd[0][0:1, :],
                          scalar1=consts[0:1, 3:4], scalar2=None, op0=ALU.mult),
                          waits=[sum_last, st3["rows2_used"][tp_],
                                 st2["yrows_done"][t]], inc="dve")
                d2 = P.op("dve", lambda e, t=t: e.tensor_tensor(
                          out=rowsc2[0:1, 0:512],
                          in0=rows[0:1, t * 512:(t + 1) * 512],
                          in1=rows[0:1, t * 512:(t + 1) * 512], op=ALU.mult),
                          waits=[d1, st3["rowsc_free"]], inc="dve")
                d3 = P.op("dve", lambda e: e.scalar_tensor_tensor(
                          out=rowsc2[0:1, 512:1024], in0=pd[1][0:1, :],
                          scalar=consts[0:1, 3:4], in1=rowsc2[0:1, 0:512],
                          op0=ALU.mult, op1=ALU.subtract),
                          waits=[sq_last, d2], inc="dve")
                a4 = P.op("act", lambda e: e.activation(
                          out=rowsc2[0:1, 0:512],
                          in_=rowsc2[0:1, 512:1024], func=AF.Ln,
                          bias=consts[0:1, 4:5]),
                          waits=[d3, st3["rows2_used"][tp_]], inc="act")
                a5 = P.op("act", lambda e, t=t: e.activation(
                          out=rows[0:1, TOK + t * 512: TOK + (t + 1) * 512],
                          in_=rowsc2[0:1, 0:512], func=AF.Exp,
                          scale=consts[0:1, 5:6]),
                          waits=[a4, st2["yrows_done"][t]], inc="act")
                st3["rowsc_free"] = a4
                st3["rows2_used"][tp_] = a5
                st2["pd_free"][0] = d1
                st2["pd_free"][1] = d3
                st3["stage"][t] = (d1, a5)

            def p2b_fin(t):
                d1, a5 = st3["stage"][t]
                pb_ = t % 2
                seq = t // TPS
                col0 = (t % TPS) * TILE
                bm = P.op("pe", lambda e, t=t: e.matmul(
                    out=pd[2][:], lhsT=oner[:],
                    rhs=rows[0:1, t * 512:(t + 1) * 512],
                    start=True, stop=True),
                    waits=[d1, st2["pd_free"][2]], inc="pe")
                em = P.op("act", lambda e: e.copy(
                    out=mbc2[:], in_=pd[2][:]),
                    waits=[bm, st3["bc_free"]], inc="act")
                br = P.op("pe", lambda e, t=t: e.matmul(
                    out=pd[3][:], lhsT=oner[:],
                    rhs=rows[0:1, TOK + t * 512: TOK + (t + 1) * 512],
                    start=True, stop=True),
                    waits=[a5, st2["pd_free"][3], em], inc="pe")
                er = P.op("act", lambda e: e.copy(
                    out=rbc2[:], in_=pd[3][:]),
                    waits=[br, st3["bc_free"]], inc="act")
                od_toks = [None] * DC
                tt2 = None
                for ds in range(DC):
                    rcol = ds * TOK + t * 512
                    sl = ds % 2
                    w1 = [em]
                    if ds >= 2:
                        w1.append(od_toks[ds - 2])
                    else:
                        w1.append(st3["o32_free"][sl])
                    tt1 = P.op("pool", lambda e, rcol=rcol, sl=sl: e.tensor_tensor(
                        out=o32[sl][:], in0=res[:, rcol:rcol + 512],
                        in1=mbc2[:], op=ALU.subtract),
                        waits=w1, inc="pool")
                    tt2 = P.op("dve", lambda e, sl=sl: e.tensor_tensor(
                        out=o32[sl][:], in0=o32[sl][:],
                        in1=rbc2[:], op=ALU.mult),
                        waits=[tt1, er], inc="dve")
                    od_toks[ds] = P.dma("pool",
                                        out_fm[seq, ds * 128:(ds + 1) * 128,
                                               col0:col0 + TILE],
                                        o32[sl][:], waits=[tt2],
                                        sem="qout%d" % (ds % 2))
                st2["pd_free"][2] = em
                st2["pd_free"][3] = er
                st3["bc_free"] = tt2
                st3["o32_free"][0] = od_toks[6]
                st3["o32_free"][1] = od_toks[7]

            plan_y(0)
            for t in range(NT):
                yb = t % 2
                yt = y_sb[yb]
                y_ready = st2["y_ready"][t]

                h_toks = []
                for hc in range(HC):
                    if hc == 3 and t >= 1:
                        p2b_stage(t - 1)
                    gb = hc % 2
                    ub = hc % 4
                    ldu = P.dma("sync", wu_buf[:, ub * 1024:(ub + 1) * 1024], wu_sc[hc],
                                waits=[st2["wu_free"][ub]], sem="qwu%d" % ub)
                    g_last = None
                    u_last = None
                    for dci in range(DC):
                        w0 = [y_ready, st2["pg_free"][gb]] if dci == 0 else []
                        if t == 0 and hc == 0:
                            w0 = w0 + [ldw[dci]]
                        g_last = P.op("pe", lambda e, hc=hc, dci=dci, gb=gb, yt=yt: e.matmul(
                            out=pg[gb][:],
                            lhsT=wg_sb[:, dci * HID + hc * 128: dci * HID + (hc + 1) * 128],
                            rhs=yt[:, dci * 512:(dci + 1) * 512],
                            start=(dci == 0), stop=(dci == DC - 1)),
                            waits=w0, inc="pe" if dci == DC - 1 else None)
                    for dci in range(DC):
                        w0 = [ldu, y_ready, st2["pu_free"][gb]] if dci == 0 else []
                        u_last = P.op("pe", lambda e, dci=dci, gb=gb, ub=ub, yt=yt: e.matmul(
                            out=pu[gb][:],
                            lhsT=wu_buf[:, ub * 1024 + dci * 128: ub * 1024 + (dci + 1) * 128],
                            rhs=yt[:, dci * 512:(dci + 1) * 512],
                            start=(dci == 0), stop=(dci == DC - 1)),
                            waits=w0, inc="pe" if dci == DC - 1 else None)
                    st2["wu_free"][ub] = u_last
                    sl = P.op("act", lambda e, gb=gb: e.activation(
                        out=a_sb[:, gb * 512:(gb + 1) * 512], in_=pg[gb][:], func=AF.Silu,
                        scale=consts[:, 1:2]),
                        waits=[g_last, st2["a_free"][gb]], inc="act")
                    st2["pg_free"][gb] = sl
                    ht = P.op("dve", lambda e, hc=hc, gb=gb: e.tensor_tensor(
                        out=h_sb[:, hc * 512:(hc + 1) * 512],
                        in0=a_sb[:, gb * 512:(gb + 1) * 512], in1=pu[gb][:], op=ALU.mult),
                        waits=[sl, u_last, st2["h_free"]], inc="dve")
                    st2["pu_free"][gb] = ht
                    st2["a_free"][gb] = ht
                    h_toks.append(ht)

                if t + 1 < NT:
                    plan_y(t + 1)
                if t >= 1:
                    p2b_fin(t - 1)

                dbank = [pd[0], pd[1], pd[2], pd[3], pg[0], pg[1], pu[0], pu[1]]
                dfree = (st2["pd_free"]
                         + [st2["pg_free"][0], st2["pg_free"][1],
                            st2["pu_free"][0], st2["pu_free"][1]])
                spre_toks = []
                acc = [None] * 8
                for hc in range(HC):
                    wb2 = hc % 4
                    ld = P.dma("sync", wd_buf[:, wb2 * 1024:(wb2 + 1) * 1024], wd_sc[hc],
                               waits=[st2["wd_free"][wb2]], sem="qwd%d" % wb2)
                    mmt = None
                    for dsi in range(8):
                        w0 = [ld, h_toks[hc]]
                        if hc == 0:
                            w0.append(dfree[dsi])
                        need_inc = (dsi == 7) or (hc == HC - 1)
                        mmt = P.op("pe", lambda e, wb2=wb2, dsi=dsi, hc=hc: e.matmul(
                            out=dbank[dsi][:],
                            lhsT=wd_buf[:, wb2 * 1024 + dsi * 128: wb2 * 1024 + (dsi + 1) * 128],
                            rhs=h_sb[:, hc * 512:(hc + 1) * 512],
                            start=(hc == 0), stop=(hc == HC - 1)),
                            waits=w0 if (dsi == 0 or hc == 0) else [],
                            inc="pe" if need_inc else None)
                        acc[dsi] = mmt
                    st2["wd_free"][wb2] = mmt
                for dsi in range(8):
                    rcol = dsi * TOK + t * 512
                    sp = P.op("dve", lambda e, dsi=dsi, rcol=rcol, yt=yt: e.scalar_tensor_tensor(
                        out=res[:, rcol:rcol + 512], in0=dbank[dsi][:], scalar=consts[:, 2:3],
                        in1=yt[:, dsi * 512:(dsi + 1) * 512], op0=ALU.mult, op1=ALU.add),
                        waits=[acc[dsi]], inc="dve")
                    if dsi < 4:
                        st2["pd_free"][dsi] = sp
                    elif dsi < 6:
                        st2["pg_free"][dsi - 4] = sp
                    else:
                        st2["pu_free"][dsi - 6] = sp
                    spre_toks.append(sp)
                st2["y_free"][yb] = spre_toks[-1]
                st2["h_free"] = spre_toks[-1]
                st2["spre"][t] = spre_toks[-1]
                p2b_sq(t, spre_toks)
            p2b_stage(NT - 1)
            p2b_fin(NT - 1)
            P.op("pool", lambda e: e.nop(),
                 waits=[("qout0", P.cnt["qout0"]), ("qout1", P.cnt["qout1"])],
                 inc=None)
            with nc.Block("p2a") as blk:
                P.emit(blk)
        nc.all_engine_barrier()
        if phases < 4:
            return nc

    return nc



_NC_CACHE = {}

def _get_program():
    if "nc" not in _NC_CACHE:
        _NC_CACHE["nc"] = build_core_program()
    return _NC_CACHE["nc"]


def stage_weights(L0_attn_W_qkv, L0_attn_W_out, L0_ffn_w_gate, L0_ffn_w_up,
                  L0_ffn_w_down):
    w_qkv = np.ascontiguousarray(np.asarray(L0_attn_W_qkv, dtype=np.float32))
    w_out = np.asarray(L0_attn_W_out, dtype=np.float32)
    wg = np.asarray(L0_ffn_w_gate, dtype=np.int32)
    wu = np.asarray(L0_ffn_w_up, dtype=np.int32)
    wd = np.asarray(L0_ffn_w_down, dtype=np.int32)
    qk_shuf = np.ascontiguousarray(
        w_qkv[:, 0:2048].reshape(8, 128, 16, 128).transpose(2, 1, 0, 3).reshape(16, 128, 1024))
    wo_shuf = np.ascontiguousarray(
        w_out.reshape(8, 128, 8, 128).transpose(2, 1, 0, 3).reshape(8, 128, 1024))
    wgT = np.ascontiguousarray(wg.T)
    wuT = np.ascontiguousarray(wu.T)
    wu_shuf = np.ascontiguousarray(
        wuT.reshape(8, 128, 24, 128).transpose(2, 1, 0, 3).reshape(24, 128, 1024))
    wdT = np.ascontiguousarray(wd.T)
    return {"w_qkv": w_qkv, "qk_shuf": qk_shuf, "wo_shuf": wo_shuf,
            "wgT": wgT, "wu_shuf": wu_shuf, "wdT": wdT}


def kernel(L0_attn_x, L0_attn_W_qkv, L0_attn_W_out, L0_ffn_w_gate, L0_ffn_w_up,
           L0_ffn_w_down):
    x = np.ascontiguousarray(np.asarray(L0_attn_x, dtype=np.float32))
    wmap = stage_weights(L0_attn_W_qkv, L0_attn_W_out, L0_ffn_w_gate,
                         L0_ffn_w_up, L0_ffn_w_down)

    nc = _get_program()
    in_maps = []
    for c in range(NCORES):
        xc = x[c * SEQ_PER_CORE:(c + 1) * SEQ_PER_CORE]
        x_fm = np.ascontiguousarray(xc.transpose(0, 2, 1))
        m = {"x_fm": x_fm}
        m.update(wmap)
        in_maps.append(m)
    res = run_bass_kernel_spmd(nc, in_maps, core_ids=list(range(NCORES)))
    out = np.empty((B, S, D), dtype=np.float32)
    for c in range(NCORES):
        o = res.results[c]["out_fm"]
        out[c * SEQ_PER_CORE:(c + 1) * SEQ_PER_CORE] = o.transpose(0, 2, 1)
    return out


if __name__ == "__main__":
    nc = build_core_program()
    print("built ok, n_inst =", nc.n_instructions())


# version 29
# speedup vs baseline: 1.0476x; 1.0120x over previous
import numpy as np
import ml_dtypes
import concourse.bass as bass
import concourse.mybir as mybir
from concourse.bass_utils import run_bass_kernel_spmd
from contextlib import ExitStack

F32 = mybir.dt.float32
F16 = mybir.dt.float16
BF16 = mybir.dt.bfloat16
I32 = mybir.dt.int32
AF = mybir.ActivationFunctionType
ALU = mybir.AluOpType

NCORES = 8
B, S, D = 16, 2048, 1024
H, DH, WS = 16, 64, 128
HID = 3072
SEQ_PER_CORE = B // NCORES
TOK = SEQ_PER_CORE * S
TILE = 512
NT = TOK // TILE
TPS = NT // SEQ_PER_CORE
WPT = TILE // WS
DC = D // 128
HC = HID // 128
THETA = 10000.0
ESHIFT = 5.0
EPS = 1e-5


def _host_tables():
    inv_freq = 1.0 / THETA ** (np.arange(0, DH, 2, dtype=np.float64) / DH)
    d_in_head = np.arange(128) % 64
    freq_p = inv_freq[d_in_head % 32]
    pos = np.arange(WS, dtype=np.float64)
    ang = pos[None, :] * freq_p[:, None]
    cos = np.cos(ang)
    sin = np.sin(ang)
    sign = np.where(d_in_head < 32, -1.0, 1.0)[:, None]
    d_in_head2 = d_in_head
    sign = np.where((d_in_head2 % 64) < 32, -1.0, 1.0)[:, None]
    ssin = sin * sign
    cos4 = np.tile(cos, (1, WPT))
    ssin4 = np.tile(ssin, (1, WPT))
    scale = DH ** (-0.5)
    tabs = {
        "cos_q": (cos4 * scale).astype(np.float16),
        "sin_q": (ssin4 * scale).astype(np.float16),
        "cos_k": cos4.astype(np.float16),
        "sin_k": ssin4.astype(np.float16),
    }
    a128 = 128.0 * inv_freq[np.arange(64) % 32]
    c1, s1 = np.cos(a128), np.sin(a128)
    R = np.zeros((64, 64), dtype=np.float64)
    for d in range(64):
        R[d, d] = c1[d]
        if d < 32:
            R[d, d + 32] = -s1[d]
        else:
            R[d, d - 32] = s1[d]
    R2 = np.zeros((128, 128), dtype=np.float64)
    R2[:64, :64] = R
    R2[64:, 64:] = R
    tabs["m128T"] = np.ascontiguousarray(R2.T).astype(np.float16)
    j = np.arange(128)[:, None]
    i = np.arange(128)[None, :]
    half0 = (j >= i).astype(np.float64)
    half1 = (j <= i).astype(np.float64)
    maskA = np.concatenate([half0, half1], axis=1)
    madd = (1.0 - np.tile(maskA, (1, 2))) * (-60000.0)
    madd0 = madd.copy()
    madd0[:, 0:128] = -60000.0
    madd0[:, 256:384] = -60000.0
    tabs["mask_n"] = madd.astype(np.float16)
    tabs["mask_0"] = madd0.astype(np.float16)
    tabs["ident"] = np.eye(128).astype(np.float16)
    tabs["ones_col"] = np.ones((128, 1)).astype(ml_dtypes.bfloat16)
    tabs["ones_col16"] = np.ones((128, 1)).astype(np.float16)
    tabs["ones_row"] = np.ones((1, 128)).astype(ml_dtypes.bfloat16)
    consts = np.zeros((128, 8), dtype=np.float32)
    consts[:, 0] = -ESHIFT
    consts[:, 1] = 1.0 / 127.0
    consts[:, 2] = 1.0 / (127.0 * 127.0)
    consts[:, 3] = 1.0 / float(D)
    consts[:, 4] = EPS
    consts[:, 5] = -0.5
    consts[:, 6] = 1.0
    return tabs, consts


class Prog:
    ENGS = ("pe", "act", "dve", "pool", "sync")

    def __init__(self, nc, ctx):
        self.nc = nc
        self.sems = {}
        self.cnt = {}
        names = ["pe", "act", "dve", "pool",
                 "qx0", "qx1", "qrot", "qrot2", "qwk0", "qwk1", "qwk2", "qwk3",
                 "qwo0", "qwo1", "qwo2", "qwo3",
                 "qwu0", "qwu1", "qwu2", "qwu3", "qwd0", "qwd1", "qwd2", "qwd3", "qtab",
                 "qpl0", "qpl1", "qpl2", "qpl3", "qpl4", "qpl5",
                 "qis0", "qis1", "qis2", "qis3", "qis4", "qis5",
                 "qps0", "qps1", "qps2", "qps3", "qout0", "qout1",
                 "qil0", "qil1", "qil2", "qil3",
                 "qwg0", "qwg1", "qwg2", "qwg3",
                 "qwg4", "qwg5", "qwg6", "qwg7"]
        for name in names:
            self.sems[name] = ctx.enter_context(nc.semaphore("s_" + name))
            self.cnt[name] = 0
        self.ops = {k: [] for k in self.ENGS}

    def op(self, eng, fn, waits=(), inc=None, inc_k=1):
        waits = tuple(w for w in waits if w is not None)
        self.ops[eng].append((fn, waits, inc, inc_k))
        if inc is not None:
            self.cnt[inc] += inc_k
            return (inc, self.cnt[inc])
        return None

    def dma(self, eng, out, in_, waits=(), sem="qtab"):
        return self.op(eng, lambda e, o=out, i=in_: e.dma_start(out=o, in_=i),
                       waits=waits, inc=sem, inc_k=16)

    def emit(self, blk):
        engmap = {"pe": blk.tensor, "act": blk.scalar, "dve": blk.vector,
                  "pool": blk.gpsimd, "sync": blk.sync}
        for name in self.ENGS:
            ops = self.ops[name]
            if not ops:
                continue

            def body(e, ops=ops):
                hw = {}
                for fn, waits, inc, inc_k in ops:
                    need = {}
                    for (sname, val) in waits:
                        if hw.get(sname, 0) < val and need.get(sname, 0) < val:
                            need[sname] = val
                    for sname, val in need.items():
                        e.wait_ge(self.sems[sname], val)
                        hw[sname] = val
                    inst = fn(e)
                    if inc is not None:
                        inst.then_inc(self.sems[inc], inc_k)
            engmap[name](body)
        self.ops = {k: [] for k in self.ENGS}


def build_core_program(phases=4, p1_stage=4, p1_tiles=NT, p1_attn=6):
    nc = bass.Bass("TRN2", target_bir_lowering=False)
    tabs, consts_np = _host_tables()

    x_fm = nc.dram_tensor("x_fm", [SEQ_PER_CORE, D, S], F32, kind="ExternalInput").ap()
    w_qkv = nc.dram_tensor("w_qkv", [D, 3 * D], F32, kind="ExternalInput").ap()
    qk_shuf = nc.dram_tensor("qk_shuf", [16, 128, 1024], F32, kind="ExternalInput").ap()
    wo_shuf = nc.dram_tensor("wo_shuf", [8, 128, 1024], F32, kind="ExternalInput").ap()
    wgT_d = nc.dram_tensor("wgT", [D, HID], I32, kind="ExternalInput").ap()
    wu_shuf = nc.dram_tensor("wu_shuf", [HC, 128, 1024], I32, kind="ExternalInput").ap()
    wdT_d = nc.dram_tensor("wdT", [HID, D], I32, kind="ExternalInput").ap()
    out_fm = nc.dram_tensor("out_fm", [SEQ_PER_CORE, D, S], F32, kind="ExternalOutput").ap()

    wqk_sc = nc.dram_tensor("wqk_sc", [16, 128, 1024], F16, kind="Internal").ap()
    wv_sc = nc.dram_tensor("wv_sc", [8, 128, 1024], F16, kind="Internal").ap()
    wo_sc = nc.dram_tensor("wo_sc", [8, 128, 1024], F16, kind="Internal").ap()
    wg_sc = nc.dram_tensor("wg_sc", [8, 128, HID], BF16, kind="Internal").ap()
    wu_sc = nc.dram_tensor("wu_sc", [HC, 128, 1024], BF16, kind="Internal").ap()
    wd_sc = nc.dram_tensor("wd_sc", [HC, 128, 1024], BF16, kind="Internal").ap()

    cst = {k: nc.inline_tensor(v, "c_" + k).ap() for k, v in tabs.items()}
    cst["consts"] = nc.inline_tensor(consts_np, "c_consts").ap()

    with ExitStack() as top:
        sb = lambda name, shape, dt: top.enter_context(nc.sbuf_tensor(name, shape, dt))
        P = Prog(nc, top)

        res = sb("res", [128, DC * TOK], BF16)
        rows = sb("rows", [1, 2 * TOK], BF16)
        consts = sb("consts", [128, 8], F32)
        onec = sb("onec", [128, 1], BF16)
        onec16 = sb("onec16", [128, 1], F16)
        oner = sb("oner", [1, 128], BF16)

        with ExitStack() as ctx:
            psb = lambda name, shape, dt: ctx.enter_context(nc.sbuf_tensor(name, shape, dt))
            pinf = [psb(f"pinf_{i}", [128, 1024], F32) for i in range(4)]
            pot16 = [psb(f"pot16_{i}", [128, 1024], F16) for i in range(4)]

            P.dma("sync", consts[:], cst["consts"], sem="qtab")
            P.dma("sync", onec[:], cst["ones_col"], sem="qtab")
            P.dma("sync", onec16[:], cst["ones_col16"], sem="qtab")
            P.dma("sync", oner[:], cst["ones_row"], sem="qtab")

            freeL = {i: None for i in range(4)}
            freeS = {i: None for i in range(4)}
            fp16_jobs = []
            for oc in range(16):
                fp16_jobs.append((qk_shuf[oc], wqk_sc[oc]))
            for dc in range(DC):
                fp16_jobs.append((w_qkv[dc * 128:(dc + 1) * 128, 2048:3072], wv_sc[dc]))
            for ds in range(DC):
                fp16_jobs.append((wo_shuf[ds], wo_sc[ds]))
            for j, (srcap, dstap) in enumerate(fp16_jobs):
                bL = j % 4
                eng_c = "act" if j % 2 == 0 else "dve"
                ld = P.dma("sync", pinf[bL][:, 0:1024], srcap, waits=[freeL[bL]],
                           sem="qpl%d" % bL)
                cv = P.op(eng_c, lambda e, bL=bL, eng_c=eng_c: (
                    e.copy if eng_c == "act" else e.tensor_copy)(
                    out=pot16[bL][:], in_=pinf[bL][:]),
                    waits=[ld, freeS[bL]], inc=eng_c)
                freeL[bL] = cv
                freeS[bL] = P.dma("act", dstap, pot16[bL][:], waits=[cv],
                                  sem="qps%d" % bL)
            P.op("sync", lambda e: e.nop(),
                 waits=[("qps0", P.cnt["qps0"]), ("qps1", P.cnt["qps1"]),
                        ("qps2", P.cnt["qps2"]), ("qps3", P.cnt["qps3"]),
                        ("qtab", P.cnt["qtab"])])
            with nc.Block("prep") as blk:
                P.emit(blk)
        nc.all_engine_barrier()
        if phases < 2:
            return nc

        with ExitStack() as ctx:
            psb = lambda name, shape, dt: ctx.enter_context(nc.sbuf_tensor(name, shape, dt))
            pps = lambda name, shape, dt: ctx.enter_context(nc.psum_tensor(name, shape, dt))
            cos_q = psb("cos_q", [128, 512], F16); sin_q = psb("sin_q", [128, 512], F16)
            cos_k = psb("cos_k", [128, 512], F16); sin_k = psb("sin_k", [128, 512], F16)
            m128T = psb("m128T", [128, 128], F16)
            mask_n = psb("mask_n", [128, 512], F16)
            mask_0 = psb("mask_0", [128, 512], F16)
            ident = psb("ident", [128, 128], F16)
            wv_sb = psb("wv_sb", [128, 8 * 1024], F16)
            wqk_buf = psb("wqk_buf", [128, 4 * 1024], F16)
            x32 = [psb(f"x32_{i}", [128, 512], F32) for i in range(2)]
            x16 = [psb(f"x16_{i}", [128, DC * 512], F16) for i in range(2)]
            q_sb = psb("q_sb", [128, DC * 512], F16)
            k_sb = psb("k_sb", [128, DC * 512], F16)
            rotb = [psb(f"rotb{i}", [128, 1024], F16) for i in range(2)]
            q_lo = psb("q_lo", [128, DC * 512], F16)
            q_hi = psb("q_hi", [128, DC * 512], F16)
            k_lo = psb("k_lo", [128, DC * 640], F16)
            vhat = psb("vhat", [128, 5 * 1040], F16)
            pt_sb = [psb(f"pt_sb{i}", [128, 1024], F16) for i in range(2)]
            attn_fm = q_sb
            attn_tm = k_sb
            zsq = q_hi
            rowsc = psb("rowsc", [1, 1024], F32)
            recb = psb("recb", [128, 16], F32)
            pin32 = [psb(f"pin32_{i}", [128, 512], I32) for i in range(4)]
            potbf = [psb(f"potbf_{i}", [128, 512], BF16) for i in range(4)]
            ps_ab = pps("ps_ab", [128, 1024], F32)
            ps_sim = pps("ps_sim", [128, 1024], F32)
            ps_pv = pps("ps_pv", [128, 2048], F32)

            tl = []
            for t_sb, key in ((cos_q, "cos_q"), (sin_q, "sin_q"), (cos_k, "cos_k"),
                              (sin_k, "sin_k"), (m128T, "m128T"), (mask_n, "mask_n"),
                              (mask_0, "mask_0"), (ident, "ident")):
                tl.append(P.dma("sync", t_sb[:], cst[key]))
            for dc in range(DC):
                tl.append(P.dma("sync", wv_sb[:, dc * 1024:(dc + 1) * 1024], wv_sc[dc]))
            tables_tok = tl[-1]
            vz = P.op("pool", lambda e: e.memset(vhat[:], 0.0), inc="pool")
            ones_ready = None
            for sl in range(1, 5):
                ap = vhat.ap()[:, sl * 1040:(sl + 1) * 1040].rearrange(
                    "p (h c) -> p h c", h=16)[:, :, 64:65]
                ones_ready = P.op("pool", lambda e, ap=ap: e.memset(ap, 1.0),
                                  waits=[vz], inc="pool")

            pending_rows = [None]
            int_jobs = []
            for dc in range(DC):
                for p_ in range(6):
                    int_jobs.append((wgT_d[dc * 128:(dc + 1) * 128, p_ * 512:(p_ + 1) * 512],
                                     wg_sc[dc, :, p_ * 512:(p_ + 1) * 512]))
            for hc in range(HC):
                for p_ in range(2):
                    int_jobs.append((wu_shuf[hc, :, p_ * 512:(p_ + 1) * 512],
                                     wu_sc[hc, :, p_ * 512:(p_ + 1) * 512]))
            for hc in range(HC):
                for p_ in range(2):
                    int_jobs.append((wdT_d[hc * 128:(hc + 1) * 128, p_ * 512:(p_ + 1) * 512],
                                     wd_sc[hc, :, p_ * 512:(p_ + 1) * 512]))
            NJOB = len(int_jobs)
            ip = {"ld": 0, "cv": 0, "stq": 0, "xt_allow": 0,
                  "ld_tok": [None] * NJOB, "cv_tok": [None] * NJOB,
                  "st_tok": [None] * NJOB}

            xp = {"t": 1, "ld": 0, "cv": 0, "ld_tok": [None] * DC,
                  "cv_tok": [None] * DC}

            def x_step():
                if xp["t"] >= NT or xp["t"] > ip["xt_allow"]:
                    return
                t_ = xp["t"]
                xb = t_ % 2
                x16t = x16[xb]
                prev_ld, prev_cv = xp["ld"], xp["cv"]
                while xp["cv"] < prev_ld:
                    j = xp["cv"]
                    cv = P.op("act", lambda e, j=j, x16t=x16t: e.copy(
                        out=x16t[:, j * 512:(j + 1) * 512], in_=x32[j % 2][:]),
                        waits=[xp["ld_tok"][j],
                               st["x16_free"][xb] if j == 0 else None], inc="act")
                    st["x32_free"][j % 2] = cv
                    xp["cv_tok"][j] = cv
                    xp["cv"] = j + 1
                n = 0
                while n < 2 and xp["ld"] < DC and xp["ld"] < prev_cv + 2:
                    j = xp["ld"]
                    src = x_fm[t_ // TPS, j * 128:(j + 1) * 128,
                               (t_ % TPS) * TILE:(t_ % TPS) * TILE + TILE]
                    xp["ld_tok"][j] = P.dma("pool", x32[j % 2][:], src,
                                            waits=[st["x32_free"][j % 2]],
                                            sem="qx%d" % (j % 2))
                    xp["ld"] = j + 1
                    n += 1
                if xp["cv"] == DC:
                    st["x_ready"][t_] = xp["cv_tok"][DC - 1]
                    xp["t"] += 1
                    xp["ld"] = 0
                    xp["cv"] = 0

            def bg_prep():
                x_step()
                prev_ld, prev_cv, prev_st = ip["ld"], ip["cv"], ip["stq"]
                while ip["stq"] < prev_cv:
                    j = ip["stq"]
                    ip["st_tok"][j] = P.dma("act", int_jobs[j][1], potbf[j % 4][:],
                                            waits=[ip["cv_tok"][j]], sem="qis%d" % (j % 4))
                    ip["stq"] = j + 1
                while ip["cv"] < min(prev_ld, prev_st + 4):
                    j = ip["cv"]
                    ip["cv_tok"][j] = P.op("act", lambda e, b_=j % 4: e.copy(
                        out=potbf[b_][:], in_=pin32[b_][:]),
                        waits=[ip["ld_tok"][j],
                               ip["st_tok"][j - 4] if j >= 4 else None], inc="act")
                    ip["cv"] = j + 1
                n_new = 0
                while n_new < 2 and ip["ld"] < min(NJOB, prev_cv + 4):
                    j = ip["ld"]
                    ip["ld_tok"][j] = P.dma("act", pin32[j % 4][:], int_jobs[j][0],
                                            waits=[ip["cv_tok"][j - 4] if j >= 4 else None],
                                            sem="qil%d" % (j % 4))
                    ip["ld"] = j + 1
                    n_new += 1

            st = {
                "x_ready": [None] * NT,
                "x32_free": [None, None],
                "x16_free": [None, None],
                "rope_q_done": None, "tcs_k_prev": None,
                "rot_free": [None, None], "qhi_evac": None,
                "qk_used": None, "ring_copied": None,
                "psA_free": None, "psB_free": None,
                "psS0_free": None, "psS1_free": None, "psPV_free": None,
                "wqk_free": [None] * 4,
                "attn_tm_free": None, "attn_fm_used": None, "psPV_ev": None,
                "zsq_free": [None, None], "pt_free": [None, None], "pt_m_used": None,
                "pend_pv": None,
                "recb_free": None,
                "stats_free": None, "rows_used": None,
            }
            zrows_toks = []

            for t in range(min(NT, p1_tiles)):
                seq = t // TPS
                col0 = (t % TPS) * TILE
                first_tile_of_seq = (col0 == 0)
                xb = t % 2
                x16t = x16[xb]

                if t == 0:
                    x_cast_toks = []
                    for dc in range(DC):
                        b = dc % 2
                        src = x_fm[seq, dc * 128:(dc + 1) * 128, col0:col0 + TILE]
                        ld = P.dma("pool", x32[b][:], src, waits=[st["x32_free"][b]],
                                   sem="qx%d" % b)
                        cv = P.op("act", lambda e, b=b, dc=dc, x16t=x16t: e.copy(
                            out=x16t[:, dc * 512:(dc + 1) * 512], in_=x32[b][:]),
                            waits=[ld, st["x16_free"][xb]], inc="act")
                        st["x32_free"][b] = cv
                        x_cast_toks.append(cv)
                    st["x_ready"][0] = x_cast_toks[-1]
                ip["xt_allow"] = t + 1
                x_ready = st["x_ready"][t]
                if pending_rows[0] is not None:
                    pending_rows[0]()
                    pending_rows[0] = None
                bg_prep()

                qk_ev = {}
                slots = [ps_ab[:, 0:512], ps_ab[:, 512:1024],
                         ps_sim[:, 0:512], ps_sim[:, 512:1024]]
                skey = ["psA_free", "psB_free", "psS0_free", "psS1_free"]
                def emit_v(vb, x16t=x16t, x_ready=x_ready):
                    w_ = vb // 2
                    half = vb % 2
                    sl = w_ + 1
                    mmtok = None
                    for dci in range(DC):
                        w = ([x_ready, tables_tok, ones_ready, st["psS1_free"]]
                             if dci == 0 else [])
                        mmtok = P.op("pe", lambda e, w_=w_, half=half, dci=dci,
                                     x16t=x16t: e.matmul(
                            out=ps_sim[:, 512:1024],
                            lhsT=x16t[:, dci * 512 + w_ * 128: dci * 512 + w_ * 128 + 128],
                            rhs=wv_sb[:, dci * 1024 + half * 512: dci * 1024 + (half + 1) * 512],
                            start=(dci == 0), stop=(dci == DC - 1)),
                            waits=w, inc="pe" if dci == DC - 1 else None)
                    hbase = half * 8
                    dst = vhat.ap()[:, sl * 1040 + hbase * 65: sl * 1040 + (hbase + 8) * 65]
                    dst = dst.rearrange("p (h c) -> p h c", h=8)[:, :, 0:64]
                    srcv = ps_sim.ap()[:, 512:1024].rearrange("p (h c) -> p h c", h=8)
                    ev = P.op("act", lambda e, dst=dst, srcv=srcv: e.copy(
                        out=dst, in_=srcv), waits=[mmtok, st["ring_copied"]],
                        inc="act")
                    st["psS1_free"] = ev
                    st["v_ev_last"] = ev

                for idx, oc in enumerate([8, 9, 10, 11, 12, 13, 14, 15,
                                          0, 1, 2, 3, 4, 5, 6, 7]):
                    if idx in (4, 8, 12):
                        bg_prep()
                    b = idx % 4
                    s4 = idx % 3
                    ld = P.dma("sync", wqk_buf[:, b * 1024:(b + 1) * 1024], wqk_sc[oc],
                               waits=[tables_tok, st["wqk_free"][b]], sem="qwk%d" % b)
                    ps = slots[s4]
                    ps_free = st[skey[s4]]
                    mmtok = None
                    for dci in range(DC):
                        w = [ld, x_ready, ps_free] if dci == 0 else []
                        mmtok = P.op("pe", lambda e, b=b, dci=dci, ps=ps, x16t=x16t: e.matmul(
                            out=ps,
                            lhsT=wqk_buf[:, b * 1024 + dci * 128: b * 1024 + (dci + 1) * 128],
                            rhs=x16t[:, dci * 512:(dci + 1) * 512],
                            start=(dci == 0), stop=(dci == DC - 1)),
                            waits=w, inc="pe" if dci == DC - 1 else None)
                    st["wqk_free"][b] = mmtok
                    dest = q_sb if oc < 8 else k_sb
                    ci = oc % 8
                    wv_ = [mmtok]
                    if oc < 8:
                        wv_.append(st["rope_q_done"])
                    else:
                        if ci < 2:
                            wv_.append(st["attn_tm_free"])
                        if st["tcs_k_prev"] is not None:
                            wv_.append(st["tcs_k_prev"][ci // 4])
                    ev = P.op("act", lambda e, dest=dest, ci=ci, ps=ps: e.copy(
                        out=dest[:, ci * 512:(ci + 1) * 512], in_=ps),
                        waits=wv_, inc="act")
                    qk_ev[oc] = ev
                    st[skey[s4]] = ev
                    if idx % 2 == 1:
                        emit_v(idx // 2)

                if p1_stage < 2:
                    continue
                v_ready = st["v_ev_last"]

                x3 = lambda t_: t_.ap().rearrange("p (c n) -> p c n", c=DC)
                bc2 = lambda tab: tab.ap().unsqueeze(1).broadcast_to([128, 2, 512])
                bc4_ = lambda tab: tab.ap().unsqueeze(1).broadcast_to([128, 4, 512])

                def rope_pass2(src_sb, scratch, dst_qtr, cos_t, sin_t, ev_q,
                               w_scr0, w_dst):
                    tbs = [None] * 4
                    tcs = [None, None]
                    for u in range(4):
                        h = u // 2
                        bq = u % 2
                        c0 = u * 1024
                        sl = src_sb.ap()[:, c0:c0 + 1024].rearrange(
                            "p (c n) -> p c n", c=2)
                        scl = scratch.ap()[:, c0:c0 + 1024].rearrange(
                            "p (c n) -> p c n", c=2)
                        wsc = list(w_scr0[0]) if u == 0 else (
                            list(w_scr0[1]) if u == 2 else [])
                        ta = P.op("dve", lambda e, sl=sl, scl=scl, cos_t=cos_t:
                                  e.tensor_tensor(out=scl, in0=sl, in1=bc2(cos_t),
                                                  op=ALU.mult),
                                  waits=[ev_q[u]] + wsc, inc="dve")
                        rdw = [ev_q[u], st["rot_free"][bq]]
                        toks = []
                        for g in (0, 64):
                            toks.append(P.dma("pool", rotb[bq][g:g + 32, :],
                                              src_sb[g + 32:g + 64, c0:c0 + 1024],
                                              waits=rdw,
                                              sem="qrot" if bq == 0 else "qrot2"))
                            rdw = []
                            toks.append(P.dma("pool", rotb[bq][g + 32:g + 64, :],
                                              src_sb[g:g + 32, c0:c0 + 1024],
                                              sem="qrot" if bq == 0 else "qrot2"))
                        tbs[u] = P.op("dve", lambda e, u=u, bq=bq, c0=c0,
                                      sin_t=sin_t: e.tensor_tensor(
                            out=src_sb.ap()[:, c0:c0 + 1024].rearrange(
                                "p (c n) -> p c n", c=2),
                            in0=rotb[bq].ap().rearrange("p (c n) -> p c n", c=2),
                            in1=bc2(sin_t), op=ALU.mult),
                            waits=[toks[-1], ta], inc="dve")
                        st["rot_free"][bq] = tbs[u]
                        tc = P.op("dve", lambda e, sl=sl, scl=scl, u=u,
                                  dst_qtr=dst_qtr: e.tensor_tensor(
                            out=dst_qtr[u], in0=scl, in1=sl, op=ALU.add),
                            waits=[tbs[u]] + (w_dst if u == 0 else []), inc="dve")
                        tcs[h] = tc
                    return tcs

                k3 = k_lo.ap().rearrange("p (c n) -> p c n", c=DC)
                tcs_k = rope_pass2(k_sb, q_hi,
                                   [k3[:, 0:2, 128:640], k3[:, 2:4, 128:640],
                                    k3[:, 4:6, 128:640], k3[:, 6:8, 128:640]],
                                   cos_k, sin_k,
                                   [qk_ev[9], qk_ev[11], qk_ev[13], qk_ev[15]],
                                   [[st["qk_used"], st["qhi_evac"],
                                     st["zsq_free"][0], st["zsq_free"][1]],
                                    [st["qhi_evac"]]],
                                   [st["qk_used"], st["ring_copied"]])
                st["tcs_k_prev"] = tcs_k
                tcs_q = rope_pass2(q_sb, q_hi,
                                   [x3(q_lo)[:, 0:2, :], x3(q_lo)[:, 2:4, :],
                                    x3(q_lo)[:, 4:6, :], x3(q_lo)[:, 6:8, :]],
                                   cos_q, sin_q,
                                   [qk_ev[1], qk_ev[3], qk_ev[5], qk_ev[7]],
                                   [[tcs_k[0]], [tcs_k[1]]],
                                   [st["qk_used"]])
                qhi_ev_h = [None, None]
                for dci in range(DC):
                    ps = slots[dci % 3]
                    pf = st[skey[dci % 3]]
                    mm = P.op("pe", lambda e, dci=dci, ps=ps: e.matmul(
                        out=ps, lhsT=m128T[:], rhs=q_lo[:, dci * 512:(dci + 1) * 512],
                        start=True, stop=True), waits=[tcs_q[dci // 4], pf], inc="pe")
                    ev = P.op("act", lambda e, dci=dci, ps=ps: e.copy(
                        out=q_hi[:, dci * 512:(dci + 1) * 512], in_=ps),
                        waits=[mm, tcs_q[dci // 4]], inc="act")
                    st[skey[dci % 3]] = ev
                    qhi_ev_h[dci // 4] = ev
                st["qhi_evac"] = qhi_ev_h[1]
                bg_prep()

                if p1_stage < 3:
                    continue
                if first_tile_of_seq:
                    kprev = k_lo.ap().rearrange("p (c n) -> p c n", c=DC)[:, :, 0:128]
                    z1_ = P.op("pool", lambda e, kprev=kprev: e.memset(kprev, 0.0),
                               waits=[st["qk_used"], st["ring_copied"]], inc="pool")
                    seq_zero = P.op("pool", lambda e: e.memset(vhat[:, 0:1040], 0.0),
                                    waits=[z1_, st["pt_m_used"]], inc="pool")
                else:
                    seq_zero = st["ring_copied"]

                sim_last_all = None
                pend_tail = [None]
                for w_ in range(WPT):
                    wseq0 = first_tile_of_seq and w_ == 0
                    mtile = mask_0 if wseq0 else mask_n
                    pv_last = None
                    for g in range(4):
                        bankT = ps_sim if g % 2 == 0 else ps_ab
                        pt = pt_sb[g % 2]
                        if g % 2 == 0:
                            bfree = [st["psS0_free"], st["psS1_free"]]
                        else:
                            bfree = [st["psA_free"], st["psB_free"]]
                        first_w = [tcs_k[g // 2], qhi_ev_h[g // 2], v_ready,
                                   seq_zero, st["attn_tm_free"]] + bfree
                        for pp in range(2):
                            hp = 2 * g + pp
                            for sub in range(2):
                                pb = 64 * sub
                                base_i = hp * 512 + w_ * 128
                                klo_c = hp * 640
                                outc = sub * 512 + pp * 256
                                st_flag = (pp == 0)
                                w0 = first_w if (pp == 0 and sub == 0) else []
                                P.op("pe", lambda e, pb=pb, klo_c=klo_c, w_=w_,
                                     base_i=base_i, outc=outc, st_flag=st_flag,
                                     bankT=bankT: e.matmul(
                                    out=bankT[:, outc:outc + 128],
                                    lhsT=k_lo[pb:pb + 64, klo_c + 128 * w_: klo_c + 128 * w_ + 128],
                                    rhs=q_hi[pb:pb + 64, base_i:base_i + 128],
                                    start=st_flag, stop=False), waits=w0)
                                P.op("pe", lambda e, pb=pb, klo_c=klo_c, w_=w_,
                                     base_i=base_i, outc=outc, bankT=bankT: e.matmul(
                                    out=bankT[:, outc + 128:outc + 256],
                                    lhsT=k_lo[pb:pb + 64, klo_c + 128 + 128 * w_: klo_c + 256 + 128 * w_],
                                    rhs=q_lo[pb:pb + 64, base_i:base_i + 128],
                                    start=False, stop=False))
                        mk0 = P.op("pe", lambda e, bankT=bankT, mtile=mtile: e.matmul(
                            out=bankT[:, 0:512], lhsT=ident[:], rhs=mtile[:],
                            start=False, stop=True), inc="pe")
                        mk1 = P.op("pe", lambda e, bankT=bankT, mtile=mtile: e.matmul(
                            out=bankT[:, 512:1024], lhsT=ident[:], rhs=mtile[:],
                            start=False, stop=True), inc="pe")
                        sim_last_all = mk1
                        ex = P.op("act", lambda e, bankT=bankT, pt=pt: e.activation(
                            out=pt[:], in_=bankT[:], func=AF.Exp,
                            bias=consts[:, 0:1]),
                            waits=[mk0, mk1, st["pt_free"][g % 2]], inc="act")
                        if g % 2 == 0:
                            st["psS0_free"] = ex
                            st["psS1_free"] = ex
                        else:
                            st["psA_free"] = ex
                            st["psB_free"] = ex

                        def plan_pv(g=g, w_=w_, ex=ex, pt=pt):
                            pvl = None
                            for hh in range(4):
                                h = 4 * g + hh
                                sub = hh % 2
                                pp = hh // 2
                                ptc = sub * 512 + pp * 256
                                off = h * 128
                                w0 = ([ex, v_ready,
                                       st["psPV_free"] if g == 0 else st["psPV_ev"]]
                                      if hh == 0 else [])
                                P.op("pe", lambda e, h=h, w_=w_, off=off, ptc=ptc,
                                     pt=pt: e.matmul(
                                    out=ps_pv[:, off:off + 65],
                                    lhsT=pt[:, ptc: ptc + 128],
                                    rhs=vhat[:, w_ * 1040 + h * 65: w_ * 1040 + (h + 1) * 65],
                                    start=True, stop=False), waits=w0)
                                pvl = P.op("pe", lambda e, h=h, w_=w_, off=off, ptc=ptc,
                                           pt=pt: e.matmul(
                                    out=ps_pv[:, off:off + 65],
                                    lhsT=pt[:, ptc + 128: ptc + 256],
                                    rhs=vhat[:, (w_ + 1) * 1040 + h * 65: (w_ + 1) * 1040 + (h + 1) * 65],
                                    start=False, stop=True),
                                    inc="pe" if hh == 3 else None)
                            st["pt_free"][g % 2] = pvl
                            return pvl
                        if g == 1 and pend_tail[0] is not None:
                            pend_tail[0]()
                            pend_tail[0] = None
                        if st["pend_pv"] is not None:
                            pv_last = st["pend_pv"]()
                        st["pend_pv"] = plan_pv
                    pv_last = st["pend_pv"]()
                    st["pend_pv"] = None
                    st["pt_m_used"] = pv_last

                    def plan_tail(w_=w_, pv_last=pv_last):
                        pv3 = ps_pv.ap().rearrange("p (h c) -> p h c", h=16)
                        rc = P.op("dve", lambda e, pv3=pv3: e.reciprocal(
                            out=recb.ap().unsqueeze(2), in_=pv3[:, :, 64:65]),
                            waits=[pv_last, st["recb_free"]], inc="dve")
                        at3 = attn_tm.ap()[:, 0:1024].rearrange("p (h c) -> p h c", h=16)
                        ev = P.op("dve", lambda e, pv3=pv3, at3=at3: e.tensor_tensor(
                            out=at3, in0=pv3[:, :, 0:64],
                            in1=recb.ap().unsqueeze(2).broadcast_to([128, 16, 64]),
                            op=ALU.mult), waits=[rc, st["attn_tm_free"]], inc="dve")
                        st["recb_free"] = ev
                        st["psPV_ev"] = ev
                        s0v = ps_pv.ap().bitcast(F16)[:, 0:1024]
                        trs = []
                        for dci in range(DC):
                            trs.append(P.op("pe", lambda e, dci=dci, s0v=s0v: e.transpose(
                                out=s0v[:, dci * 128:(dci + 1) * 128],
                                in_=attn_tm[:, dci * 128:(dci + 1) * 128], identity=ident[:]),
                                waits=[ev] if dci == 0 else [],
                                inc="pe" if dci == DC - 1 else None))
                        cp1 = P.op("dve", lambda e, s0v=s0v, w_=w_: e.tensor_copy(
                            out=attn_fm.ap().rearrange("p (c n) -> p c n", c=DC)[:, :, w_ * 128:(w_ + 1) * 128],
                            in_=s0v.rearrange("p (c n) -> p c n", c=DC)),
                            waits=[trs[-1], st["attn_fm_used"]], inc="dve")
                        st["psPV_free"] = cp1
                        st["attn_tm_free"] = trs[-1]
                        return cp1
                    pend_tail[0] = plan_tail
                    bg_prep()
                cp1 = pend_tail[0]()
                pend_tail[0] = None
                bg_prep()
                st["qk_used"] = sim_last_all

                ksrc = k_lo.ap().rearrange("p (c n) -> p c n", c=DC)[:, :, 512:640]
                kdst = k_lo.ap().rearrange("p (c n) -> p c n", c=DC)[:, :, 0:128]
                rc1 = P.op("pool", lambda e, ksrc=ksrc, kdst=kdst: e.tensor_copy(
                    out=kdst, in_=ksrc), waits=[st["qk_used"]], inc="pool")
                rc2 = P.op("pool", lambda e: e.tensor_copy(
                    out=vhat[:, 0:1040], in_=vhat[:, 4 * 1040:5 * 1040]),
                    waits=[rc1, st["pt_m_used"]], inc="pool")
                st["ring_copied"] = rc2

                if p1_stage < 4:
                    continue
                z1_toks = []
                last_opj_mm = [None]
                for ds in range(DC):
                    ps = ps_ab[:, 0:512] if ds % 2 == 0 else ps_ab[:, 512:1024]
                    pf = st["psA_free"] if ds % 2 == 0 else st["psB_free"]
                    wb = ds % 4
                    ldo = P.dma("sync", wqk_buf[:, wb * 1024:(wb + 1) * 1024], wo_sc[ds],
                                waits=[st["wqk_free"][wb]], sem="qwo%d" % wb)
                    mmtok = None
                    for hoc in range(DC):
                        w0 = [cp1, pf, ldo] if hoc == 0 else []
                        mmtok = P.op("pe", lambda e, hoc=hoc, ps=ps, wb=wb: e.matmul(
                            out=ps,
                            lhsT=wqk_buf[:, wb * 1024 + hoc * 128: wb * 1024 + (hoc + 1) * 128],
                            rhs=attn_fm[:, hoc * 512:(hoc + 1) * 512],
                            start=(hoc == 0), stop=(hoc == DC - 1)),
                            waits=w0, inc="pe" if hoc == DC - 1 else None)
                    st["wqk_free"][wb] = mmtok
                    last_opj_mm[0] = mmtok
                    rcol = ds * TOK + t * 512
                    zt = P.op("dve", lambda e, ds=ds, ps=ps, rcol=rcol, x16t=x16t: e.tensor_tensor(
                        out=res[:, rcol:rcol + 512], in0=ps,
                        in1=x16t[:, ds * 512:(ds + 1) * 512], op=ALU.add),
                        waits=[mmtok], inc="dve")
                    z1_toks.append(zt)
                    if ds % 2 == 0:
                        st["psA_free"] = zt
                    else:
                        st["psB_free"] = zt
                st["x16_free"][xb] = z1_toks[-1]
                st["attn_fm_used"] = last_opj_mm[0]
                st["rope_q_done"] = last_opj_mm[0]

                sum_last = None
                sq_last = None
                for ds in range(DC):
                    rcol = ds * TOK + t * 512
                    zb = ds % 2
                    sq = P.op("dve", lambda e, rcol=rcol, zb=zb: e.tensor_tensor(
                        out=zsq[:, zb * 512:(zb + 1) * 512], in0=res[:, rcol:rcol + 512],
                        in1=res[:, rcol:rcol + 512], op=ALU.mult),
                        waits=[z1_toks[ds], st["zsq_free"][zb], st["qk_used"],
                               st["rows_used"] if ds == 0 else None], inc="dve")
                    w0 = [z1_toks[ds], st["psA_free"] if ds == 0 else None]
                    sum_last = P.op("pe", lambda e, rcol=rcol, ds=ds: e.matmul(
                        out=ps_ab[0:1, 0:512], lhsT=onec[:], rhs=res[:, rcol:rcol + 512],
                        start=(ds == 0), stop=(ds == DC - 1)),
                        waits=w0, inc="pe")
                    w1 = [sq, st["psB_free"] if ds == 0 else None]
                    sq_last = P.op("pe", lambda e, zb=zb, ds=ds: e.matmul(
                        out=ps_ab[0:1, 512:1024], lhsT=onec16[:],
                        rhs=zsq[:, zb * 512:(zb + 1) * 512],
                        start=(ds == 0), stop=(ds == DC - 1)),
                        waits=w1, inc="pe")
                    st["zsq_free"][zb] = sq_last
                def mk_rows(t=t, sum_last=sum_last, sq_last=sq_last):
                    d1 = P.op("dve", lambda e, t=t: e.tensor_scalar(
                              out=rows[0:1, t * 512:(t + 1) * 512], in0=ps_ab[0:1, 0:512],
                              scalar1=consts[0:1, 3:4], scalar2=None, op0=ALU.mult),
                              waits=[sum_last, st["rows_used"]], inc="dve")
                    d2 = P.op("dve", lambda e, t=t: e.tensor_tensor(
                              out=rowsc[0:1, 0:512],
                              in0=rows[0:1, t * 512:(t + 1) * 512],
                              in1=rows[0:1, t * 512:(t + 1) * 512], op=ALU.mult),
                              waits=[d1], inc="dve")
                    d3 = P.op("dve", lambda e: e.scalar_tensor_tensor(
                              out=rowsc[0:1, 512:1024], in0=ps_ab[0:1, 512:1024],
                              scalar=consts[0:1, 3:4], in1=rowsc[0:1, 0:512],
                              op0=ALU.mult, op1=ALU.subtract),
                              waits=[sq_last, d2], inc="dve")
                    a4 = P.op("act", lambda e: e.activation(out=rowsc[0:1, 0:512],
                              in_=rowsc[0:1, 512:1024], func=AF.Ln, bias=consts[0:1, 4:5]),
                              waits=[d3, st["rows_used"]], inc="act")
                    a5 = P.op("act", lambda e, t=t: e.activation(
                              out=rows[0:1, TOK + t * 512: TOK + (t + 1) * 512],
                              in_=rowsc[0:1, 0:512], func=AF.Exp,
                              scale=consts[0:1, 5:6]), waits=[a4], inc="act")
                    zrows_toks.append(a5)
                    st["psA_free"] = d1
                    st["psB_free"] = d3
                    st["rows_used"] = a5
                st["psA_free"] = sum_last
                st["psB_free"] = sq_last
                bg_prep()
                if t == NT - 1:
                    mk_rows()
                else:
                    pending_rows[0] = mk_rows

            while ip["stq"] < NJOB:
                bg_prep()
            P.op("sync", lambda e: e.nop(),
                 waits=[("qis0", P.cnt["qis0"]), ("qis1", P.cnt["qis1"]),
                        ("qis2", P.cnt["qis2"]), ("qis3", P.cnt["qis3"])])
            p1_z1 = list(zrows_toks)
            with nc.Block("p1") as blk:
                P.emit(blk)
        nc.all_engine_barrier()
        if phases < 3:
            return nc

        with ExitStack() as ctx:
            psb = lambda name, shape, dt: ctx.enter_context(nc.sbuf_tensor(name, shape, dt))
            pps = lambda name, shape, dt: ctx.enter_context(nc.psum_tensor(name, shape, dt))
            wg_sb = psb("wg_sb", [128, DC * HID], BF16)
            wu_buf = psb("wu_buf", [128, 4 * 1024], BF16)
            wd_buf = psb("wd_buf", [128, 4 * 1024], BF16)
            y_sb = [psb(f"y_sb{i}", [128, DC * 512], BF16) for i in range(2)]
            y2 = psb("y2", [128, 512], BF16)
            a_sb = psb("a_sb", [128, 2 * 512], BF16)
            h_sb = psb("h_sb", [128, HC * 512], BF16)
            mbc = psb("mbc", [128, 512], BF16)
            rbc = psb("rbc", [128, 512], BF16)
            ssq = psb("ssq", [128, 4096], BF16)
            rowsc2 = psb("rowsc2", [1, 1024], F32)
            mbc2 = psb("mbc2", [128, 512], BF16)
            rbc2 = psb("rbc2", [128, 512], BF16)
            o32 = [psb(f"o32_{i}", [128, 512], F32) for i in range(2)]
            pg = [pps("pg0", [128, 512], F32), pps("pg1", [128, 512], F32)]
            pu = [pps("pu0", [128, 512], F32), pps("pu1", [128, 512], F32)]
            pd = [pps(f"pd{i}", [128, 512], F32) for i in range(4)]

            ldw = []
            for dc in range(DC):
                ldw.append(P.dma("sync", wg_sb[:, dc * HID:(dc + 1) * HID], wg_sc[dc],
                                 sem="qwg%d" % dc))

            st2 = {
                "pg_free": [None, None], "pu_free": [None, None],
                "pd_free": [None] * 4, "a_free": [None, None],
                "h_free": None, "y_free": [None, None], "bc_free": None,
                "y_ready": [None] * (NT + 1),
                "wd_free": [None] * 4, "wu_free": [None] * 4, "spre": [None] * NT,
                "yrows_done": [None] * (NT + 1),
            }
            st3 = {"ssq_free": None, "rows2_used": [None, None],
                   "bc_free": None, "rowsc_free": None, "stage": [None] * NT,
                   "sq": [None] * NT, "o32_free": [None, None]}

            def plan_y(t):
                yb = t % 2
                bm = P.op("pe", lambda e, t=t: e.matmul(
                    out=pg[0][:], lhsT=oner[:], rhs=rows[0:1, t * 512:(t + 1) * 512],
                    start=True, stop=True),
                    waits=[st2["pg_free"][0], st2["bc_free"]], inc="pe")
                em = P.op("act", lambda e: e.copy(out=mbc[:], in_=pd[0][:] if False else pg[0][:]),
                          waits=[bm], inc="act")
                br = P.op("pe", lambda e, t=t: e.matmul(
                    out=pg[1][:], lhsT=oner[:], rhs=rows[0:1, TOK + t * 512: TOK + (t + 1) * 512],
                    start=True, stop=True), waits=[st2["pg_free"][1], em], inc="pe")
                er = P.op("act", lambda e: e.copy(out=rbc[:], in_=pg[1][:]),
                          waits=[br], inc="act")
                st2["pg_free"][0] = em
                st2["pg_free"][1] = er
                st2["yrows_done"][t] = er
                prev_ty2 = None
                for dc in range(DC):
                    rcol = dc * TOK + t * 512
                    w0 = [em, st2["y_free"][yb], prev_ty2]
                    ty1 = P.op("dve", lambda e, rcol=rcol: e.tensor_tensor(
                        out=y2[:], in0=res[:, rcol:rcol + 512], in1=mbc[:], op=ALU.subtract),
                        waits=w0, inc="dve")
                    ty2 = P.op("dve", lambda e, dc=dc, yb=yb: e.tensor_tensor(
                        out=y_sb[yb][:, dc * 512:(dc + 1) * 512], in0=y2[:], in1=rbc[:],
                        op=ALU.mult), waits=[ty1, er], inc="dve")
                    prev_ty2 = ty2
                st2["bc_free"] = prev_ty2
                st2["y_ready"][t] = prev_ty2

            def p2b_sq(t, spre_toks):
                sq_tts = []
                for ds in range(DC):
                    rcol = ds * TOK + t * 512
                    sq = P.op("dve", lambda e, rcol=rcol, ds=ds: e.tensor_tensor(
                        out=ssq[:, ds * 512:(ds + 1) * 512],
                        in0=res[:, rcol:rcol + 512],
                        in1=res[:, rcol:rcol + 512], op=ALU.mult),
                        waits=[spre_toks[ds], st3["ssq_free"] if ds == 0 else None],
                        inc="dve")
                    sq_tts.append(sq)
                st3["sq"][t] = (sq_tts, spre_toks)

            def p2b_stage(t):
                tp_ = t % 2
                sq_tts, spre_toks = st3["sq"][t]
                sum_last = None
                sq_last = None
                for ds in range(DC):
                    rcol = ds * TOK + t * 512
                    sum_last = P.op("pe", lambda e, rcol=rcol, ds=ds: e.matmul(
                        out=pd[0][0:1, :], lhsT=onec[:], rhs=res[:, rcol:rcol + 512],
                        start=(ds == 0), stop=(ds == DC - 1)),
                        waits=[spre_toks[ds]], inc="pe" if ds == DC - 1 else None)
                    sq_last = P.op("pe", lambda e, ds=ds: e.matmul(
                        out=pd[1][0:1, :], lhsT=onec[:],
                        rhs=ssq[:, ds * 512:(ds + 1) * 512],
                        start=(ds == 0), stop=(ds == DC - 1)),
                        waits=[sq_tts[ds]], inc="pe")
                st3["ssq_free"] = sq_last
                d1 = P.op("dve", lambda e, t=t: e.tensor_scalar(
                          out=rows[0:1, t * 512:(t + 1) * 512], in0=pd[0][0:1, :],
                          scalar1=consts[0:1, 3:4], scalar2=None, op0=ALU.mult),
                          waits=[sum_last, st3["rows2_used"][tp_],
                                 st2["yrows_done"][t]], inc="dve")
                d2 = P.op("dve", lambda e, t=t: e.tensor_tensor(
                          out=rowsc2[0:1, 0:512],
                          in0=rows[0:1, t * 512:(t + 1) * 512],
                          in1=rows[0:1, t * 512:(t + 1) * 512], op=ALU.mult),
                          waits=[d1, st3["rowsc_free"]], inc="dve")
                d3 = P.op("dve", lambda e: e.scalar_tensor_tensor(
                          out=rowsc2[0:1, 512:1024], in0=pd[1][0:1, :],
                          scalar=consts[0:1, 3:4], in1=rowsc2[0:1, 0:512],
                          op0=ALU.mult, op1=ALU.subtract),
                          waits=[sq_last, d2], inc="dve")
                a4 = P.op("act", lambda e: e.activation(
                          out=rowsc2[0:1, 0:512],
                          in_=rowsc2[0:1, 512:1024], func=AF.Ln,
                          bias=consts[0:1, 4:5]),
                          waits=[d3, st3["rows2_used"][tp_]], inc="act")
                a5 = P.op("act", lambda e, t=t: e.activation(
                          out=rows[0:1, TOK + t * 512: TOK + (t + 1) * 512],
                          in_=rowsc2[0:1, 0:512], func=AF.Exp,
                          scale=consts[0:1, 5:6]),
                          waits=[a4, st2["yrows_done"][t]], inc="act")
                st3["rowsc_free"] = a4
                st3["rows2_used"][tp_] = a5
                st2["pd_free"][0] = d1
                st2["pd_free"][1] = d3
                st3["stage"][t] = (d1, a5)

            def p2b_fin(t):
                d1, a5 = st3["stage"][t]
                pb_ = t % 2
                seq = t // TPS
                col0 = (t % TPS) * TILE
                bm = P.op("pe", lambda e, t=t: e.matmul(
                    out=pd[2][:], lhsT=oner[:],
                    rhs=rows[0:1, t * 512:(t + 1) * 512],
                    start=True, stop=True),
                    waits=[d1, st2["pd_free"][2]], inc="pe")
                em = P.op("act", lambda e: e.copy(
                    out=mbc2[:], in_=pd[2][:]),
                    waits=[bm, st3["bc_free"]], inc="act")
                br = P.op("pe", lambda e, t=t: e.matmul(
                    out=pd[3][:], lhsT=oner[:],
                    rhs=rows[0:1, TOK + t * 512: TOK + (t + 1) * 512],
                    start=True, stop=True),
                    waits=[a5, st2["pd_free"][3], em], inc="pe")
                er = P.op("act", lambda e: e.copy(
                    out=rbc2[:], in_=pd[3][:]),
                    waits=[br, st3["bc_free"]], inc="act")
                od_toks = [None] * DC
                tt2 = None
                for ds in range(DC):
                    rcol = ds * TOK + t * 512
                    sl = ds % 2
                    w1 = [em]
                    if ds >= 2:
                        w1.append(od_toks[ds - 2])
                    else:
                        w1.append(st3["o32_free"][sl])
                    tt1 = P.op("pool", lambda e, rcol=rcol, sl=sl: e.tensor_tensor(
                        out=o32[sl][:], in0=res[:, rcol:rcol + 512],
                        in1=mbc2[:], op=ALU.subtract),
                        waits=w1, inc="pool")
                    tt2 = P.op("dve", lambda e, sl=sl: e.tensor_tensor(
                        out=o32[sl][:], in0=o32[sl][:],
                        in1=rbc2[:], op=ALU.mult),
                        waits=[tt1, er], inc="dve")
                    od_toks[ds] = P.dma("pool",
                                        out_fm[seq, ds * 128:(ds + 1) * 128,
                                               col0:col0 + TILE],
                                        o32[sl][:], waits=[tt2],
                                        sem="qout%d" % (ds % 2))
                st2["pd_free"][2] = em
                st2["pd_free"][3] = er
                st3["bc_free"] = tt2
                st3["o32_free"][0] = od_toks[6]
                st3["o32_free"][1] = od_toks[7]

            plan_y(0)
            for t in range(NT):
                yb = t % 2
                yt = y_sb[yb]
                y_ready = st2["y_ready"][t]

                h_toks = []
                for hc in range(HC):
                    if hc == 3 and t >= 1:
                        p2b_stage(t - 1)
                    gb = hc % 2
                    ub = hc % 4
                    ldu = P.dma("sync", wu_buf[:, ub * 1024:(ub + 1) * 1024], wu_sc[hc],
                                waits=[st2["wu_free"][ub]], sem="qwu%d" % ub)
                    g_last = None
                    u_last = None
                    for dci in range(DC):
                        w0 = [y_ready, st2["pg_free"][gb]] if dci == 0 else []
                        if t == 0 and hc == 0:
                            w0 = w0 + [ldw[dci]]
                        g_last = P.op("pe", lambda e, hc=hc, dci=dci, gb=gb, yt=yt: e.matmul(
                            out=pg[gb][:],
                            lhsT=wg_sb[:, dci * HID + hc * 128: dci * HID + (hc + 1) * 128],
                            rhs=yt[:, dci * 512:(dci + 1) * 512],
                            start=(dci == 0), stop=(dci == DC - 1)),
                            waits=w0, inc="pe" if dci == DC - 1 else None)
                    for dci in range(DC):
                        w0 = [ldu, y_ready, st2["pu_free"][gb]] if dci == 0 else []
                        u_last = P.op("pe", lambda e, dci=dci, gb=gb, ub=ub, yt=yt: e.matmul(
                            out=pu[gb][:],
                            lhsT=wu_buf[:, ub * 1024 + dci * 128: ub * 1024 + (dci + 1) * 128],
                            rhs=yt[:, dci * 512:(dci + 1) * 512],
                            start=(dci == 0), stop=(dci == DC - 1)),
                            waits=w0, inc="pe" if dci == DC - 1 else None)
                    st2["wu_free"][ub] = u_last
                    sl = P.op("act", lambda e, gb=gb: e.activation(
                        out=a_sb[:, gb * 512:(gb + 1) * 512], in_=pg[gb][:], func=AF.Silu,
                        scale=consts[:, 1:2]),
                        waits=[g_last, st2["a_free"][gb]], inc="act")
                    st2["pg_free"][gb] = sl
                    ht = P.op("dve", lambda e, hc=hc, gb=gb: e.tensor_tensor(
                        out=h_sb[:, hc * 512:(hc + 1) * 512],
                        in0=a_sb[:, gb * 512:(gb + 1) * 512], in1=pu[gb][:], op=ALU.mult),
                        waits=[sl, u_last, st2["h_free"]], inc="dve")
                    st2["pu_free"][gb] = ht
                    st2["a_free"][gb] = ht
                    h_toks.append(ht)

                if t + 1 < NT:
                    plan_y(t + 1)
                if t >= 1:
                    p2b_fin(t - 1)

                dbank = [pd[0], pd[1], pd[2], pd[3], pg[0], pg[1], pu[0], pu[1]]
                dfree = (st2["pd_free"]
                         + [st2["pg_free"][0], st2["pg_free"][1],
                            st2["pu_free"][0], st2["pu_free"][1]])
                spre_toks = []
                acc = [None] * 8
                for hc in range(HC):
                    wb2 = hc % 4
                    ld = P.dma("sync", wd_buf[:, wb2 * 1024:(wb2 + 1) * 1024], wd_sc[hc],
                               waits=[st2["wd_free"][wb2]], sem="qwd%d" % wb2)
                    mmt = None
                    for dsi in range(8):
                        w0 = [ld, h_toks[hc]]
                        if hc == 0:
                            w0.append(dfree[dsi])
                        need_inc = (dsi == 7) or (hc == HC - 1)
                        mmt = P.op("pe", lambda e, wb2=wb2, dsi=dsi, hc=hc: e.matmul(
                            out=dbank[dsi][:],
                            lhsT=wd_buf[:, wb2 * 1024 + dsi * 128: wb2 * 1024 + (dsi + 1) * 128],
                            rhs=h_sb[:, hc * 512:(hc + 1) * 512],
                            start=(hc == 0), stop=(hc == HC - 1)),
                            waits=w0 if (dsi == 0 or hc == 0) else [],
                            inc="pe" if need_inc else None)
                        acc[dsi] = mmt
                    st2["wd_free"][wb2] = mmt
                for dsi in range(8):
                    rcol = dsi * TOK + t * 512
                    sp = P.op("dve", lambda e, dsi=dsi, rcol=rcol, yt=yt: e.scalar_tensor_tensor(
                        out=res[:, rcol:rcol + 512], in0=dbank[dsi][:], scalar=consts[:, 2:3],
                        in1=yt[:, dsi * 512:(dsi + 1) * 512], op0=ALU.mult, op1=ALU.add),
                        waits=[acc[dsi]], inc="dve")
                    if dsi < 4:
                        st2["pd_free"][dsi] = sp
                    elif dsi < 6:
                        st2["pg_free"][dsi - 4] = sp
                    else:
                        st2["pu_free"][dsi - 6] = sp
                    spre_toks.append(sp)
                st2["y_free"][yb] = spre_toks[-1]
                st2["h_free"] = spre_toks[-1]
                st2["spre"][t] = spre_toks[-1]
                p2b_sq(t, spre_toks)
            p2b_stage(NT - 1)
            p2b_fin(NT - 1)
            P.op("pool", lambda e: e.nop(),
                 waits=[("qout0", P.cnt["qout0"]), ("qout1", P.cnt["qout1"])],
                 inc=None)
            with nc.Block("p2a") as blk:
                P.emit(blk)
        nc.all_engine_barrier()
        if phases < 4:
            return nc

    return nc



_NC_CACHE = {}

def _get_program():
    if "nc" not in _NC_CACHE:
        _NC_CACHE["nc"] = build_core_program()
    return _NC_CACHE["nc"]


def stage_weights(L0_attn_W_qkv, L0_attn_W_out, L0_ffn_w_gate, L0_ffn_w_up,
                  L0_ffn_w_down):
    w_qkv = np.ascontiguousarray(np.asarray(L0_attn_W_qkv, dtype=np.float32))
    w_out = np.asarray(L0_attn_W_out, dtype=np.float32)
    wg = np.asarray(L0_ffn_w_gate, dtype=np.int32)
    wu = np.asarray(L0_ffn_w_up, dtype=np.int32)
    wd = np.asarray(L0_ffn_w_down, dtype=np.int32)
    qk_shuf = np.ascontiguousarray(
        w_qkv[:, 0:2048].reshape(8, 128, 16, 128).transpose(2, 1, 0, 3).reshape(16, 128, 1024))
    wo_shuf = np.ascontiguousarray(
        w_out.reshape(8, 128, 8, 128).transpose(2, 1, 0, 3).reshape(8, 128, 1024))
    wgT = np.ascontiguousarray(wg.T)
    wuT = np.ascontiguousarray(wu.T)
    wu_shuf = np.ascontiguousarray(
        wuT.reshape(8, 128, 24, 128).transpose(2, 1, 0, 3).reshape(24, 128, 1024))
    wdT = np.ascontiguousarray(wd.T)
    return {"w_qkv": w_qkv, "qk_shuf": qk_shuf, "wo_shuf": wo_shuf,
            "wgT": wgT, "wu_shuf": wu_shuf, "wdT": wdT}


def kernel(L0_attn_x, L0_attn_W_qkv, L0_attn_W_out, L0_ffn_w_gate, L0_ffn_w_up,
           L0_ffn_w_down):
    x = np.ascontiguousarray(np.asarray(L0_attn_x, dtype=np.float32))
    wmap = stage_weights(L0_attn_W_qkv, L0_attn_W_out, L0_ffn_w_gate,
                         L0_ffn_w_up, L0_ffn_w_down)

    nc = _get_program()
    in_maps = []
    for c in range(NCORES):
        xc = x[c * SEQ_PER_CORE:(c + 1) * SEQ_PER_CORE]
        x_fm = np.ascontiguousarray(xc.transpose(0, 2, 1))
        m = {"x_fm": x_fm}
        m.update(wmap)
        in_maps.append(m)
    res = run_bass_kernel_spmd(nc, in_maps, core_ids=list(range(NCORES)))
    out = np.empty((B, S, D), dtype=np.float32)
    for c in range(NCORES):
        o = res.results[c]["out_fm"]
        out[c * SEQ_PER_CORE:(c + 1) * SEQ_PER_CORE] = o.transpose(0, 2, 1)
    return out


if __name__ == "__main__":
    nc = build_core_program()
    print("built ok, n_inst =", nc.n_instructions())


# version 30
# speedup vs baseline: 1.0478x; 1.0002x over previous
import numpy as np
import ml_dtypes
import concourse.bass as bass
import concourse.mybir as mybir
from concourse.bass_utils import run_bass_kernel_spmd
from contextlib import ExitStack

F32 = mybir.dt.float32
F16 = mybir.dt.float16
BF16 = mybir.dt.bfloat16
I32 = mybir.dt.int32
AF = mybir.ActivationFunctionType
ALU = mybir.AluOpType

NCORES = 8
B, S, D = 16, 2048, 1024
H, DH, WS = 16, 64, 128
HID = 3072
SEQ_PER_CORE = B // NCORES
TOK = SEQ_PER_CORE * S
TILE = 512
NT = TOK // TILE
TPS = NT // SEQ_PER_CORE
WPT = TILE // WS
DC = D // 128
HC = HID // 128
THETA = 10000.0
ESHIFT = 5.0
EPS = 1e-5


def _host_tables():
    inv_freq = 1.0 / THETA ** (np.arange(0, DH, 2, dtype=np.float64) / DH)
    d_in_head = np.arange(128) % 64
    freq_p = inv_freq[d_in_head % 32]
    pos = np.arange(WS, dtype=np.float64)
    ang = pos[None, :] * freq_p[:, None]
    cos = np.cos(ang)
    sin = np.sin(ang)
    sign = np.where(d_in_head < 32, -1.0, 1.0)[:, None]
    d_in_head2 = d_in_head
    sign = np.where((d_in_head2 % 64) < 32, -1.0, 1.0)[:, None]
    ssin = sin * sign
    cos4 = np.tile(cos, (1, WPT))
    ssin4 = np.tile(ssin, (1, WPT))
    scale = DH ** (-0.5)
    tabs = {
        "cos_q": (cos4 * scale).astype(np.float16),
        "sin_q": (ssin4 * scale).astype(np.float16),
        "cos_k": cos4.astype(np.float16),
        "sin_k": ssin4.astype(np.float16),
    }
    a128 = 128.0 * inv_freq[np.arange(64) % 32]
    c1, s1 = np.cos(a128), np.sin(a128)
    R = np.zeros((64, 64), dtype=np.float64)
    for d in range(64):
        R[d, d] = c1[d]
        if d < 32:
            R[d, d + 32] = -s1[d]
        else:
            R[d, d - 32] = s1[d]
    R2 = np.zeros((128, 128), dtype=np.float64)
    R2[:64, :64] = R
    R2[64:, 64:] = R
    tabs["m128T"] = np.ascontiguousarray(R2.T).astype(np.float16)
    j = np.arange(128)[:, None]
    i = np.arange(128)[None, :]
    half0 = (j >= i).astype(np.float64)
    half1 = (j <= i).astype(np.float64)
    maskA = np.concatenate([half0, half1], axis=1)
    madd = (1.0 - np.tile(maskA, (1, 2))) * (-60000.0)
    madd0 = madd.copy()
    madd0[:, 0:128] = -60000.0
    madd0[:, 256:384] = -60000.0
    tabs["mask_n"] = madd.astype(np.float16)
    tabs["mask_0"] = madd0.astype(np.float16)
    tabs["ident"] = np.eye(128).astype(np.float16)
    tabs["ones_col"] = np.ones((128, 1)).astype(ml_dtypes.bfloat16)
    tabs["ones_col16"] = np.ones((128, 1)).astype(np.float16)
    tabs["ones_row"] = np.ones((1, 128)).astype(ml_dtypes.bfloat16)
    consts = np.zeros((128, 8), dtype=np.float32)
    consts[:, 0] = -ESHIFT
    consts[:, 1] = 1.0 / 127.0
    consts[:, 2] = 1.0 / (127.0 * 127.0)
    consts[:, 3] = 1.0 / float(D)
    consts[:, 4] = EPS
    consts[:, 5] = -0.5
    consts[:, 6] = 1.0
    return tabs, consts


class Prog:
    ENGS = ("pe", "act", "dve", "pool", "sync")

    def __init__(self, nc, ctx):
        self.nc = nc
        self.sems = {}
        self.cnt = {}
        names = ["pe", "act", "dve", "pool",
                 "qx0", "qx1", "qrot", "qrot2", "qwk0", "qwk1", "qwk2", "qwk3",
                 "qwo0", "qwo1", "qwo2", "qwo3",
                 "qwu0", "qwu1", "qwu2", "qwu3", "qwd0", "qwd1", "qwd2", "qwd3", "qtab",
                 "qpl0", "qpl1", "qpl2", "qpl3", "qpl4", "qpl5",
                 "qis0", "qis1", "qis2", "qis3", "qis4", "qis5",
                 "qps0", "qps1", "qps2", "qps3", "qout0", "qout1",
                 "qil0", "qil1", "qil2", "qil3",
                 "qwg0", "qwg1", "qwg2", "qwg3",
                 "qwg4", "qwg5", "qwg6", "qwg7"]
        for name in names:
            self.sems[name] = ctx.enter_context(nc.semaphore("s_" + name))
            self.cnt[name] = 0
        self.ops = {k: [] for k in self.ENGS}

    def op(self, eng, fn, waits=(), inc=None, inc_k=1):
        waits = tuple(w for w in waits if w is not None)
        self.ops[eng].append((fn, waits, inc, inc_k))
        if inc is not None:
            self.cnt[inc] += inc_k
            return (inc, self.cnt[inc])
        return None

    def dma(self, eng, out, in_, waits=(), sem="qtab"):
        return self.op(eng, lambda e, o=out, i=in_: e.dma_start(out=o, in_=i),
                       waits=waits, inc=sem, inc_k=16)

    def emit(self, blk):
        engmap = {"pe": blk.tensor, "act": blk.scalar, "dve": blk.vector,
                  "pool": blk.gpsimd, "sync": blk.sync}
        for name in self.ENGS:
            ops = self.ops[name]
            if not ops:
                continue

            def body(e, ops=ops):
                hw = {}
                for fn, waits, inc, inc_k in ops:
                    need = {}
                    for (sname, val) in waits:
                        if hw.get(sname, 0) < val and need.get(sname, 0) < val:
                            need[sname] = val
                    for sname, val in need.items():
                        e.wait_ge(self.sems[sname], val)
                        hw[sname] = val
                    inst = fn(e)
                    if inc is not None:
                        inst.then_inc(self.sems[inc], inc_k)
            engmap[name](body)
        self.ops = {k: [] for k in self.ENGS}


def build_core_program(phases=4, p1_stage=4, p1_tiles=NT, p1_attn=6):
    nc = bass.Bass("TRN2", target_bir_lowering=False)
    tabs, consts_np = _host_tables()

    x_fm = nc.dram_tensor("x_fm", [SEQ_PER_CORE, D, S], F32, kind="ExternalInput").ap()
    w_qkv = nc.dram_tensor("w_qkv", [D, 3 * D], F32, kind="ExternalInput").ap()
    qk_shuf = nc.dram_tensor("qk_shuf", [16, 128, 1024], F32, kind="ExternalInput").ap()
    wo_shuf = nc.dram_tensor("wo_shuf", [8, 128, 1024], F32, kind="ExternalInput").ap()
    wgT_d = nc.dram_tensor("wgT", [D, HID], I32, kind="ExternalInput").ap()
    wu_shuf = nc.dram_tensor("wu_shuf", [HC, 128, 1024], I32, kind="ExternalInput").ap()
    wdT_d = nc.dram_tensor("wdT", [HID, D], I32, kind="ExternalInput").ap()
    out_fm = nc.dram_tensor("out_fm", [SEQ_PER_CORE, D, S], F32, kind="ExternalOutput").ap()

    wqk_sc = nc.dram_tensor("wqk_sc", [16, 128, 1024], F16, kind="Internal").ap()
    wv_sc = nc.dram_tensor("wv_sc", [8, 128, 1024], F16, kind="Internal").ap()
    wo_sc = nc.dram_tensor("wo_sc", [8, 128, 1024], F16, kind="Internal").ap()
    wg_sc = nc.dram_tensor("wg_sc", [8, 128, HID], BF16, kind="Internal").ap()
    wu_sc = nc.dram_tensor("wu_sc", [HC, 128, 1024], BF16, kind="Internal").ap()
    wd_sc = nc.dram_tensor("wd_sc", [HC, 128, 1024], BF16, kind="Internal").ap()

    cst = {k: nc.inline_tensor(v, "c_" + k).ap() for k, v in tabs.items()}
    cst["consts"] = nc.inline_tensor(consts_np, "c_consts").ap()

    with ExitStack() as top:
        sb = lambda name, shape, dt: top.enter_context(nc.sbuf_tensor(name, shape, dt))
        P = Prog(nc, top)

        res = sb("res", [128, DC * TOK], BF16)
        rows = sb("rows", [1, 2 * TOK], BF16)
        consts = sb("consts", [128, 8], F32)
        onec = sb("onec", [128, 1], BF16)
        onec16 = sb("onec16", [128, 1], F16)
        oner = sb("oner", [1, 128], BF16)

        with ExitStack() as ctx:
            psb = lambda name, shape, dt: ctx.enter_context(nc.sbuf_tensor(name, shape, dt))
            pinf = [psb(f"pinf_{i}", [128, 1024], F32) for i in range(4)]
            pot16 = [psb(f"pot16_{i}", [128, 1024], F16) for i in range(4)]

            P.dma("sync", consts[:], cst["consts"], sem="qtab")
            P.dma("sync", onec[:], cst["ones_col"], sem="qtab")
            P.dma("sync", onec16[:], cst["ones_col16"], sem="qtab")
            P.dma("sync", oner[:], cst["ones_row"], sem="qtab")

            freeL = {i: None for i in range(4)}
            freeS = {i: None for i in range(4)}
            fp16_jobs = []
            for oc in range(16):
                fp16_jobs.append((qk_shuf[oc], wqk_sc[oc]))
            for dc in range(DC):
                fp16_jobs.append((w_qkv[dc * 128:(dc + 1) * 128, 2048:3072], wv_sc[dc]))
            for ds in range(DC):
                fp16_jobs.append((wo_shuf[ds], wo_sc[ds]))
            for j, (srcap, dstap) in enumerate(fp16_jobs):
                bL = j % 4
                eng_c = "act" if j % 2 == 0 else "dve"
                ld = P.dma("sync", pinf[bL][:, 0:1024], srcap, waits=[freeL[bL]],
                           sem="qpl%d" % bL)
                cv = P.op(eng_c, lambda e, bL=bL, eng_c=eng_c: (
                    e.copy if eng_c == "act" else e.tensor_copy)(
                    out=pot16[bL][:], in_=pinf[bL][:]),
                    waits=[ld, freeS[bL]], inc=eng_c)
                freeL[bL] = cv
                freeS[bL] = P.dma("act", dstap, pot16[bL][:], waits=[cv],
                                  sem="qps%d" % bL)
            P.op("sync", lambda e: e.nop(),
                 waits=[("qps0", P.cnt["qps0"]), ("qps1", P.cnt["qps1"]),
                        ("qps2", P.cnt["qps2"]), ("qps3", P.cnt["qps3"]),
                        ("qtab", P.cnt["qtab"])])
            with nc.Block("prep") as blk:
                P.emit(blk)
        nc.all_engine_barrier()
        if phases < 2:
            return nc

        with ExitStack() as ctx:
            psb = lambda name, shape, dt: ctx.enter_context(nc.sbuf_tensor(name, shape, dt))
            pps = lambda name, shape, dt: ctx.enter_context(nc.psum_tensor(name, shape, dt))
            cos_q = psb("cos_q", [128, 512], F16); sin_q = psb("sin_q", [128, 512], F16)
            cos_k = psb("cos_k", [128, 512], F16); sin_k = psb("sin_k", [128, 512], F16)
            m128T = psb("m128T", [128, 128], F16)
            mask_n = psb("mask_n", [128, 512], F16)
            mask_0 = psb("mask_0", [128, 512], F16)
            ident = psb("ident", [128, 128], F16)
            wv_sb = psb("wv_sb", [128, 8 * 1024], F16)
            wqk_buf = psb("wqk_buf", [128, 4 * 1024], F16)
            x32 = [psb(f"x32_{i}", [128, 512], F32) for i in range(2)]
            x16 = [psb(f"x16_{i}", [128, DC * 512], F16) for i in range(2)]
            q_sb = psb("q_sb", [128, DC * 512], F16)
            k_sb = psb("k_sb", [128, DC * 512], F16)
            rotb = [psb(f"rotb{i}", [128, 1024], F16) for i in range(2)]
            q_lo = psb("q_lo", [128, DC * 512], F16)
            q_hi = psb("q_hi", [128, DC * 512], F16)
            k_lo = psb("k_lo", [128, DC * 640], F16)
            vhat = psb("vhat", [128, 5 * 1040], F16)
            pt_sb = [psb(f"pt_sb{i}", [128, 1024], F16) for i in range(2)]
            attn_fm = q_sb
            attn_tm = k_sb
            zsq = q_hi
            rowsc = psb("rowsc", [1, 1024], F32)
            recb = psb("recb", [128, 16], F32)
            pin32 = [psb(f"pin32_{i}", [128, 512], I32) for i in range(4)]
            potbf = [psb(f"potbf_{i}", [128, 512], BF16) for i in range(4)]
            ps_ab = pps("ps_ab", [128, 1024], F32)
            ps_sim = pps("ps_sim", [128, 1024], F32)
            ps_pv = pps("ps_pv", [128, 2048], F32)

            tl = []
            for t_sb, key in ((cos_q, "cos_q"), (sin_q, "sin_q"), (cos_k, "cos_k"),
                              (sin_k, "sin_k"), (m128T, "m128T"), (mask_n, "mask_n"),
                              (mask_0, "mask_0"), (ident, "ident")):
                tl.append(P.dma("sync", t_sb[:], cst[key]))
            for dc in range(DC):
                tl.append(P.dma("sync", wv_sb[:, dc * 1024:(dc + 1) * 1024], wv_sc[dc]))
            tables_tok = tl[-1]
            vz = P.op("pool", lambda e: e.memset(vhat[:], 0.0), inc="pool")
            ones_ready = None
            for sl in range(1, 5):
                ap = vhat.ap()[:, sl * 1040:(sl + 1) * 1040].rearrange(
                    "p (h c) -> p h c", h=16)[:, :, 64:65]
                ones_ready = P.op("pool", lambda e, ap=ap: e.memset(ap, 1.0),
                                  waits=[vz], inc="pool")

            pending_rows = [None]
            int_jobs = []
            for dc in range(DC):
                for p_ in range(6):
                    int_jobs.append((wgT_d[dc * 128:(dc + 1) * 128, p_ * 512:(p_ + 1) * 512],
                                     wg_sc[dc, :, p_ * 512:(p_ + 1) * 512]))
            for hc in range(HC):
                for p_ in range(2):
                    int_jobs.append((wu_shuf[hc, :, p_ * 512:(p_ + 1) * 512],
                                     wu_sc[hc, :, p_ * 512:(p_ + 1) * 512]))
            for hc in range(HC):
                for p_ in range(2):
                    int_jobs.append((wdT_d[hc * 128:(hc + 1) * 128, p_ * 512:(p_ + 1) * 512],
                                     wd_sc[hc, :, p_ * 512:(p_ + 1) * 512]))
            NJOB = len(int_jobs)
            ip = {"ld": 0, "cv": 0, "stq": 0, "xt_allow": 0,
                  "ld_tok": [None] * NJOB, "cv_tok": [None] * NJOB,
                  "st_tok": [None] * NJOB}

            xp = {"t": 1, "ld": 0, "cv": 0, "ld_tok": [None] * DC,
                  "cv_tok": [None] * DC}

            def x_step():
                if xp["t"] >= NT or xp["t"] > ip["xt_allow"]:
                    return
                t_ = xp["t"]
                xb = t_ % 2
                x16t = x16[xb]
                prev_ld, prev_cv = xp["ld"], xp["cv"]
                while xp["cv"] < prev_ld:
                    j = xp["cv"]
                    cv = P.op("act", lambda e, j=j, x16t=x16t: e.copy(
                        out=x16t[:, j * 512:(j + 1) * 512], in_=x32[j % 2][:]),
                        waits=[xp["ld_tok"][j],
                               st["x16_free"][xb] if j == 0 else None], inc="act")
                    st["x32_free"][j % 2] = cv
                    xp["cv_tok"][j] = cv
                    xp["cv"] = j + 1
                n = 0
                while n < 2 and xp["ld"] < DC and xp["ld"] < prev_cv + 2:
                    j = xp["ld"]
                    src = x_fm[t_ // TPS, j * 128:(j + 1) * 128,
                               (t_ % TPS) * TILE:(t_ % TPS) * TILE + TILE]
                    xp["ld_tok"][j] = P.dma("pool", x32[j % 2][:], src,
                                            waits=[st["x32_free"][j % 2]],
                                            sem="qx%d" % (j % 2))
                    xp["ld"] = j + 1
                    n += 1
                if xp["cv"] == DC:
                    st["x_ready"][t_] = xp["cv_tok"][DC - 1]
                    xp["t"] += 1
                    xp["ld"] = 0
                    xp["cv"] = 0

            def bg_prep():
                x_step()
                prev_ld, prev_cv, prev_st = ip["ld"], ip["cv"], ip["stq"]
                while ip["stq"] < prev_cv:
                    j = ip["stq"]
                    ip["st_tok"][j] = P.dma("act", int_jobs[j][1], potbf[j % 4][:],
                                            waits=[ip["cv_tok"][j]], sem="qis%d" % (j % 4))
                    ip["stq"] = j + 1
                while ip["cv"] < min(prev_ld, prev_st + 4):
                    j = ip["cv"]
                    ip["cv_tok"][j] = P.op("act", lambda e, b_=j % 4: e.copy(
                        out=potbf[b_][:], in_=pin32[b_][:]),
                        waits=[ip["ld_tok"][j],
                               ip["st_tok"][j - 4] if j >= 4 else None], inc="act")
                    ip["cv"] = j + 1
                n_new = 0
                while n_new < 2 and ip["ld"] < min(NJOB, prev_cv + 4):
                    j = ip["ld"]
                    ip["ld_tok"][j] = P.dma("act", pin32[j % 4][:], int_jobs[j][0],
                                            waits=[ip["cv_tok"][j - 4] if j >= 4 else None],
                                            sem="qil%d" % (j % 4))
                    ip["ld"] = j + 1
                    n_new += 1

            st = {
                "x_ready": [None] * NT,
                "x32_free": [None, None],
                "x16_free": [None, None],
                "rope_q_done": None, "tcs_k_prev": None,
                "rot_free": [None, None], "qhi_evac": None,
                "qk_used": None, "ring_copied": None,
                "psA_free": None, "psB_free": None,
                "psS0_free": None, "psS1_free": None, "psPV_free": None,
                "wqk_free": [None] * 4,
                "attn_tm_free": None, "attn_fm_used": None, "psPV_ev": None,
                "zsq_free": [None, None], "pt_free": [None, None], "pt_m_used": None,
                "pend_pv": None,
                "recb_free": None,
                "stats_free": None, "rows_used": None,
            }
            zrows_toks = []

            for t in range(min(NT, p1_tiles)):
                seq = t // TPS
                col0 = (t % TPS) * TILE
                first_tile_of_seq = (col0 == 0)
                xb = t % 2
                x16t = x16[xb]

                if t == 0:
                    x_cast_toks = []
                    for dc in range(DC):
                        b = dc % 2
                        src = x_fm[seq, dc * 128:(dc + 1) * 128, col0:col0 + TILE]
                        ld = P.dma("pool", x32[b][:], src, waits=[st["x32_free"][b]],
                                   sem="qx%d" % b)
                        cv = P.op("act", lambda e, b=b, dc=dc, x16t=x16t: e.copy(
                            out=x16t[:, dc * 512:(dc + 1) * 512], in_=x32[b][:]),
                            waits=[ld, st["x16_free"][xb]], inc="act")
                        st["x32_free"][b] = cv
                        x_cast_toks.append(cv)
                    st["x_ready"][0] = x_cast_toks[-1]
                ip["xt_allow"] = t + 1
                x_ready = st["x_ready"][t]
                if pending_rows[0] is not None:
                    pending_rows[0]()
                    pending_rows[0] = None
                bg_prep()

                qk_ev = {}
                slots = [ps_ab[:, 0:512], ps_ab[:, 512:1024],
                         ps_sim[:, 0:512], ps_sim[:, 512:1024]]
                skey = ["psA_free", "psB_free", "psS0_free", "psS1_free"]
                def emit_v(vb, x16t=x16t, x_ready=x_ready):
                    w_ = vb // 2
                    half = vb % 2
                    sl = w_ + 1
                    mmtok = None
                    for dci in range(DC):
                        w = ([x_ready, tables_tok, ones_ready, st["psS1_free"]]
                             if dci == 0 else [])
                        mmtok = P.op("pe", lambda e, w_=w_, half=half, dci=dci,
                                     x16t=x16t: e.matmul(
                            out=ps_sim[:, 512:1024],
                            lhsT=x16t[:, dci * 512 + w_ * 128: dci * 512 + w_ * 128 + 128],
                            rhs=wv_sb[:, dci * 1024 + half * 512: dci * 1024 + (half + 1) * 512],
                            start=(dci == 0), stop=(dci == DC - 1)),
                            waits=w, inc="pe" if dci == DC - 1 else None)
                    hbase = half * 8
                    dst = vhat.ap()[:, sl * 1040 + hbase * 65: sl * 1040 + (hbase + 8) * 65]
                    dst = dst.rearrange("p (h c) -> p h c", h=8)[:, :, 0:64]
                    srcv = ps_sim.ap()[:, 512:1024].rearrange("p (h c) -> p h c", h=8)
                    ev = P.op("act", lambda e, dst=dst, srcv=srcv: e.copy(
                        out=dst, in_=srcv), waits=[mmtok, st["ring_copied"]],
                        inc="act")
                    st["psS1_free"] = ev
                    st["v_ev_last"] = ev

                for idx, oc in enumerate([8, 9, 10, 11, 12, 13, 14, 15,
                                          0, 1, 2, 3, 4, 5, 6, 7]):
                    if idx in (4, 8, 12):
                        bg_prep()
                    b = idx % 4
                    s4 = idx % 3
                    ld = P.dma("sync", wqk_buf[:, b * 1024:(b + 1) * 1024], wqk_sc[oc],
                               waits=[tables_tok, st["wqk_free"][b]], sem="qwk%d" % b)
                    ps = slots[s4]
                    ps_free = st[skey[s4]]
                    mmtok = None
                    for dci in range(DC):
                        w = [ld, x_ready, ps_free] if dci == 0 else []
                        mmtok = P.op("pe", lambda e, b=b, dci=dci, ps=ps, x16t=x16t: e.matmul(
                            out=ps,
                            lhsT=wqk_buf[:, b * 1024 + dci * 128: b * 1024 + (dci + 1) * 128],
                            rhs=x16t[:, dci * 512:(dci + 1) * 512],
                            start=(dci == 0), stop=(dci == DC - 1)),
                            waits=w, inc="pe" if dci == DC - 1 else None)
                    st["wqk_free"][b] = mmtok
                    dest = q_sb if oc < 8 else k_sb
                    ci = oc % 8
                    wv_ = [mmtok]
                    if oc < 8:
                        wv_.append(st["rope_q_done"])
                    else:
                        if ci < 2:
                            wv_.append(st["attn_tm_free"])
                        if st["tcs_k_prev"] is not None:
                            wv_.append(st["tcs_k_prev"][ci // 4])
                    ev = P.op("act", lambda e, dest=dest, ci=ci, ps=ps: e.copy(
                        out=dest[:, ci * 512:(ci + 1) * 512], in_=ps),
                        waits=wv_, inc="act")
                    qk_ev[oc] = ev
                    st[skey[s4]] = ev
                    if idx % 2 == 1:
                        emit_v(idx // 2)

                if p1_stage < 2:
                    continue
                v_ready = st["v_ev_last"]

                x3 = lambda t_: t_.ap().rearrange("p (c n) -> p c n", c=DC)
                bc2 = lambda tab: tab.ap().unsqueeze(1).broadcast_to([128, 2, 512])
                bc4_ = lambda tab: tab.ap().unsqueeze(1).broadcast_to([128, 4, 512])

                def rope_pass2(src_sb, scratch, dst_qtr, cos_t, sin_t, ev_q,
                               w_scr0, w_dst):
                    tbs = [None] * 4
                    tcs = [None, None]
                    for u in range(4):
                        h = u // 2
                        bq = u % 2
                        c0 = u * 1024
                        sl = src_sb.ap()[:, c0:c0 + 1024].rearrange(
                            "p (c n) -> p c n", c=2)
                        scl = scratch.ap()[:, c0:c0 + 1024].rearrange(
                            "p (c n) -> p c n", c=2)
                        wsc = list(w_scr0[0]) if u == 0 else (
                            list(w_scr0[1]) if u == 2 else [])
                        ta = P.op("dve", lambda e, sl=sl, scl=scl, cos_t=cos_t:
                                  e.tensor_tensor(out=scl, in0=sl, in1=bc2(cos_t),
                                                  op=ALU.mult),
                                  waits=[ev_q[u]] + wsc, inc="dve")
                        rdw = [ev_q[u], st["rot_free"][bq]]
                        toks = []
                        for g in (0, 64):
                            toks.append(P.dma("pool", rotb[bq][g:g + 32, :],
                                              src_sb[g + 32:g + 64, c0:c0 + 1024],
                                              waits=rdw,
                                              sem="qrot" if bq == 0 else "qrot2"))
                            rdw = []
                            toks.append(P.dma("pool", rotb[bq][g + 32:g + 64, :],
                                              src_sb[g:g + 32, c0:c0 + 1024],
                                              sem="qrot" if bq == 0 else "qrot2"))
                        tbs[u] = P.op("dve", lambda e, u=u, bq=bq, c0=c0,
                                      sin_t=sin_t: e.tensor_tensor(
                            out=src_sb.ap()[:, c0:c0 + 1024].rearrange(
                                "p (c n) -> p c n", c=2),
                            in0=rotb[bq].ap().rearrange("p (c n) -> p c n", c=2),
                            in1=bc2(sin_t), op=ALU.mult),
                            waits=[toks[-1], ta], inc="dve")
                        st["rot_free"][bq] = tbs[u]
                        tc = P.op("dve", lambda e, sl=sl, scl=scl, u=u,
                                  dst_qtr=dst_qtr: e.tensor_tensor(
                            out=dst_qtr[u], in0=scl, in1=sl, op=ALU.add),
                            waits=[tbs[u]] + (w_dst if u == 0 else []), inc="dve")
                        tcs[h] = tc
                    return tcs

                k3 = k_lo.ap().rearrange("p (c n) -> p c n", c=DC)
                tcs_k = rope_pass2(k_sb, q_hi,
                                   [k3[:, 0:2, 128:640], k3[:, 2:4, 128:640],
                                    k3[:, 4:6, 128:640], k3[:, 6:8, 128:640]],
                                   cos_k, sin_k,
                                   [qk_ev[9], qk_ev[11], qk_ev[13], qk_ev[15]],
                                   [[st["qk_used"], st["qhi_evac"],
                                     st["zsq_free"][0], st["zsq_free"][1]],
                                    [st["qhi_evac"]]],
                                   [st["qk_used"], st["ring_copied"]])
                st["tcs_k_prev"] = tcs_k
                tcs_q = rope_pass2(q_sb, q_hi,
                                   [x3(q_lo)[:, 0:2, :], x3(q_lo)[:, 2:4, :],
                                    x3(q_lo)[:, 4:6, :], x3(q_lo)[:, 6:8, :]],
                                   cos_q, sin_q,
                                   [qk_ev[1], qk_ev[3], qk_ev[5], qk_ev[7]],
                                   [[tcs_k[0]], [tcs_k[1]]],
                                   [st["qk_used"]])
                qhi_ev_h = [None, None]
                for dci in range(DC):
                    ps = slots[dci % 3]
                    pf = st[skey[dci % 3]]
                    mm = P.op("pe", lambda e, dci=dci, ps=ps: e.matmul(
                        out=ps, lhsT=m128T[:], rhs=q_lo[:, dci * 512:(dci + 1) * 512],
                        start=True, stop=True), waits=[tcs_q[dci // 4], pf], inc="pe")
                    ev = P.op("act", lambda e, dci=dci, ps=ps: e.copy(
                        out=q_hi[:, dci * 512:(dci + 1) * 512], in_=ps),
                        waits=[mm, tcs_q[dci // 4]], inc="act")
                    st[skey[dci % 3]] = ev
                    qhi_ev_h[dci // 4] = ev
                st["qhi_evac"] = qhi_ev_h[1]
                bg_prep()

                if p1_stage < 3:
                    continue
                if first_tile_of_seq:
                    kprev = k_lo.ap().rearrange("p (c n) -> p c n", c=DC)[:, :, 0:128]
                    z1_ = P.op("pool", lambda e, kprev=kprev: e.memset(kprev, 0.0),
                               waits=[st["qk_used"], st["ring_copied"]], inc="pool")
                    seq_zero = P.op("pool", lambda e: e.memset(vhat[:, 0:1040], 0.0),
                                    waits=[z1_, st["pt_m_used"]], inc="pool")
                else:
                    seq_zero = st["ring_copied"]

                sim_last_all = None
                pend_tail = [None]
                for w_ in range(WPT):
                    wseq0 = first_tile_of_seq and w_ == 0
                    mtile = mask_0 if wseq0 else mask_n
                    pv_last = None
                    for g in range(4):
                        bankT = ps_sim if g % 2 == 0 else ps_ab
                        pt = pt_sb[g % 2]
                        if g % 2 == 0:
                            bfree = [st["psS0_free"], st["psS1_free"]]
                        else:
                            bfree = [st["psA_free"], st["psB_free"]]
                        first_w = [tcs_k[g // 2], qhi_ev_h[g // 2], v_ready,
                                   seq_zero, st["attn_tm_free"]] + bfree
                        for pp in range(2):
                            hp = 2 * g + pp
                            for sub in range(2):
                                pb = 64 * sub
                                base_i = hp * 512 + w_ * 128
                                klo_c = hp * 640
                                outc = sub * 512 + pp * 256
                                st_flag = (pp == 0)
                                w0 = first_w if (pp == 0 and sub == 0) else []
                                P.op("pe", lambda e, pb=pb, klo_c=klo_c, w_=w_,
                                     base_i=base_i, outc=outc, st_flag=st_flag,
                                     bankT=bankT: e.matmul(
                                    out=bankT[:, outc:outc + 128],
                                    lhsT=k_lo[pb:pb + 64, klo_c + 128 * w_: klo_c + 128 * w_ + 128],
                                    rhs=q_hi[pb:pb + 64, base_i:base_i + 128],
                                    start=st_flag, stop=False), waits=w0)
                                P.op("pe", lambda e, pb=pb, klo_c=klo_c, w_=w_,
                                     base_i=base_i, outc=outc, bankT=bankT: e.matmul(
                                    out=bankT[:, outc + 128:outc + 256],
                                    lhsT=k_lo[pb:pb + 64, klo_c + 128 + 128 * w_: klo_c + 256 + 128 * w_],
                                    rhs=q_lo[pb:pb + 64, base_i:base_i + 128],
                                    start=False, stop=False))
                        mk0 = P.op("pe", lambda e, bankT=bankT, mtile=mtile: e.matmul(
                            out=bankT[:, 0:512], lhsT=ident[:], rhs=mtile[:],
                            start=False, stop=True), inc="pe")
                        mk1 = P.op("pe", lambda e, bankT=bankT, mtile=mtile: e.matmul(
                            out=bankT[:, 512:1024], lhsT=ident[:], rhs=mtile[:],
                            start=False, stop=True), inc="pe")
                        sim_last_all = mk1
                        ex = P.op("act", lambda e, bankT=bankT, pt=pt: e.activation(
                            out=pt[:], in_=bankT[:], func=AF.Exp,
                            bias=consts[:, 0:1]),
                            waits=[mk0, mk1, st["pt_free"][g % 2]], inc="act")
                        if g % 2 == 0:
                            st["psS0_free"] = ex
                            st["psS1_free"] = ex
                        else:
                            st["psA_free"] = ex
                            st["psB_free"] = ex

                        def plan_pv(g=g, w_=w_, ex=ex, pt=pt):
                            pvl = None
                            for hh in range(4):
                                h = 4 * g + hh
                                sub = hh % 2
                                pp = hh // 2
                                ptc = sub * 512 + pp * 256
                                off = h * 128
                                w0 = ([ex, v_ready,
                                       st["psPV_free"] if g == 0 else st["psPV_ev"]]
                                      if hh == 0 else [])
                                P.op("pe", lambda e, h=h, w_=w_, off=off, ptc=ptc,
                                     pt=pt: e.matmul(
                                    out=ps_pv[:, off:off + 65],
                                    lhsT=pt[:, ptc: ptc + 128],
                                    rhs=vhat[:, w_ * 1040 + h * 65: w_ * 1040 + (h + 1) * 65],
                                    start=True, stop=False), waits=w0)
                                pvl = P.op("pe", lambda e, h=h, w_=w_, off=off, ptc=ptc,
                                           pt=pt: e.matmul(
                                    out=ps_pv[:, off:off + 65],
                                    lhsT=pt[:, ptc + 128: ptc + 256],
                                    rhs=vhat[:, (w_ + 1) * 1040 + h * 65: (w_ + 1) * 1040 + (h + 1) * 65],
                                    start=False, stop=True),
                                    inc="pe" if hh == 3 else None)
                            st["pt_free"][g % 2] = pvl
                            return pvl
                        if g == 1 and pend_tail[0] is not None:
                            pend_tail[0]()
                            pend_tail[0] = None
                        if st["pend_pv"] is not None:
                            pv_last = st["pend_pv"]()
                        st["pend_pv"] = plan_pv
                    pv_last = st["pend_pv"]()
                    st["pend_pv"] = None
                    st["pt_m_used"] = pv_last

                    def plan_tail(w_=w_, pv_last=pv_last):
                        pv3 = ps_pv.ap().rearrange("p (h c) -> p h c", h=16)
                        rc = P.op("dve", lambda e, pv3=pv3: e.reciprocal(
                            out=recb.ap().unsqueeze(2), in_=pv3[:, :, 64:65]),
                            waits=[pv_last, st["recb_free"]], inc="dve")
                        at3 = attn_tm.ap()[:, 0:1024].rearrange("p (h c) -> p h c", h=16)
                        ev = P.op("dve", lambda e, pv3=pv3, at3=at3: e.tensor_tensor(
                            out=at3, in0=pv3[:, :, 0:64],
                            in1=recb.ap().unsqueeze(2).broadcast_to([128, 16, 64]),
                            op=ALU.mult), waits=[rc, st["attn_tm_free"]], inc="dve")
                        st["recb_free"] = ev
                        st["psPV_ev"] = ev
                        s0v = ps_pv.ap().bitcast(F16)[:, 0:1024]
                        trs = []
                        for dci in range(DC):
                            trs.append(P.op("pe", lambda e, dci=dci, s0v=s0v: e.transpose(
                                out=s0v[:, dci * 128:(dci + 1) * 128],
                                in_=attn_tm[:, dci * 128:(dci + 1) * 128], identity=ident[:]),
                                waits=[ev] if dci == 0 else [],
                                inc="pe" if dci == DC - 1 else None))
                        cp1 = P.op("dve", lambda e, s0v=s0v, w_=w_: e.tensor_copy(
                            out=attn_fm.ap().rearrange("p (c n) -> p c n", c=DC)[:, :, w_ * 128:(w_ + 1) * 128],
                            in_=s0v.rearrange("p (c n) -> p c n", c=DC)),
                            waits=[trs[-1], st["attn_fm_used"]], inc="dve")
                        st["psPV_free"] = cp1
                        st["attn_tm_free"] = trs[-1]
                        return cp1
                    pend_tail[0] = plan_tail
                    bg_prep()
                cp1 = pend_tail[0]()
                pend_tail[0] = None
                bg_prep()
                st["qk_used"] = sim_last_all

                ksrc = k_lo.ap().rearrange("p (c n) -> p c n", c=DC)[:, :, 512:640]
                kdst = k_lo.ap().rearrange("p (c n) -> p c n", c=DC)[:, :, 0:128]
                rc1 = P.op("pool", lambda e, ksrc=ksrc, kdst=kdst: e.tensor_copy(
                    out=kdst, in_=ksrc), waits=[st["qk_used"]], inc="pool")
                rc2 = P.op("pool", lambda e: e.tensor_copy(
                    out=vhat[:, 0:1040], in_=vhat[:, 4 * 1040:5 * 1040]),
                    waits=[rc1, st["pt_m_used"]], inc="pool")
                st["ring_copied"] = rc2

                if p1_stage < 4:
                    continue
                z1_toks = []
                last_opj_mm = [None]
                for ds in range(DC):
                    ps = ps_ab[:, 0:512] if ds % 2 == 0 else ps_ab[:, 512:1024]
                    pf = st["psA_free"] if ds % 2 == 0 else st["psB_free"]
                    wb = ds % 4
                    ldo = P.dma("sync", wqk_buf[:, wb * 1024:(wb + 1) * 1024], wo_sc[ds],
                                waits=[st["wqk_free"][wb]], sem="qwo%d" % wb)
                    mmtok = None
                    for hoc in range(DC):
                        w0 = [cp1, pf, ldo] if hoc == 0 else []
                        mmtok = P.op("pe", lambda e, hoc=hoc, ps=ps, wb=wb: e.matmul(
                            out=ps,
                            lhsT=wqk_buf[:, wb * 1024 + hoc * 128: wb * 1024 + (hoc + 1) * 128],
                            rhs=attn_fm[:, hoc * 512:(hoc + 1) * 512],
                            start=(hoc == 0), stop=(hoc == DC - 1)),
                            waits=w0, inc="pe" if hoc == DC - 1 else None)
                    st["wqk_free"][wb] = mmtok
                    last_opj_mm[0] = mmtok
                    rcol = ds * TOK + t * 512
                    zt = P.op("dve", lambda e, ds=ds, ps=ps, rcol=rcol, x16t=x16t: e.tensor_tensor(
                        out=res[:, rcol:rcol + 512], in0=ps,
                        in1=x16t[:, ds * 512:(ds + 1) * 512], op=ALU.add),
                        waits=[mmtok], inc="dve")
                    z1_toks.append(zt)
                    if ds % 2 == 0:
                        st["psA_free"] = zt
                    else:
                        st["psB_free"] = zt
                st["x16_free"][xb] = z1_toks[-1]
                st["attn_fm_used"] = last_opj_mm[0]
                st["rope_q_done"] = last_opj_mm[0]

                sum_last = None
                sq_last = None
                for ds in range(DC):
                    rcol = ds * TOK + t * 512
                    zb = ds % 2
                    sq = P.op("dve", lambda e, rcol=rcol, zb=zb: e.tensor_tensor(
                        out=zsq[:, zb * 512:(zb + 1) * 512], in0=res[:, rcol:rcol + 512],
                        in1=res[:, rcol:rcol + 512], op=ALU.mult),
                        waits=[z1_toks[ds], st["zsq_free"][zb], st["qk_used"],
                               st["rows_used"] if ds == 0 else None], inc="dve")
                    w0 = [z1_toks[ds], st["psA_free"] if ds == 0 else None]
                    sum_last = P.op("pe", lambda e, rcol=rcol, ds=ds: e.matmul(
                        out=ps_ab[0:1, 0:512], lhsT=onec[:], rhs=res[:, rcol:rcol + 512],
                        start=(ds == 0), stop=(ds == DC - 1)),
                        waits=w0, inc="pe")
                    w1 = [sq, st["psB_free"] if ds == 0 else None]
                    sq_last = P.op("pe", lambda e, zb=zb, ds=ds: e.matmul(
                        out=ps_ab[0:1, 512:1024], lhsT=onec16[:],
                        rhs=zsq[:, zb * 512:(zb + 1) * 512],
                        start=(ds == 0), stop=(ds == DC - 1)),
                        waits=w1, inc="pe")
                    st["zsq_free"][zb] = sq_last
                def mk_rows(t=t, sum_last=sum_last, sq_last=sq_last):
                    d1 = P.op("dve", lambda e, t=t: e.tensor_scalar(
                              out=rows[0:1, t * 512:(t + 1) * 512], in0=ps_ab[0:1, 0:512],
                              scalar1=consts[0:1, 3:4], scalar2=None, op0=ALU.mult),
                              waits=[sum_last, st["rows_used"]], inc="dve")
                    d2 = P.op("dve", lambda e, t=t: e.tensor_tensor(
                              out=rowsc[0:1, 0:512],
                              in0=rows[0:1, t * 512:(t + 1) * 512],
                              in1=rows[0:1, t * 512:(t + 1) * 512], op=ALU.mult),
                              waits=[d1], inc="dve")
                    d3 = P.op("dve", lambda e: e.scalar_tensor_tensor(
                              out=rowsc[0:1, 512:1024], in0=ps_ab[0:1, 512:1024],
                              scalar=consts[0:1, 3:4], in1=rowsc[0:1, 0:512],
                              op0=ALU.mult, op1=ALU.subtract),
                              waits=[sq_last, d2], inc="dve")
                    a4 = P.op("act", lambda e: e.activation(out=rowsc[0:1, 0:512],
                              in_=rowsc[0:1, 512:1024], func=AF.Ln, bias=consts[0:1, 4:5]),
                              waits=[d3, st["rows_used"]], inc="act")
                    a5 = P.op("act", lambda e, t=t: e.activation(
                              out=rows[0:1, TOK + t * 512: TOK + (t + 1) * 512],
                              in_=rowsc[0:1, 0:512], func=AF.Exp,
                              scale=consts[0:1, 5:6]), waits=[a4], inc="act")
                    zrows_toks.append(a5)
                    st["psA_free"] = d1
                    st["psB_free"] = d3
                    st["rows_used"] = a5
                st["psA_free"] = sum_last
                st["psB_free"] = sq_last
                bg_prep()
                if t == NT - 1:
                    mk_rows()
                else:
                    pending_rows[0] = mk_rows

            while ip["stq"] < NJOB:
                bg_prep()
            P.op("sync", lambda e: e.nop(),
                 waits=[("qis0", P.cnt["qis0"]), ("qis1", P.cnt["qis1"]),
                        ("qis2", P.cnt["qis2"]), ("qis3", P.cnt["qis3"])])
            p1_z1 = list(zrows_toks)
            with nc.Block("p1") as blk:
                P.emit(blk)
        nc.all_engine_barrier()
        if phases < 3:
            return nc

        with ExitStack() as ctx:
            psb = lambda name, shape, dt: ctx.enter_context(nc.sbuf_tensor(name, shape, dt))
            pps = lambda name, shape, dt: ctx.enter_context(nc.psum_tensor(name, shape, dt))
            wg_sb = psb("wg_sb", [128, DC * HID], BF16)
            wu_buf = psb("wu_buf", [128, 4 * 1024], BF16)
            wd_buf = psb("wd_buf", [128, 4 * 1024], BF16)
            y_sb = [psb(f"y_sb{i}", [128, DC * 512], BF16) for i in range(2)]
            y2 = psb("y2", [128, 512], BF16)
            a_sb = psb("a_sb", [128, 2 * 512], BF16)
            h_sb = psb("h_sb", [128, HC * 512], BF16)
            mbc = psb("mbc", [128, 512], BF16)
            rbc = psb("rbc", [128, 512], BF16)
            ssq = psb("ssq", [128, 4096], BF16)
            rowsc2 = psb("rowsc2", [1, 1024], F32)
            mbc2 = psb("mbc2", [128, 512], BF16)
            rbc2 = psb("rbc2", [128, 512], BF16)
            o32 = [psb(f"o32_{i}", [128, 512], F32) for i in range(2)]
            pg = [pps("pg0", [128, 512], F32), pps("pg1", [128, 512], F32)]
            pu = [pps("pu0", [128, 512], F32), pps("pu1", [128, 512], F32)]
            pd = [pps(f"pd{i}", [128, 512], F32) for i in range(4)]

            ldw = []
            for dc in range(DC):
                ldw.append(P.dma("sync", wg_sb[:, dc * HID:(dc + 1) * HID], wg_sc[dc],
                                 sem="qwg%d" % dc))

            st2 = {
                "pg_free": [None, None], "pu_free": [None, None],
                "pd_free": [None] * 4, "a_free": [None, None],
                "h_free": None, "y_free": [None, None], "bc_free": None,
                "y_ready": [None] * (NT + 1),
                "wd_free": [None] * 4, "wu_free": [None] * 4, "spre": [None] * NT,
                "yrows_done": [None] * (NT + 1),
            }
            st3 = {"ssq_free": None, "rows2_used": [None, None],
                   "bc_free": None, "rowsc_free": None, "stage": [None] * NT,
                   "sq": [None] * NT, "o32_free": [None, None]}

            def plan_y(t):
                yb = t % 2
                bm = P.op("pe", lambda e, t=t: e.matmul(
                    out=pg[0][:], lhsT=oner[:], rhs=rows[0:1, t * 512:(t + 1) * 512],
                    start=True, stop=True),
                    waits=[st2["pg_free"][0], st2["bc_free"]], inc="pe")
                em = P.op("act", lambda e: e.copy(out=mbc[:], in_=pd[0][:] if False else pg[0][:]),
                          waits=[bm], inc="act")
                br = P.op("pe", lambda e, t=t: e.matmul(
                    out=pg[1][:], lhsT=oner[:], rhs=rows[0:1, TOK + t * 512: TOK + (t + 1) * 512],
                    start=True, stop=True), waits=[st2["pg_free"][1], em], inc="pe")
                er = P.op("act", lambda e: e.copy(out=rbc[:], in_=pg[1][:]),
                          waits=[br], inc="act")
                st2["pg_free"][0] = em
                st2["pg_free"][1] = er
                st2["yrows_done"][t] = er
                prev_ty2 = None
                for dc in range(DC):
                    rcol = dc * TOK + t * 512
                    w0 = [em, st2["y_free"][yb], prev_ty2]
                    ty1 = P.op("dve", lambda e, rcol=rcol: e.tensor_tensor(
                        out=y2[:], in0=res[:, rcol:rcol + 512], in1=mbc[:], op=ALU.subtract),
                        waits=w0, inc="dve")
                    ty2 = P.op("dve", lambda e, dc=dc, yb=yb: e.tensor_tensor(
                        out=y_sb[yb][:, dc * 512:(dc + 1) * 512], in0=y2[:], in1=rbc[:],
                        op=ALU.mult), waits=[ty1, er], inc="dve")
                    prev_ty2 = ty2
                st2["bc_free"] = prev_ty2
                st2["y_ready"][t] = prev_ty2

            def p2b_sq(t, spre_toks):
                sq_tts = []
                for ds in range(DC):
                    rcol = ds * TOK + t * 512
                    sq = P.op("dve", lambda e, rcol=rcol, ds=ds: e.tensor_tensor(
                        out=ssq[:, ds * 512:(ds + 1) * 512],
                        in0=res[:, rcol:rcol + 512],
                        in1=res[:, rcol:rcol + 512], op=ALU.mult),
                        waits=[spre_toks[ds], st3["ssq_free"] if ds == 0 else None],
                        inc="dve")
                    sq_tts.append(sq)
                st3["sq"][t] = (sq_tts, spre_toks)

            def p2b_stage(t):
                tp_ = t % 2
                sq_tts, spre_toks = st3["sq"][t]
                sum_last = None
                sq_last = None
                for ds in range(DC):
                    rcol = ds * TOK + t * 512
                    sum_last = P.op("pe", lambda e, rcol=rcol, ds=ds: e.matmul(
                        out=pd[0][0:1, :], lhsT=onec[:], rhs=res[:, rcol:rcol + 512],
                        start=(ds == 0), stop=(ds == DC - 1)),
                        waits=[spre_toks[ds]], inc="pe" if ds == DC - 1 else None)
                    sq_last = P.op("pe", lambda e, ds=ds: e.matmul(
                        out=pd[1][0:1, :], lhsT=onec[:],
                        rhs=ssq[:, ds * 512:(ds + 1) * 512],
                        start=(ds == 0), stop=(ds == DC - 1)),
                        waits=[sq_tts[ds]], inc="pe")
                st3["ssq_free"] = sq_last
                d1 = P.op("dve", lambda e, t=t: e.tensor_scalar(
                          out=rows[0:1, t * 512:(t + 1) * 512], in0=pd[0][0:1, :],
                          scalar1=consts[0:1, 3:4], scalar2=None, op0=ALU.mult),
                          waits=[sum_last, st3["rows2_used"][tp_],
                                 st2["yrows_done"][t]], inc="dve")
                d2 = P.op("dve", lambda e, t=t: e.tensor_tensor(
                          out=rowsc2[0:1, 0:512],
                          in0=rows[0:1, t * 512:(t + 1) * 512],
                          in1=rows[0:1, t * 512:(t + 1) * 512], op=ALU.mult),
                          waits=[d1, st3["rowsc_free"]], inc="dve")
                d3 = P.op("dve", lambda e: e.scalar_tensor_tensor(
                          out=rowsc2[0:1, 512:1024], in0=pd[1][0:1, :],
                          scalar=consts[0:1, 3:4], in1=rowsc2[0:1, 0:512],
                          op0=ALU.mult, op1=ALU.subtract),
                          waits=[sq_last, d2], inc="dve")
                a4 = P.op("act", lambda e: e.activation(
                          out=rowsc2[0:1, 0:512],
                          in_=rowsc2[0:1, 512:1024], func=AF.Ln,
                          bias=consts[0:1, 4:5]),
                          waits=[d3, st3["rows2_used"][tp_]], inc="act")
                a5 = P.op("act", lambda e, t=t: e.activation(
                          out=rows[0:1, TOK + t * 512: TOK + (t + 1) * 512],
                          in_=rowsc2[0:1, 0:512], func=AF.Exp,
                          scale=consts[0:1, 5:6]),
                          waits=[a4, st2["yrows_done"][t]], inc="act")
                st3["rowsc_free"] = a4
                st3["rows2_used"][tp_] = a5
                st2["pd_free"][0] = d1
                st2["pd_free"][1] = d3
                st3["stage"][t] = (d1, a5)

            def p2b_fin(t):
                d1, a5 = st3["stage"][t]
                pb_ = t % 2
                seq = t // TPS
                col0 = (t % TPS) * TILE
                bm = P.op("pe", lambda e, t=t: e.matmul(
                    out=pd[2][:], lhsT=oner[:],
                    rhs=rows[0:1, t * 512:(t + 1) * 512],
                    start=True, stop=True),
                    waits=[d1, st2["pd_free"][2]], inc="pe")
                em = P.op("act", lambda e: e.copy(
                    out=mbc2[:], in_=pd[2][:]),
                    waits=[bm, st3["bc_free"]], inc="act")
                br = P.op("pe", lambda e, t=t: e.matmul(
                    out=pd[3][:], lhsT=oner[:],
                    rhs=rows[0:1, TOK + t * 512: TOK + (t + 1) * 512],
                    start=True, stop=True),
                    waits=[a5, st2["pd_free"][3]], inc="pe")
                er = P.op("act", lambda e: e.copy(
                    out=rbc2[:], in_=pd[3][:]),
                    waits=[br, st3["bc_free"]], inc="act")
                od_toks = [None] * DC
                tt2 = None
                for ds in range(DC):
                    rcol = ds * TOK + t * 512
                    sl = ds % 2
                    w1 = [em]
                    if ds >= 2:
                        w1.append(od_toks[ds - 2])
                    else:
                        w1.append(st3["o32_free"][sl])
                    tt1 = P.op("pool", lambda e, rcol=rcol, sl=sl: e.tensor_tensor(
                        out=o32[sl][:], in0=res[:, rcol:rcol + 512],
                        in1=mbc2[:], op=ALU.subtract),
                        waits=w1, inc="pool")
                    tt2 = P.op("dve", lambda e, sl=sl: e.tensor_tensor(
                        out=o32[sl][:], in0=o32[sl][:],
                        in1=rbc2[:], op=ALU.mult),
                        waits=[tt1, er], inc="dve")
                    od_toks[ds] = P.dma("pool",
                                        out_fm[seq, ds * 128:(ds + 1) * 128,
                                               col0:col0 + TILE],
                                        o32[sl][:], waits=[tt2],
                                        sem="qout%d" % (ds % 2))
                st2["pd_free"][2] = em
                st2["pd_free"][3] = er
                st3["bc_free"] = tt2
                st3["o32_free"][0] = od_toks[6]
                st3["o32_free"][1] = od_toks[7]

            plan_y(0)
            for t in range(NT):
                yb = t % 2
                yt = y_sb[yb]
                y_ready = st2["y_ready"][t]

                h_toks = []
                for hc in range(HC):
                    if hc == 3 and t >= 1:
                        p2b_stage(t - 1)
                    gb = hc % 2
                    ub = hc % 4
                    ldu = P.dma("sync", wu_buf[:, ub * 1024:(ub + 1) * 1024], wu_sc[hc],
                                waits=[st2["wu_free"][ub]], sem="qwu%d" % ub)
                    g_last = None
                    u_last = None
                    for dci in range(DC):
                        w0 = [y_ready, st2["pg_free"][gb]] if dci == 0 else []
                        if t == 0 and hc == 0:
                            w0 = w0 + [ldw[dci]]
                        g_last = P.op("pe", lambda e, hc=hc, dci=dci, gb=gb, yt=yt: e.matmul(
                            out=pg[gb][:],
                            lhsT=wg_sb[:, dci * HID + hc * 128: dci * HID + (hc + 1) * 128],
                            rhs=yt[:, dci * 512:(dci + 1) * 512],
                            start=(dci == 0), stop=(dci == DC - 1)),
                            waits=w0, inc="pe" if dci == DC - 1 else None)
                    for dci in range(DC):
                        w0 = [ldu, y_ready, st2["pu_free"][gb]] if dci == 0 else []
                        u_last = P.op("pe", lambda e, dci=dci, gb=gb, ub=ub, yt=yt: e.matmul(
                            out=pu[gb][:],
                            lhsT=wu_buf[:, ub * 1024 + dci * 128: ub * 1024 + (dci + 1) * 128],
                            rhs=yt[:, dci * 512:(dci + 1) * 512],
                            start=(dci == 0), stop=(dci == DC - 1)),
                            waits=w0, inc="pe" if dci == DC - 1 else None)
                    st2["wu_free"][ub] = u_last
                    sl = P.op("act", lambda e, gb=gb: e.activation(
                        out=a_sb[:, gb * 512:(gb + 1) * 512], in_=pg[gb][:], func=AF.Silu,
                        scale=consts[:, 1:2]),
                        waits=[g_last, st2["a_free"][gb]], inc="act")
                    st2["pg_free"][gb] = sl
                    ht = P.op("dve", lambda e, hc=hc, gb=gb: e.tensor_tensor(
                        out=h_sb[:, hc * 512:(hc + 1) * 512],
                        in0=a_sb[:, gb * 512:(gb + 1) * 512], in1=pu[gb][:], op=ALU.mult),
                        waits=[sl, u_last, st2["h_free"]], inc="dve")
                    st2["pu_free"][gb] = ht
                    st2["a_free"][gb] = ht
                    h_toks.append(ht)

                if t + 1 < NT:
                    plan_y(t + 1)
                if t >= 1:
                    p2b_fin(t - 1)

                dbank = [pd[0], pd[1], pd[2], pd[3], pg[0], pg[1], pu[0], pu[1]]
                dfree = (st2["pd_free"]
                         + [st2["pg_free"][0], st2["pg_free"][1],
                            st2["pu_free"][0], st2["pu_free"][1]])
                spre_toks = []
                acc = [None] * 8
                for hc in range(HC):
                    wb2 = hc % 4
                    ld = P.dma("sync", wd_buf[:, wb2 * 1024:(wb2 + 1) * 1024], wd_sc[hc],
                               waits=[st2["wd_free"][wb2]], sem="qwd%d" % wb2)
                    mmt = None
                    for dsi in range(8):
                        w0 = [ld, h_toks[hc]]
                        if hc == 0:
                            w0.append(dfree[dsi])
                        need_inc = (dsi == 7) or (hc == HC - 1)
                        mmt = P.op("pe", lambda e, wb2=wb2, dsi=dsi, hc=hc: e.matmul(
                            out=dbank[dsi][:],
                            lhsT=wd_buf[:, wb2 * 1024 + dsi * 128: wb2 * 1024 + (dsi + 1) * 128],
                            rhs=h_sb[:, hc * 512:(hc + 1) * 512],
                            start=(hc == 0), stop=(hc == HC - 1)),
                            waits=w0 if (dsi == 0 or hc == 0) else [],
                            inc="pe" if need_inc else None)
                        acc[dsi] = mmt
                    st2["wd_free"][wb2] = mmt
                for dsi in range(8):
                    rcol = dsi * TOK + t * 512
                    sp = P.op("dve", lambda e, dsi=dsi, rcol=rcol, yt=yt: e.scalar_tensor_tensor(
                        out=res[:, rcol:rcol + 512], in0=dbank[dsi][:], scalar=consts[:, 2:3],
                        in1=yt[:, dsi * 512:(dsi + 1) * 512], op0=ALU.mult, op1=ALU.add),
                        waits=[acc[dsi]], inc="dve")
                    if dsi < 4:
                        st2["pd_free"][dsi] = sp
                    elif dsi < 6:
                        st2["pg_free"][dsi - 4] = sp
                    else:
                        st2["pu_free"][dsi - 6] = sp
                    spre_toks.append(sp)
                st2["y_free"][yb] = spre_toks[-1]
                st2["h_free"] = spre_toks[-1]
                st2["spre"][t] = spre_toks[-1]
                p2b_sq(t, spre_toks)
            p2b_stage(NT - 1)
            p2b_fin(NT - 1)
            P.op("pool", lambda e: e.nop(),
                 waits=[("qout0", P.cnt["qout0"]), ("qout1", P.cnt["qout1"])],
                 inc=None)
            with nc.Block("p2a") as blk:
                P.emit(blk)
        nc.all_engine_barrier()
        if phases < 4:
            return nc

    return nc



_NC_CACHE = {}

def _get_program():
    if "nc" not in _NC_CACHE:
        _NC_CACHE["nc"] = build_core_program()
    return _NC_CACHE["nc"]


def stage_weights(L0_attn_W_qkv, L0_attn_W_out, L0_ffn_w_gate, L0_ffn_w_up,
                  L0_ffn_w_down):
    w_qkv = np.ascontiguousarray(np.asarray(L0_attn_W_qkv, dtype=np.float32))
    w_out = np.asarray(L0_attn_W_out, dtype=np.float32)
    wg = np.asarray(L0_ffn_w_gate, dtype=np.int32)
    wu = np.asarray(L0_ffn_w_up, dtype=np.int32)
    wd = np.asarray(L0_ffn_w_down, dtype=np.int32)
    qk_shuf = np.ascontiguousarray(
        w_qkv[:, 0:2048].reshape(8, 128, 16, 128).transpose(2, 1, 0, 3).reshape(16, 128, 1024))
    wo_shuf = np.ascontiguousarray(
        w_out.reshape(8, 128, 8, 128).transpose(2, 1, 0, 3).reshape(8, 128, 1024))
    wgT = np.ascontiguousarray(wg.T)
    wuT = np.ascontiguousarray(wu.T)
    wu_shuf = np.ascontiguousarray(
        wuT.reshape(8, 128, 24, 128).transpose(2, 1, 0, 3).reshape(24, 128, 1024))
    wdT = np.ascontiguousarray(wd.T)
    return {"w_qkv": w_qkv, "qk_shuf": qk_shuf, "wo_shuf": wo_shuf,
            "wgT": wgT, "wu_shuf": wu_shuf, "wdT": wdT}


def kernel(L0_attn_x, L0_attn_W_qkv, L0_attn_W_out, L0_ffn_w_gate, L0_ffn_w_up,
           L0_ffn_w_down):
    x = np.ascontiguousarray(np.asarray(L0_attn_x, dtype=np.float32))
    wmap = stage_weights(L0_attn_W_qkv, L0_attn_W_out, L0_ffn_w_gate,
                         L0_ffn_w_up, L0_ffn_w_down)

    nc = _get_program()
    in_maps = []
    for c in range(NCORES):
        xc = x[c * SEQ_PER_CORE:(c + 1) * SEQ_PER_CORE]
        x_fm = np.ascontiguousarray(xc.transpose(0, 2, 1))
        m = {"x_fm": x_fm}
        m.update(wmap)
        in_maps.append(m)
    res = run_bass_kernel_spmd(nc, in_maps, core_ids=list(range(NCORES)))
    out = np.empty((B, S, D), dtype=np.float32)
    for c in range(NCORES):
        o = res.results[c]["out_fm"]
        out[c * SEQ_PER_CORE:(c + 1) * SEQ_PER_CORE] = o.transpose(0, 2, 1)
    return out


if __name__ == "__main__":
    nc = build_core_program()
    print("built ok, n_inst =", nc.n_instructions())
